# version 03
# baseline (speedup 1.0000x reference)
import contextlib
import numpy as np
import ml_dtypes
import concourse.bass as bass
import concourse.mybir as mybir
from concourse.bass_utils import run_bass_kernel_spmd

F32 = mybir.dt.float32
BF16 = mybir.dt.bfloat16
AF = mybir.ActivationFunctionType
ALU = mybir.AluOpType
BF = ml_dtypes.bfloat16
ET = mybir.EngineType

NCORES = 8
T = 1024
HALO = 8
TE = T + HALO
CHUNK = 256
NCH = 4
H = 64
PD = 64
ND = 128
DM = 2048
DI = 4096
DXBC = 4352
DPROJ = 8512
INTER = 8192
EPS = 1e-5

N_CTRL = 130156
NSAMP = INTER * DM
NIV = N_CTRL - 1
NIV_PAD = 131072
IVC = NIV_PAD // NCORES

ALL_ENG = [ET.PE, ET.DVE, ET.Activation, ET.SP, ET.Pool]



_SP_CACHE = {}


def _sp():
    if "t" not in _SP_CACHE:
        t = np.linspace(0.0, N_CTRL - 1.0, NSAMP, dtype=np.float64)
        iv = np.clip(np.floor(t).astype(np.int64), 0, N_CTRL - 2)
        start = np.zeros(NIV + 1, dtype=np.int64)
        start[:NIV] = np.searchsorted(iv, np.arange(NIV), side="left")
        start[NIV] = NSAMP
        lens = np.diff(start)
        phi = t[start[:NIV]] - np.arange(NIV)
        c = np.zeros((4, 4))
        c[0] = [0.0, -0.5, 1.0, -0.5]
        c[1] = [1.0, 0.0, -2.5, 1.5]
        c[2] = [0.0, 0.5, 2.0, -1.5]
        c[3] = [0.0, 0.0, -0.5, 0.5]
        binom = np.array([[1, 0, 0, 0],
                          [1, 1, 0, 0],
                          [1, 2, 1, 0],
                          [1, 3, 3, 1]], dtype=np.float64)
        W = np.zeros((4, 4, NIV_PAD), dtype=np.float64)
        for e in range(4):
            for m in range(4):
                acc = np.zeros(NIV)
                for d in range(e, 4):
                    acc += binom[d, e] * (phi ** (d - e)) * c[m, d]
                W[e, m, :NIV] = acc
        r = (N_CTRL - 1.0) / (NSAMP - 1.0)
        s = np.arange(129, dtype=np.float64) * r
        u = np.stack([s ** e for e in range(4)], axis=0)
        ubd = np.zeros((12, 387))
        for tt in range(3):
            for e in range(4):
                ubd[4 * tt + e, 129 * tt:129 * tt + 129] = u[e]
        idx_rows = np.repeat(np.arange(NIV), lens)
        idx_slot = (np.arange(NSAMP) - start[idx_rows]).astype(np.int64)
        _SP_CACHE.update(t=t, start=start, lens=lens, W_em=W.astype(np.float32),
                         u4=u.astype(np.float32), ubd=ubd.astype(np.float32),
                         idx_rows=idx_rows, idx_slot=idx_slot)
    return _SP_CACHE


def _cp_padded(cp):
    tail = NIV_PAD + 8 - 1 - N_CTRL
    return np.concatenate([cp[0:1], cp, np.full(tail, cp[-1], cp.dtype)]).astype(np.float32)


def _dejump(w_iv_full):
    sp = _sp()
    return w_iv_full[sp["idx_rows"], sp["idx_slot"]]


def _hadamard(n):
    h = np.array([[1.0]])
    while h.shape[0] < n:
        h = np.block([[h, h], [h, -h]])
    return h


def _tri_masks():
    m = np.zeros((2, 128, CHUNK), dtype=np.float32)
    for hs in range(2):
        for s in range(128):
            m[hs, s, 128 * hs + s:] = 1.0
    return m


def _tile_T(w):
    O, K = w.shape
    Ot = (O + 127) // 128
    Kt = (K + 127) // 128
    wp = np.zeros((Ot * 128, Kt * 128), dtype=w.dtype)
    wp[:O, :K] = w
    return np.ascontiguousarray(wp.reshape(Ot, 128, Kt, 128).transpose(0, 2, 3, 1))



class Ctr:
    def __init__(self, sem):
        self.sem = sem
        self.n = 0

    def inc(self, instr, k=1):
        self.n += k
        return instr


def _inc(ctr, instr, k):
    ctr.n += k
    instr.then_inc(ctr.sem, k)
    return instr



def build_recon():
    nc = bass.Bass("TRN2", target_bir_lowering=False)
    NW = IVC // 1024
    G = IVC // 384
    REM = (IVC - G * 384) // 128
    cpw = nc.dram_tensor("cpw", [3, IVC + 8], F32, kind="ExternalInput").ap()
    wem = nc.dram_tensor("wem", [16, NW, 1024], F32, kind="ExternalInput").ap()
    ubd_i = nc.dram_tensor("ubd", [12, 387], F32, kind="ExternalInput").ap()
    u4_i = nc.dram_tensor("u4", [4, 129], F32, kind="ExternalInput").ap()
    w_iv = nc.dram_tensor("w_iv", [3, IVC, 129], BF16, kind="ExternalOutput").ap()
    h_dram = nc.dram_tensor("h_dram", [3, 4, IVC], BF16, kind="Internal").ap()

    with (
        nc.sbuf_tensor("cps", [NW, 1024 + 8], F32) as cps,
        nc.sbuf_tensor("wemt", [NW, 16 * 1024], F32) as wemt,
        nc.sbuf_tensor("hacc", [NW, 1024], F32) as hacc,
        nc.sbuf_tensor("htmp", [NW, 1024], F32) as htmp,
        nc.sbuf_tensor("hbf", [NW, 1024], BF16) as hbf,
        nc.sbuf_tensor("hsb", [12, IVC // 128 * 128 // 3 + 256], BF16) as hsb,
        nc.sbuf_tensor("ubd_t", [12, 387], BF16) as ubd_t,
        nc.sbuf_tensor("u4_t", [4, 129], BF16) as u4_t,
        nc.sbuf_tensor("stage", [128, 4 * 387], BF16) as stage,
        nc.psum_tensor("ps0", [128, 387], F32) as ps0,
        nc.psum_tensor("ps1", [128, 387], F32) as ps1,
        nc.semaphore("ld") as ld_s,
        nc.semaphore("dv") as dv_s,
        nc.semaphore("hs") as hs_s,
        nc.semaphore("mm") as mm_s,
        nc.semaphore("ev") as ev_s,
        nc.semaphore("st") as st_s,
        nc.Block() as block,
    ):
        ld = Ctr(ld_s)
        dv = Ctr(dv_s)
        hs = Ctr(hs_s)
        mm = Ctr(mm_s)
        ev = Ctr(ev_s)
        st = Ctr(st_s)
        ps = [ps0, ps1]
        GM = G * 128

        @block.gpsimd
        def _(g):
            _inc(ld, g.dma_start(out=ubd_t[:, :], in_=ubd_i[:, :]), 16)
            _inc(ld, g.dma_start(out=u4_t[:, :], in_=u4_i[:, :]), 16)
            _inc(ld, g.dma_start(out=wemt[:, :], in_=wem.transpose([1, 0, 2])), 16)

        gi = 0
        for w in range(3):
            @block.sync
            def _(s, w=w, dvn=dv.n):
                if w > 0:
                    s.wait_ge(dv.sem, dvn)
                src = cpw[w, :]
                src_ov = bass.AP(src.tensor, src.offset, [[1024, NW], [1, 1032]])
                _inc(ld, s.dma_start(out=cps[:, :], in_=src_ov), 16)

            ld_now = ld.n
            for e in range(4):
                @block.vector
                def _(v, w=w, e=e, ld_now=ld_now, hsn=hs.n):
                    if e == 0:
                        v.wait_ge(ld.sem, ld_now)
                    v.wait_ge(hs.sem, hsn)
                    for m in range(4):
                        wslice = wemt[:, (e * 4 + m) * 1024:(e * 4 + m + 1) * 1024]
                        cslice = cps[:, m:m + 1024]
                        if m == 0:
                            v.tensor_tensor(hacc[:, :], wslice, cslice, ALU.mult)
                        else:
                            v.tensor_tensor(htmp[:, :], wslice, cslice, ALU.mult)
                            if m < 3:
                                v.tensor_add(hacc[:, :], hacc[:, :], htmp[:, :])
                            else:
                                _inc(dv, v.tensor_add(hbf[:, :], hacc[:, :], htmp[:, :]), 1)

                @block.sync
                def _(s, w=w, e=e, dvn=dv.n):
                    s.wait_ge(dv.sem, dvn)
                    _inc(hs, s.dma_start(out=h_dram[w, e, :].rearrange("(p f) -> p f", p=NW),
                                         in_=hbf[:, :]), 16)

            @block.scalar
            def _(s, w=w, hsn=hs.n, mmn=mm.n):
                s.wait_ge(hs.sem, hsn)
                if w > 0:
                    s.wait_ge(mm.sem, mmn)
                hh = h_dram[w]
                for e in range(4):
                    for tt in range(3):
                        src = bass.AP(hh.tensor, hh.offset + e * IVC + 128 * tt, [[384, G], [1, 128]])
                        _inc(hs, s.dma_start(out=hsb[4 * tt + e:4 * tt + e + 1, 0:GM], in_=src), 16)
                _inc(hs, s.dma_start(out=hsb[0:4, GM:GM + 256], in_=h_dram[w][:, G * 384:IVC]), 16)

            hs_ready = hs.n
            for g in range(G + REM):
                psb = ps[gi % 2]

                @block.tensor
                def _(tn, g=g, gi=gi, psb=psb, hs_ready=hs_ready):
                    if g == 0:
                        tn.wait_ge(hs.sem, hs_ready)
                        tn.wait_ge(ld.sem, 32)
                    if gi >= 2:
                        tn.wait_ge(ev.sem, gi - 1)
                    if g < G:
                        _inc(mm, tn.matmul(psb[:, :], hsb[:, g * 128:(g + 1) * 128],
                                           ubd_t[:, :], start=True, stop=True), 1)
                    else:
                        tt = g - G
                        _inc(mm, tn.matmul(psb[:, 0:129], hsb[0:4, GM + tt * 128:GM + (tt + 1) * 128],
                                           u4_t[:, :], start=True, stop=True), 1)

                @block.scalar
                def _(sc, g=g, gi=gi, psb=psb):
                    sc.wait_ge(mm.sem, gi + 1)
                    if gi >= 4:
                        sc.wait_ge(st.sem, 16 * (gi - 3))
                    wdt = 387 if g < G else 129
                    _inc(ev, sc.copy(stage[:, (gi % 4) * 387:(gi % 4) * 387 + wdt], psb[:, 0:wdt]), 1)

                @block.gpsimd
                def _(gp, g=g, gi=gi, w=w):
                    gp.wait_ge(ev.sem, gi + 1)
                    if g < G:
                        base = w_iv[w]
                        dst = bass.AP(base.tensor, base.offset + g * 384 * 129,
                                      [[129, 128], [128 * 129, 3], [1, 129]])
                        wdt = 387
                    else:
                        tt = g - G
                        dst = w_iv[w][G * 384 + tt * 128:G * 384 + (tt + 1) * 128, :]
                        wdt = 129
                    _inc(st, gp.dma_start(out=dst, in_=stage[:, (gi % 4) * 387:(gi % 4) * 387 + wdt]), 16)

                gi += 1

        @block.sync
        def _(s):
            s.wait_ge(st.sem, st.n)
    return nc


def host_recon_inputs(core, gate_cp, up_cp, down_cp):
    sp = _sp()
    cps = []
    for cp in (gate_cp, up_cp, down_cp):
        pad = _cp_padded(np.asarray(cp))
        cps.append(pad[IVC * core: IVC * core + IVC + 8])
    wem = sp["W_em"][:, :, IVC * core: IVC * (core + 1)].reshape(16, IVC // 1024, 1024)
    return {
        "cpw": np.stack(cps).astype(np.float32),
        "wem": np.ascontiguousarray(wem),
        "ubd": sp["ubd"],
        "u4": sp["u4"],
    }



def build_p1(a_from_log):
    nc = bass.Bass("TRN2", target_bir_lowering=False)
    NOT = 67
    resid_i = nc.dram_tensor("resid", [DM, TE], F32, kind="ExternalInput").ap()
    w_in = nc.dram_tensor("w_in", [NOT, 16, 128, 128], F32, kind="ExternalInput").ap()
    conv_w = nc.dram_tensor("conv_w", [34, 128, 4], F32, kind="ExternalInput").ap()
    conv_b = nc.dram_tensor("conv_b", [128, 34], F32, kind="ExternalInput").ap()
    dtb_i = nc.dram_tensor("dtb", [64, 1], F32, kind="ExternalInput").ap()
    apar_i = nc.dram_tensor("apar", [64, 1], F32, kind="ExternalInput").ap()
    ones_i = nc.dram_tensor("onesin", [1, 128], F32, kind="ExternalInput").ap()
    masks_i = nc.dram_tensor("masks", [2, 128, 256], F32, kind="ExternalInput").ap()

    z_o = nc.dram_tensor("z_o", [DI, TE], BF16, kind="ExternalOutput").ap()
    x_o = nc.dram_tensor("x_o", [DI, T], BF16, kind="ExternalOutput").ap()
    b_o = nc.dram_tensor("b_o", [ND, T], BF16, kind="ExternalOutput").ap()
    c_o = nc.dram_tensor("c_o", [ND, T], BF16, kind="ExternalOutput").ap()
    dt_o = nc.dram_tensor("dt_o", [64, T], F32, kind="ExternalOutput").ap()
    cs_o = nc.dram_tensor("cs_o", [64, T], F32, kind="ExternalOutput").ap()
    dte_o = nc.dram_tensor("dte_o", [64, T], F32, kind="ExternalOutput").ap()
    dec_o = nc.dram_tensor("dec_o", [64, 16], F32, kind="ExternalOutput").ap()
    cbm_o = nc.dram_tensor("cbm_o", [NCH, 2, 128, 256], BF16, kind="ExternalOutput").ap()

    CH = [(0, 512), (512, 512), (1024, 8)]

    with contextlib.ExitStack() as stk:
        def sb(name, shape, dt):
            return stk.enter_context(nc.sbuf_tensor(name, shape, dt))

        def sem(name):
            return Ctr(stk.enter_context(nc.semaphore(name)))

        ones_c = sb("ones_c", [1, 128], BF16)
        sumsel = sb("sumsel", [128, 2], BF16)
        sqsel = sb("sqsel", [128, 2], BF16)
        xln = sb("xln", [128, 16 * TE], BF16)
        xbc = sb("xbc", [128, 34 * TE], BF16)
        bsb = sb("bsb", [128, T], BF16)
        csb = sb("csb", [128, T], BF16)
        masksb = sb("masksb", [128, 512], BF16)
        wbuf = [sb("wbuf0", [128, 2048], BF16), sb("wbuf1", [128, 2048], BF16)]
        zst = [sb("zst0", [128, TE], BF16), sb("zst1", [128, TE], BF16)]
        cvst = sb("cvst", [128, T], BF16)
        cacc = sb("cacc", [128, T], F32)
        convw = sb("convw", [128, 34 * 4], F32)
        convb = sb("convb", [128, 34], F32)
        srow = sb("srow", [2, TE], F32)
        sqrow = sb("sqrow", [1, TE], F32)
        scr0 = sb("scr0", [1, TE], F32)
        scr1 = sb("scr1", [1, TE], F32)
        murow = sb("murow", [1, TE], F32)
        rsrow = sb("rsrow", [1, TE], F32)
        murb = sb("murb", [1, TE], BF16)
        rsrb = sb("rsrb", [1, TE], BF16)
        mubc = sb("mubc", [128, TE], F32)
        rsbc = sb("rsbc", [128, TE], F32)
        rt = [sb("rt0", [128, TE], F32), sb("rt1", [128, TE], F32)]
        castb = [sb("cast0", [128, TE], BF16), sb("cast1", [128, TE], BF16)]
        sqb = [sb("sq0", [128, TE], BF16), sb("sq1", [128, TE], BF16)]
        dts = sb("dts", [64, 2 * TE + 8], F32)
        dtv = sb("dtv", [64, T], F32)
        csv = sb("csv", [64, T], F32)
        dtev = sb("dtev", [64, T], F32)
        decv = sb("decv", [64, 16], F32)
        zeros64 = sb("zeros64", [64, 256], F32)
        acol = sb("acol", [64, 1], F32)
        onecol = sb("onecol", [64, 1], F32)
        epsc = sb("epsc", [1, 1], F32)
        cbst = sb("cbst", [128, 256], BF16)

        cst = sem("cst")
        sqs = sem("sqs")
        rts = [sem("rts0"), sem("rts1")]
        wbs = [sem("wbs0"), sem("wbs1")]
        zss = [sem("zss0"), sem("zss1")]
        cvs = sem("cvs")
        cbs = sem("cbs")
        mst = sem("mst")
        pe = sem("pe")
        ve = sem("ve")
        ac = sem("ac")
        block = stk.enter_context(nc.Block())

        @block.gpsimd
        def _(g):
            _inc(cst, g.dma_start(out=ones_c[:, :], in_=ones_i[:, :]), 16)
            _inc(cst, g.dma_start(out=masksb[:, 0:256], in_=masks_i[0]), 16)
            _inc(cst, g.dma_start(out=masksb[:, 256:512], in_=masks_i[1]), 16)

        @block.sync
        def _(s):
            _inc(cst, s.dma_start(out=convw[:, :], in_=conv_w.transpose([1, 0, 2])), 16)
            _inc(cst, s.dma_start(out=convb[:, :], in_=conv_b[:, :]), 16)
            _inc(cst, s.dma_start(out=dts[:, 2 * TE:2 * TE + 1], in_=dtb_i[:, :]), 16)
            _inc(cst, s.dma_start(out=dts[:, 2 * TE + 1:2 * TE + 2], in_=apar_i[:, :]), 16)

        @block.vector
        def _(v):
            v.wait_ge(cst.sem, cst.n)
            v.memset(sumsel[:, 0:1], 1.0)
            v.memset(sumsel[:, 1:2], 0.0)
            v.memset(sqsel[:, 0:1], 0.0)
            v.memset(sqsel[:, 1:2], 1.0)
            v.memset(zeros64[:, :], 0.0)
            v.memset(onecol[:, :], 1.0)
            _inc(ve, v.memset(epsc[:, :], EPS), 1)

        with (
            nc.psum_tensor("st0", [2, 512], F32) as st0,
            nc.psum_tensor("st1", [2, 512], F32) as st1,
            nc.psum_tensor("st2", [2, 8], F32) as st2,
            nc.psum_tensor("bc", [128, 512], F32) as bc_ps,
        ):
            stp = [st0, st1, st2]
            for g in range(16):
                slot = g % 2

                @block.sync
                def _(s, g=g, slot=slot, ven=ve.n):
                    s.wait_ge(ve.sem, ven)
                    _inc(rts[slot], s.dma_start(out=rt[slot][:, :],
                                                in_=resid_i[128 * g:128 * (g + 1), :]), 16)

                @block.vector
                def _(v, g=g, slot=slot, need=16 * (g // 2 + 1), pen=pe.n):
                    v.wait_ge(rts[slot].sem, need)
                    if g >= 2:
                        v.wait_ge(pe.sem, pen - 6)
                    _inc(ve, v.tensor_copy(castb[slot][:, :], rt[slot][:, :]), 1)

                @block.scalar
                def _(sc, g=g, slot=slot, ven=ve.n, pen=pe.n):
                    sc.wait_ge(ve.sem, ven)
                    if g >= 2:
                        sc.wait_ge(pe.sem, pen - 6)
                    _inc(ac, sc.activation(sqb[slot][:, :], castb[slot][:, :], AF.Square), 1)

                @block.tensor
                def _(tn, g=g, slot=slot, ven=ve.n, acn=ac.n):
                    tn.wait_ge(ve.sem, ven)
                    tn.wait_ge(ac.sem, acn)
                    for ci, (c0, cw) in enumerate(CH):
                        _inc(pe, tn.matmul(stp[ci][:, 0:cw], sumsel[:, :], castb[slot][:, c0:c0 + cw],
                                           start=(g == 0), stop=(g == 15)), 1)
                        _inc(pe, tn.matmul(stp[ci][:, 0:cw], sqsel[:, :], sqb[slot][:, c0:c0 + cw],
                                           start=False, stop=(g == 15)), 1)

            @block.scalar
            def _(sc, pen=pe.n):
                sc.wait_ge(pe.sem, pen)
                for ci, (c0, cw) in enumerate(CH):
                    _inc(ac, sc.copy(srow[:, c0:c0 + cw], stp[ci][:, 0:cw]), 1)

            @block.gpsimd
            def _(g, acn=ac.n):
                g.wait_ge(ac.sem, acn)
                _inc(sqs, g.dma_start(out=sqrow[:, :], in_=srow[1:2, :]), 16)

            @block.vector
            def _(v, acn=ac.n):
                v.wait_ge(ac.sem, acn)
                v.wait_ge(sqs.sem, 16)
                v.tensor_scalar(murow[:, :], srow[0:1, :], 1.0 / DM, None, ALU.mult)
                v.tensor_scalar(scr0[:, :], sqrow[0:1, :], 1.0 / DM, None, ALU.mult)
                v.tensor_tensor(scr1[:, :], murow[:, :], murow[:, :], ALU.mult)
                v.tensor_sub(scr0[:, :], scr0[:, :], scr1[:, :])
                _inc(ve, v.tensor_scalar(scr0[:, :], scr0[:, :], EPS, None, ALU.add), 1)

            @block.scalar
            def _(sc, ven=ve.n):
                sc.wait_ge(ve.sem, ven)
                _inc(ac, sc.activation(scr1[:, :], scr0[:, :], AF.Sqrt), 1)

            @block.vector
            def _(v, acn=ac.n):
                v.wait_ge(ac.sem, acn)
                v.reciprocal(rsrow[:, :], scr1[:, :])
                v.tensor_copy(murb[:, :], murow[:, :])
                _inc(ve, v.tensor_copy(rsrb[:, :], rsrow[:, :]), 1)

            for rowb, dst in [(murb, mubc), (rsrb, rsbc)]:
                for c0, cw in CH:
                    @block.tensor
                    def _(tn, rowb=rowb, c0=c0, cw=cw, ven=ve.n, acn=ac.n):
                        tn.wait_ge(ve.sem, ven)
                        tn.wait_ge(ac.sem, acn)
                        _inc(pe, tn.matmul(bc_ps[:, 0:cw], ones_c[0:1, :], rowb[:, c0:c0 + cw],
                                           start=True, stop=True), 1)

                    @block.scalar
                    def _(sc, dst=dst, c0=c0, cw=cw, pen=pe.n):
                        sc.wait_ge(pe.sem, pen)
                        _inc(ac, sc.copy(dst[:, c0:c0 + cw], bc_ps[:, 0:cw]), 1)

            for g in range(16):
                slot = g % 2

                @block.sync
                def _(s, g=g, slot=slot, ven=ve.n):
                    s.wait_ge(ve.sem, ven)
                    _inc(rts[slot], s.dma_start(out=rt[slot][:, :],
                                                in_=resid_i[128 * g:128 * (g + 1), :]), 16)

                @block.vector
                def _(v, g=g, slot=slot, need=16 * (8 + g // 2 + 1), acn=ac.n):
                    v.wait_ge(rts[slot].sem, need)
                    v.wait_ge(ac.sem, acn)
                    xt = xln[:, g * TE:(g + 1) * TE]
                    v.tensor_sub(rt[slot][:, :], rt[slot][:, :], mubc[:, :])
                    _inc(ve, v.tensor_tensor(xt, rt[slot][:, :], rsbc[:, :], ALU.mult), 1)

        with (
            nc.psum_tensor("pp0", [128, 512], F32) as pp0,
            nc.psum_tensor("pp1", [128, 512], F32) as pp1,
            nc.psum_tensor("pp2", [128, 8], F32) as pp2,
            nc.psum_tensor("pp3", [128, 512], F32) as pp3,
            nc.psum_tensor("pp4", [128, 512], F32) as pp4,
            nc.psum_tensor("pp5", [128, 8], F32) as pp5,
        ):
            pps = [[pp0, pp1, pp2], [pp3, pp4, pp5]]
            ve_apply = ve.n
            pe_mark = {}
            ev_done = {}

            for ot in range(NOT):
                slot = ot % 2

                @block.gpsimd
                def _(g, ot=ot, slot=slot):
                    if ot >= 2:
                        g.wait_ge(pe.sem, pe_mark[ot - 2])
                    _inc(wbs[slot], g.dma_start(out=wbuf[slot][:, :],
                                                in_=w_in[ot].transpose([1, 0, 2])), 16)

                @block.tensor
                def _(tn, ot=ot, slot=slot, need=16 * (ot // 2 + 1), ven=ve_apply):
                    tn.wait_ge(wbs[slot].sem, need)
                    tn.wait_ge(ve.sem, ven)
                    if ot >= 2:
                        tn.wait_ge(ac.sem, ev_done[ot - 2])
                    for ci, (c0, cw) in enumerate(CH):
                        for g in range(16):
                            ins = tn.matmul(pps[slot][ci][:, 0:cw], wbuf[slot][:, g * 128:(g + 1) * 128],
                                            xln[:, g * TE + c0:g * TE + c0 + cw],
                                            start=(g == 0), stop=(g == 15))
                            if ci == 2 and g == 15:
                                _inc(pe, ins, 1)

                pe_mark[ot] = pe.n

                @block.scalar
                def _(sc, ot=ot, slot=slot, pen=pe.n):
                    sc.wait_ge(pe.sem, pen)
                    if ot < 32:
                        if ot >= 2:
                            sc.wait_ge(zss[ot % 2].sem, 16 * (ot // 2))
                        for ci, (c0, cw) in enumerate(CH):
                            ins = sc.copy(zst[ot % 2][:, c0:c0 + cw], pps[slot][ci][:, 0:cw])
                            if ci == 2:
                                _inc(ac, ins, 1)
                    elif ot < 66:
                        xt = xbc[:, (ot - 32) * TE:(ot - 31) * TE]
                        for ci, (c0, cw) in enumerate(CH):
                            ins = sc.copy(xt[:, c0:c0 + cw], pps[slot][ci][:, 0:cw])
                            if ci == 2:
                                _inc(ac, ins, 1)
                    else:
                        for ci, (c0, cw) in enumerate(CH):
                            ins = sc.copy(dts[0:64, c0:c0 + cw], pps[slot][ci][0:64, 0:cw])
                            if ci == 2:
                                _inc(ac, ins, 1)

                ev_done[ot] = ac.n

                if ot < 32:
                    @block.sync
                    def _(s, ot=ot, acn=ac.n):
                        s.wait_ge(ac.sem, acn)
                        _inc(zss[ot % 2], s.dma_start(out=z_o[128 * ot:128 * (ot + 1), :],
                                                      in_=zst[ot % 2][:, :]), 16)

        in_proj_ac = ac.n
        for ft in range(34):
            @block.vector
            def _(v, ft=ft, acn=ac.n):
                v.wait_ge(ac.sem, acn)
                xt = xbc[:, ft * TE:(ft + 1) * TE]
                w0 = convw[:, 4 * ft + 0:4 * ft + 1]
                v.tensor_scalar(cacc[:, :], xt[:, 5:5 + T], w0, None, ALU.mult)
                for k in (1, 2, 3):
                    wk = convw[:, 4 * ft + k:4 * ft + k + 1]
                    ins = v.scalar_tensor_tensor(cacc[:, :], xt[:, 5 + k:5 + k + T], wk, cacc[:, :],
                                                 ALU.mult, ALU.add)
                    if k == 3:
                        _inc(ve, ins, 1)

            @block.scalar
            def _(sc, ft=ft, ven=ve.n):
                sc.wait_ge(ve.sem, ven)
                if 1 <= ft <= 32:
                    sc.wait_ge(cvs.sem, 16 * ft)
                dd = cvst[:, :] if ft < 32 else (bsb[:, :] if ft == 32 else csb[:, :])
                _inc(ac, sc.activation(dd, cacc[:, :], AF.Silu, bias=convb[:, ft:ft + 1]), 1)

            @block.sync
            def _(s, ft=ft, acn=ac.n):
                s.wait_ge(ac.sem, acn)
                if ft < 32:
                    _inc(cvs, s.dma_start(out=x_o[128 * ft:128 * (ft + 1), :], in_=cvst[:, :]), 16)
                elif ft == 32:
                    _inc(mst, s.dma_start(out=b_o[:, :], in_=bsb[:, :]), 16)
                else:
                    _inc(mst, s.dma_start(out=c_o[:, :], in_=csb[:, :]), 16)

        @block.scalar
        def _(sc, acn=ac.n, cstn=cst.n):
            sc.wait_ge(ac.sem, acn)
            sc.wait_ge(cst.sem, cstn)
            dtb = dts[:, 2 * TE:2 * TE + 1]
            raw = dts[0:64, HALO:HALO + T]
            B = dts[0:64, TE:TE + T]
            _inc(ac, sc.activation(B, raw, AF.Abs, bias=dtb), 1)
            _inc(ac, sc.activation(dtev[:, :], B, AF.Exp, scale=-1.0), 1)
            _inc(ac, sc.activation(B, dtev[:, :], AF.Ln, bias=onecol[:, 0:1]), 1)
            _inc(ac, sc.activation(dtev[:, :], raw, AF.Relu, bias=dtb), 1)
            if a_from_log:
                _inc(ac, sc.activation(acol[:, :], dts[0:64, 2 * TE + 1:2 * TE + 2], AF.Exp), 1)

        @block.vector
        def _(v, acn=ac.n):
            v.wait_ge(ac.sem, acn)
            v.tensor_add(dtv[:, :], dts[0:64, TE:TE + T], dtev[:, :])
            sgn = -1.0 if a_from_log else 1.0
            a_src = acol[:, 0:1] if a_from_log else dts[0:64, 2 * TE + 1:2 * TE + 2]
            v.tensor_scalar(dts[0:64, 0:T], dtv[:, :], a_src, sgn, ALU.mult, ALU.mult)
            for c in range(NCH):
                last = (c == NCH - 1)
                ins = v.tensor_tensor_scan(csv[:, c * 256:(c + 1) * 256], dts[0:64, c * 256:(c + 1) * 256],
                                           zeros64[:, :], 0.0, ALU.add, ALU.add)
                if last:
                    _inc(ve, ins, 1)

        @block.scalar
        def _(sc, ven=ve.n):
            sc.wait_ge(ve.sem, ven)
            for c in range(NCH):
                last = csv[:, c * 256 + 255:c * 256 + 256]
                _inc(ac, sc.activation(decv[:, c:c + 1], last, AF.Exp), 1)
                _inc(ac, sc.activation(dtev[:, c * 256:(c + 1) * 256], csv[:, c * 256:(c + 1) * 256],
                                       AF.Exp, scale=-1.0, bias=last), 1)

        @block.vector
        def _(v, acn=ac.n):
            v.wait_ge(ac.sem, acn)
            v.memset(decv[:, 4:5], 1.0)
            for c in range(1, NCH):
                v.tensor_tensor(decv[:, 4 + c:5 + c], decv[:, 3 + c:4 + c], decv[:, c - 1:c], ALU.mult)
            v.tensor_tensor(decv[:, 8:9], decv[:, 7:8], decv[:, 3:4], ALU.mult)
            sgn = -1.0 if a_from_log else 1.0
            a_src = acol[:, :] if a_from_log else dts[0:64, 2 * TE + 1:2 * TE + 2]
            _inc(ve, v.tensor_scalar(decv[:, 9:10], a_src, sgn, None, ALU.mult), 1)

        @block.sync
        def _(s, ven=ve.n):
            s.wait_ge(ve.sem, ven)
            _inc(mst, s.dma_start(out=dt_o[:, :], in_=dtv[:, :]), 16)
            _inc(mst, s.dma_start(out=cs_o[:, :], in_=csv[:, :]), 16)
            _inc(mst, s.dma_start(out=dte_o[:, :], in_=dtev[:, :]), 16)
            _inc(mst, s.dma_start(out=dec_o[:, :], in_=decv[:, :]), 16)

        with nc.psum_tensor("cbp", [128, 256], F32) as cbp:
            first = True
            for c in range(NCH):
                for hs in range(2):
                    @block.tensor
                    def _(tn, c=c, hs=hs, first=first, acn=in_proj_ac, ven=ve.n):
                        tn.wait_ge(ac.sem, ac.n)
                        if not first:
                            tn.wait_ge(ve.sem, ven)
                        _inc(pe, tn.matmul(cbp[:, :], bsb[:, c * 256 + hs * 128:c * 256 + (hs + 1) * 128],
                                           csb[:, c * 256:(c + 1) * 256], start=True, stop=True), 1)

                    @block.vector
                    def _(v, c=c, hs=hs, first=first, pen=pe.n, cbn=cbs.n):
                        v.wait_ge(pe.sem, pen)
                        if not first:
                            v.wait_ge(cbs.sem, cbn)
                        _inc(ve, v.tensor_tensor(cbst[:, :], cbp[:, :],
                                                 masksb[:, hs * 256:(hs + 1) * 256], ALU.mult), 1)

                    @block.sync
                    def _(s, c=c, hs=hs, ven=ve.n):
                        s.wait_ge(ve.sem, ven)
                        _inc(cbs, s.dma_start(out=cbm_o[c, hs], in_=cbst[:, :]), 16)

                    first = False

        @block.sync
        def _(s):
            s.wait_ge(mst.sem, mst.n)
            s.wait_ge(cvs.sem, cvs.n)
            s.wait_ge(cbs.sem, cbs.n)
            s.wait_ge(zss[0].sem, zss[0].n)
            s.wait_ge(zss[1].sem, zss[1].n)
    return nc



def build_p2():
    nc = bass.Bass("TRN2", target_bir_lowering=False)
    xT_i = nc.dram_tensor("xT", [T, DI], BF16, kind="ExternalInput").ap()
    bT_i = nc.dram_tensor("bT", [T, ND], BF16, kind="ExternalInput").ap()
    cfm_i = nc.dram_tensor("cfm", [ND, T], BF16, kind="ExternalInput").ap()
    xfm_i = nc.dram_tensor("xfm", [DI, T], BF16, kind="ExternalInput").ap()
    dtT_i = nc.dram_tensor("dtT", [T, 64], F32, kind="ExternalInput").ap()
    dteT_i = nc.dram_tensor("dteT", [T, 64], F32, kind="ExternalInput").ap()
    csT_i = nc.dram_tensor("csT", [T, 64], F32, kind="ExternalInput").ap()
    csf_i = nc.dram_tensor("csf", [NCH, 16384], F32, kind="ExternalInput").ap()
    cbm_i = nc.dram_tensor("cbm", [NCH, 2, 128, 256], BF16, kind="ExternalInput").ap()
    dpar_i = nc.dram_tensor("dpar", [1, 64], F32, kind="ExternalInput").ap()
    cdrow_i = nc.dram_tensor("cdrow", [1, NCH * 64], F32, kind="ExternalInput").ap()
    ones_i = nc.dram_tensor("onesin", [1, 128], F32, kind="ExternalInput").ap()

    y_o = nc.dram_tensor("y_o", [DI, T], F32, kind="ExternalOutput").ap()
    sloc_o = nc.dram_tensor("sloc_o", [H, ND, PD], F32, kind="ExternalOutput").ap()
    sprev_o = nc.dram_tensor("sprev_o", [NCH - 1, H, ND, PD], F32, kind="ExternalOutput").ap()
    cexp_o = nc.dram_tensor("cexp_o", [NCH, H, ND, CHUNK], BF16, kind="ExternalOutput").ap()

    with contextlib.ExitStack() as stk:
        def sb(name, shape, dt):
            return stk.enter_context(nc.sbuf_tensor(name, shape, dt))

        def sem(name):
            return Ctr(stk.enter_context(nc.semaphore(name)))

        def psm(name, shape):
            return stk.enter_context(nc.psum_tensor(name, shape, F32))

        ones_c = sb("ones_c", [1, 128], BF16)
        ones_f = sb("ones_f", [1, 128], F32)
        cbm_s = sb("cbm_s", [128, 8 * 256], BF16)
        dbc = sb("dbc", [128, 64], F32)
        cdbc = sb("cdbc", [128, NCH * 64], F32)
        xts = [sb("xts0", [128, 2 * DI], BF16), sb("xts1", [128, 2 * DI], BF16)]
        bts = [sb("bts0", [128, 2 * ND], BF16), sb("bts1", [128, 2 * ND], BF16)]
        sml = [sb("sml0", [128, 2 * 192], F32), sb("sml1", [128, 2 * 192], F32)]
        csf = sb("csf_s", [1, 16384], F32)
        cfm = [sb("cfm0", [128, 256], BF16), sb("cfm1", [128, 256], BF16)]
        xfm = [sb("xfm0", [128, 32 * 256], BF16), sb("xfm1", [128, 32 * 256], BF16)]
        csn = [sb("csn0", [128, 2 * 64], F32), sb("csn1", [128, 2 * 64], F32)]
        xdt = sb("xdt", [128, 2 * 64], BF16)
        bdec = sb("bdec", [128, 2 * ND], BF16)
        eb = [sb("eb0", [128, 256], BF16), sb("eb1", [128, 256], BF16)]
        ebx = [sb("ebx0", [128, 256], BF16), sb("ebx1", [128, 256], BF16)]
        pb = [sb("pb0", [128, 256], BF16), sb("pb1", [128, 256], BF16)]
        expb = sb("expb", [128, 256], F32)
        cexb = [sb("cexb0", [128, 256], BF16), sb("cexb1", [128, 256], BF16)]
        yst = [sb("yst0", [128, 256], F32), sb("yst1", [128, 256], F32)]
        saft = [sb("saft0", [128, H * PD], F32), sb("saft1", [128, H * PD], F32)]

        cst = sem("cst")
        ldc = [sem("ldc0"), sem("ldc1")]
        ys = [sem("ys0"), sem("ys1")]
        cxs = [sem("cxs0"), sem("cxs1")]
        sps = [sem("sps0"), sem("sps1")]
        mst = sem("mst")
        pe = sem("pe")
        ve = sem("ve")
        ac = sem("ac")

        ps_cs = psm("ps_cs", [128, 256])
        ps_y = psm("ps_y", [64, 256])
        ps_s = psm("ps_s", [128, 64])
        ps_bc = psm("ps_bc", [128, 256])
        block = stk.enter_context(nc.Block())

        @block.gpsimd
        def _(g):
            _inc(cst, g.dma_start(out=ones_c[:, :], in_=ones_i[:, :]), 16)
            _inc(cst, g.dma_start(out=ones_f[:, :], in_=ones_i[:, :]), 16)
            for c in range(NCH):
                for hs in range(2):
                    _inc(cst, g.dma_start(out=cbm_s[:, (c * 2 + hs) * 256:(c * 2 + hs + 1) * 256],
                                          in_=cbm_i[c, hs]), 16)

        @block.sync
        def _(s):
            _inc(cst, s.dma_start(out=dbc[0:1, 0:64], in_=dpar_i[:, :]), 16)
            _inc(cst, s.dma_start(out=cdbc[0:1, :], in_=cdrow_i[:, :]), 16)

        @block.tensor
        def _(tn, cstn=cst.n):
            tn.wait_ge(cst.sem, cstn)
            pass

        rowbf = sb("rowbf", [1, NCH * 64], BF16)

        @block.vector
        def _(v, cstn=cst.n):
            v.wait_ge(cst.sem, cstn)
            _inc(ve, v.tensor_copy(rowbf[0:1, 0:64], dbc[0:1, 0:64]), 1)

        @block.tensor
        def _(tn, ven=ve.n):
            tn.wait_ge(ve.sem, ven)
            _inc(pe, tn.matmul(ps_bc[:, 0:64], ones_c[0:1, :], rowbf[0:1, 0:64],
                               start=True, stop=True), 1)

        @block.scalar
        def _(sc, pen=pe.n):
            sc.wait_ge(pe.sem, pen)
            _inc(ac, sc.copy(dbc[:, :], ps_bc[:, 0:64]), 1)

        @block.vector
        def _(v, acn=ac.n):
            v.wait_ge(ac.sem, acn)
            _inc(ve, v.tensor_copy(rowbf[0:1, :], cdbc[0:1, :]), 1)

        @block.tensor
        def _(tn, ven=ve.n):
            tn.wait_ge(ve.sem, ven)
            _inc(pe, tn.matmul(ps_bc[:, 0:256], ones_c[0:1, :], rowbf[0:1, :],
                               start=True, stop=True), 1)

        @block.scalar
        def _(sc, pen=pe.n):
            sc.wait_ge(pe.sem, pen)
            _inc(ac, sc.copy(cdbc[:, :], ps_bc[:, 0:256]), 1)

        @block.vector
        def _(v, acn=ac.n):
            v.wait_ge(ac.sem, acn)
            v.memset(saft[1][:, :], 0.0)
            _inc(ve, v.nop(hint="init_done"), 1)
        ve_init = ve.n
        ac_init = ac.n

        ys_cnt = {0: 0, 1: 0}

        def sp_need_fn(c, h):
            if c - 1 >= 1:
                prior_chunks = len([cc for cc in range(1, c - 1) if (cc + 1) % 2 == c % 2])
                return 16 * (prior_chunks * 64 + h + 1)
            return None

        ld_need = {}
        for c in range(NCH):
            slot = c % 2

            @block.sync
            def _(s, c=c, slot=slot, ven=ve.n, acn=ac.n, pen=pe.n):
                s.wait_ge(ve.sem, ven)
                s.wait_ge(ac.sem, acn)
                s.wait_ge(pe.sem, pen)
                n0 = 256 * c
                _inc(ldc[slot], s.dma_start(out=xts[slot][:, 0:DI], in_=xT_i[n0:n0 + 128, :]), 16)
                _inc(ldc[slot], s.dma_start(out=xts[slot][:, DI:2 * DI], in_=xT_i[n0 + 128:n0 + 256, :]), 16)
                _inc(ldc[slot], s.dma_start(out=bts[slot][:, 0:ND], in_=bT_i[n0:n0 + 128, :]), 16)
                _inc(ldc[slot], s.dma_start(out=bts[slot][:, ND:2 * ND], in_=bT_i[n0 + 128:n0 + 256, :]), 16)
                for half in range(2):
                    hb = n0 + 128 * half
                    _inc(ldc[slot], s.dma_start(out=sml[slot][:, half * 192:half * 192 + 64],
                                                in_=dtT_i[hb:hb + 128, :]), 16)
                    _inc(ldc[slot], s.dma_start(out=sml[slot][:, half * 192 + 64:half * 192 + 128],
                                                in_=dteT_i[hb:hb + 128, :]), 16)
                    _inc(ldc[slot], s.dma_start(out=sml[slot][:, half * 192 + 128:half * 192 + 192],
                                                in_=csT_i[hb:hb + 128, :]), 16)
                _inc(ldc[slot], s.dma_start(out=csf[:, :], in_=csf_i[c:c + 1, :]), 16)
                _inc(ldc[slot], s.dma_start(out=cfm[slot][:, :], in_=cfm_i[:, n0:n0 + 256]), 16)
                src = xfm_i[:, n0:n0 + 256]
                src3 = bass.AP(src.tensor, src.offset, [[T, 128], [128 * T, 32], [1, 256]])
                _inc(ldc[slot], s.dma_start(out=xfm[slot][:, :], in_=src3), 16)

            ld_need[c] = 16 * 13 * (c // 2 + 1)

            @block.vector
            def _(v, c=c, slot=slot, need=ld_need[c]):
                v.wait_ge(ldc[slot].sem, need)
                v.tensor_scalar(csn[slot][:, 0:64], sml[slot][:, 128:192], -1.0, None, ALU.mult)
                _inc(ve, v.tensor_scalar(csn[slot][:, 64:128], sml[slot][:, 192 + 128:192 + 192],
                                         -1.0, None, ALU.mult), 1)

            for h in range(H):
                hpar = h % 2
                g2 = h // 2
                base = 64 * hpar
                slot = c % 2

                @block.vector
                def _(v, c=c, slot=slot, h=h, pen=pe.n):
                    if h >= 1 or c >= 1:
                        v.wait_ge(pe.sem, pen)
                    for half in range(2):
                        v.tensor_scalar(xdt[:, half * 64:(half + 1) * 64],
                                        xts[slot][:, half * DI + 64 * h:half * DI + 64 * (h + 1)],
                                        sml[slot][:, half * 192 + h:half * 192 + h + 1],
                                        None, ALU.mult)
                        ins = v.tensor_scalar(bdec[:, half * ND:(half + 1) * ND],
                                              bts[slot][:, half * ND:(half + 1) * ND],
                                              sml[slot][:, half * 192 + 64 + h:half * 192 + 64 + h + 1],
                                              None, ALU.mult)
                        if half == 1:
                            _inc(ve, ins, 1)

                @block.tensor
                def _(tn, c=c, slot=slot, h=h, ven=ve.n, acn=ac.n):
                    tn.wait_ge(ve.sem, ven)
                    tn.wait_ge(ac.sem, acn)
                    _inc(pe, tn.matmul(ps_cs[:, :], ones_f[0:1, :],
                                       csf[0:1, 256 * h:256 * (h + 1)],
                                       start=True, stop=True), 1)

                @block.scalar
                def _(sc, pen=pe.n, ven=ve.n):
                    sc.wait_ge(pe.sem, pen)
                    sc.wait_ge(ve.sem, ven)
                    _inc(ac, sc.activation(expb[:, :], ps_cs[:, :], AF.Exp), 1)

                @block.vector
                def _(v, c=c, slot=slot, h=h, pen=pe.n, acn=ac.n, cxn=cxs[h % 2].n):
                    v.wait_ge(pe.sem, pen)
                    v.wait_ge(ac.sem, acn)
                    v.wait_ge(cxs[h % 2].sem, cxn)
                    for half in range(2):
                        v.tensor_scalar(eb[half][:, :], ps_cs[:, :],
                                        csn[slot][:, half * 64 + h:half * 64 + h + 1],
                                        0.0, ALU.add, ALU.min)
                    _inc(ve, v.tensor_tensor(cexb[h % 2][:, :], cfm[slot][:, :], expb[:, :], ALU.mult), 1)

                @block.scalar
                def _(sc, ven=ve.n, pen=pe.n):
                    sc.wait_ge(ve.sem, ven)
                    sc.wait_ge(pe.sem, pen)
                    _inc(ac, sc.activation(ebx[0][:, :], eb[0][:, :], AF.Exp), 1)
                    _inc(ac, sc.activation(ebx[1][:, :], eb[1][:, :], AF.Exp), 1)

                @block.vector
                def _(v, c=c, acn=ac.n):
                    v.wait_ge(ac.sem, acn)
                    v.tensor_tensor(pb[0][:, :], ebx[0][:, :],
                                    cbm_s[:, (c * 2 + 0) * 256:(c * 2 + 1) * 256], ALU.mult)
                    _inc(ve, v.tensor_tensor(pb[1][:, :], ebx[1][:, :],
                                             cbm_s[:, (c * 2 + 1) * 256:(c * 2 + 2) * 256], ALU.mult), 1)

                @block.tensor
                def _(tn, ven=ve.n):
                    tn.wait_ge(ve.sem, ven)
                    for half in range(2):
                        tn.matmul(ps_y[:, :], xdt[:, half * 64:(half + 1) * 64], pb[half][:, :],
                                  start=(half == 0), stop=(half == 1))
                    for half in range(2):
                        ins = tn.matmul(ps_s[:, :], bdec[:, half * ND:(half + 1) * ND],
                                        xdt[:, half * 64:(half + 1) * 64],
                                        start=(half == 0), stop=(half == 1))
                        if half == 1:
                            _inc(pe, ins, 1)

                @block.vector
                def _(v, c=c, slot=slot, h=h, hpar=hpar, g2=g2, base=base, pen=pe.n,
                      ys_need=16 * ys_cnt[g2 % 2], sp_need=sp_need_fn(c, h)):
                    v.wait_ge(pe.sem, pen)
                    if hpar == 0:
                        v.wait_ge(ys[g2 % 2].sem, ys_need)
                    dst = yst[g2 % 2][base:base + 64, :]
                    v.tensor_scalar(dst, xfm[slot][base:base + 64, g2 * 256:(g2 + 1) * 256],
                                    dbc[base:base + 64, h:h + 1], None, ALU.mult)
                    _inc(ve, v.tensor_tensor(dst, dst, ps_y[:, :], ALU.add), 1)
                    if sp_need is not None:
                        v.wait_ge(sps[c % 2].sem, sp_need)
                    cdcol = cdbc[:, 64 * c + h:64 * c + h + 1]
                    _inc(ve, v.scalar_tensor_tensor(saft[c % 2][:, 64 * h:64 * (h + 1)],
                                                    saft[(c + 1) % 2][:, 64 * h:64 * (h + 1)],
                                                    cdcol, ps_s[:, :], ALU.mult, ALU.add), 1)

                @block.gpsimd
                def _(g, c=c, h=h, hpar=hpar, g2=g2, ven=ve.n):
                    g.wait_ge(ve.sem, ven)
                    _inc(cxs[h % 2], g.dma_start(out=cexp_o[c, h], in_=cexb[h % 2][:, :]), 16)
                    if hpar == 1:
                        _inc(ys[g2 % 2], g.dma_start(out=y_o[128 * g2:128 * (g2 + 1), 256 * c:256 * (c + 1)],
                                                     in_=yst[g2 % 2][:, :]), 16)
                    if c >= 1:
                        _inc(sps[(c + 1) % 2], g.dma_start(out=sprev_o[c - 1, h],
                                                           in_=saft[(c + 1) % 2][:, 64 * h:64 * (h + 1)]), 16)
                    if c == NCH - 1:
                        _inc(mst, g.dma_start(out=sloc_o[h], in_=saft[c % 2][:, 64 * h:64 * (h + 1)]), 16)

                if hpar == 1:
                    ys_cnt[g2 % 2] += 1

        @block.sync
        def _(s):
            s.wait_ge(mst.sem, mst.n)
            s.wait_ge(cxs[0].sem, cxs[0].n)
            s.wait_ge(cxs[1].sem, cxs[1].n)
            s.wait_ge(ys[0].sem, ys[0].n)
            s.wait_ge(ys[1].sem, ys[1].n)
            s.wait_ge(sps[0].sem, sps[0].n)
            s.wait_ge(sps[1].sem, sps[1].n)
    return nc


def host_p2_inputs(r1, masks_dummy=None):
    xT = np.ascontiguousarray(r1["x_o"].T)
    bT = np.ascontiguousarray(r1["b_o"].T)
    dtT = np.ascontiguousarray(r1["dt_o"].T)
    dteT = np.ascontiguousarray(r1["dte_o"].T)
    csT = np.ascontiguousarray(r1["cs_o"].T)
    csf = np.ascontiguousarray(r1["cs_o"].reshape(64, NCH, 256).transpose(1, 0, 2).reshape(NCH, 16384))
    cdrow = np.ascontiguousarray(r1["dec_o"][:, 0:NCH].T.reshape(1, NCH * 64))
    return dict(
        xT=xT, bT=bT, cfm=r1["c_o"], xfm=r1["x_o"], dtT=dtT, dteT=dteT, csT=csT,
        csf=csf, cbm=r1["cbm_o"], dpar=None, cdrow=cdrow,
        onesin=np.ones((1, 128), np.float32),
    )



def build_p3():
    nc = bass.Bass("TRN2", target_bir_lowering=False)
    y_i = nc.dram_tensor("y_i", [DI, T], F32, kind="ExternalInput").ap()
    sg_i = nc.dram_tensor("sg_i", [4, H, ND, PD], F32, kind="ExternalInput").ap()
    sprev_i = nc.dram_tensor("sprev_i", [NCH, H, ND, PD], F32, kind="ExternalInput").ap()
    cexp_i = nc.dram_tensor("cexp_i", [NCH, H, ND, CHUNK], BF16, kind="ExternalInput").ap()
    z_i = nc.dram_tensor("z_i", [DI, T], BF16, kind="ExternalInput").ap()
    dgrow_i = nc.dram_tensor("dgrow", [1, 4 * 64], F32, kind="ExternalInput").ap()
    cdcrow_i = nc.dram_tensor("cdcrow", [1, NCH * 64], F32, kind="ExternalInput").ap()
    normw_i = nc.dram_tensor("normw", [128, 32], F32, kind="ExternalInput").ap()
    wout_i = nc.dram_tensor("wout", [16, 32, 128, 128], F32, kind="ExternalInput").ap()
    resid_i = nc.dram_tensor("resid", [DM, T], F32, kind="ExternalInput").ap()
    ones_i = nc.dram_tensor("onesin", [1, 128], F32, kind="ExternalInput").ap()
    ones2_i = nc.dram_tensor("ones2", [128, 1], F32, kind="ExternalInput").ap()

    resid_o = nc.dram_tensor("resid_o", [DM, T], F32, kind="ExternalOutput").ap()
    xln3_o = nc.dram_tensor("xln3_o", [DM, T], BF16, kind="ExternalOutput").ap()

    with contextlib.ExitStack() as stk:
        def sb(name, shape, dt):
            return stk.enter_context(nc.sbuf_tensor(name, shape, dt))

        def sem(name):
            return Ctr(stk.enter_context(nc.semaphore(name)))

        def psm(name, shape):
            return stk.enter_context(nc.psum_tensor(name, shape, F32))

        ones_c = sb("ones_c", [1, 128], BF16)
        ones2b = sb("ones2b", [128, 1], BF16)
        rowbf = sb("rowbf", [1, 256], BF16)
        dgbc = sb("dgbc", [128, 256], F32)
        cdcbc = sb("cdcbc", [128, 256], F32)
        normw = sb("normw_s", [128, 32], F32)
        epsc = sb("epsc", [1, 1], F32)
        sgb = [sb("sgb0", [128, 64], F32), sb("sgb1", [128, 64], F32)]
        spb = [sb("spb0", [128, 64], F32), sb("spb1", [128, 64], F32)]
        cxb = [sb("cxb0", [128, 256], BF16), sb("cxb1", [128, 256], BF16)]
        spt = sb("spt", [128, 64], BF16)
        sinb = sb("sinb", [128, H * 64], F32)
        acc = [sb("acc0", [128, 64], F32), sb("acc1", [128, 64], F32)]
        yst = [sb("yst0", [128, 256], F32), sb("yst1", [128, 256], F32)]
        zt = [sb("zt0", [128, 256], BF16), sb("zt1", [128, 256], BF16)]
        zsil = sb("zsil", [128, 256], BF16)
        ygbuf = sb("ygbuf", [128, 32 * T], BF16)
        ynbuf = ygbuf
        mbuf = sb("mbuf", [128, 16 * T], BF16)
        sqst = [sb("sqst0", [128, T], BF16), sb("sqst1", [128, T], BF16)]
        wbuf = [sb("wbuf0", [128, 4096], BF16), sb("wbuf1", [128, 4096], BF16)]
        srow = sb("srow", [2, T], F32)
        sqrow = sb("sqrow", [1, T], F32)
        scr0 = sb("scr0", [1, T], F32)
        scr1 = sb("scr1", [1, T], F32)
        murow = sb("murow", [1, T], F32)
        rsrow = sb("rsrow", [1, T], F32)
        murb = sb("murb", [1, T], BF16)
        rsrb = sb("rsrb", [1, T], BF16)
        mubc = sb("mubc", [128, T], F32)
        rsbc = sb("rsbc", [128, T], F32)
        rst = [sb("rst0", [128, T], F32), sb("rst1", [128, T], F32)]
        rot0 = sb("rot0", [128, T], F32)
        rot = [rot0, rot0]
        xlt = [sb("xlt0", [128, T], BF16), sb("xlt1", [128, T], BF16)]

        cst = sem("cst")
        sgs = [sem("sgs0"), sem("sgs1")]
        sls = [sem("sls0"), sem("sls1")]
        cxs = [sem("cxs0"), sem("cxs1")]
        yls = [sem("yls0"), sem("yls1")]
        zls = [sem("zls0"), sem("zls1")]
        wls = [sem("wls0"), sem("wls1")]
        rls = [sem("rls0"), sem("rls1")]
        ros = [sem("ros0"), sem("ros1")]
        xos = [sem("xos0"), sem("xos1")]
        mst = sem("mst")
        pe = sem("pe")
        ve = sem("ve")
        ac = sem("ac")

        ps_bc = psm("ps_bc", [128, 512])
        ps_y = psm("ps_y", [64, 256])
        ps_s0 = psm("ps_s0", [1, 512])
        ps_s1 = psm("ps_s1", [1, 512])
        ps_q0 = psm("ps_q0", [1, 512])
        ps_q1 = psm("ps_q1", [1, 512])
        ps_m0 = psm("ps_m0", [128, 512])
        ps_m1 = psm("ps_m1", [128, 512])
        block = stk.enter_context(nc.Block())

        @block.gpsimd
        def _(g):
            _inc(cst, g.dma_start(out=ones_c[:, :], in_=ones_i[:, :]), 16)
            _inc(cst, g.dma_start(out=ones2b[:, :], in_=ones2_i[:, :]), 16)

        @block.sync
        def _(s):
            _inc(cst, s.dma_start(out=normw[:, :], in_=normw_i[:, :]), 16)
            _inc(cst, s.dma_start(out=dgbc[0:1, :], in_=dgrow_i[:, :]), 16)
            _inc(cst, s.dma_start(out=cdcbc[0:1, :], in_=cdcrow_i[:, :]), 16)

        for rowsrc, dst in [(dgbc, dgbc), (cdcbc, cdcbc)]:
            @block.vector
            def _(v, rowsrc=rowsrc, cstn=cst.n, acn=ac.n):
                v.wait_ge(cst.sem, cstn)
                v.wait_ge(ac.sem, acn)
                _inc(ve, v.tensor_copy(rowbf[0:1, :], rowsrc[0:1, :]), 1)

            @block.tensor
            def _(tn, ven=ve.n):
                tn.wait_ge(ve.sem, ven)
                _inc(pe, tn.matmul(ps_bc[:, 0:256], ones_c[0:1, :], rowbf[0:1, :],
                                   start=True, stop=True), 1)

            @block.scalar
            def _(sc, dst=dst, pen=pe.n):
                sc.wait_ge(pe.sem, pen)
                _inc(ac, sc.copy(dst[:, :], ps_bc[:, 0:256]), 1)

        @block.vector
        def _(v, cstn=cst.n):
            v.wait_ge(cst.sem, cstn)
            _inc(ve, v.memset(epsc[:, :], EPS), 1)

        for h in range(H):
            for j in range(4):
                slot = j % 2

                @block.sync
                def _(s, h=h, j=j, slot=slot, ven=ve.n):
                    s.wait_ge(ve.sem, ven)
                    _inc(sgs[slot], s.dma_start(out=sgb[slot][:, :], in_=sg_i[j, h]), 16)

                @block.vector
                def _(v, h=h, j=j, slot=slot, need=16 * (h * 2 + j // 2 + 1), acn=ac.n):
                    v.wait_ge(sgs[slot].sem, need)
                    v.wait_ge(ac.sem, acn)
                    a_in, a_out = acc[(j + 1) % 2], acc[j % 2]
                    if j == 0:
                        ins = v.tensor_copy(a_out[:, :], sgb[slot][:, :])
                    elif j < 3:
                        ins = v.scalar_tensor_tensor(a_out[:, :], a_in[:, :],
                                                     dgbc[:, 64 * j + h:64 * j + h + 1],
                                                     sgb[slot][:, :], ALU.mult, ALU.add)
                    else:
                        ins = v.scalar_tensor_tensor(sinb[:, 64 * h:64 * (h + 1)], a_in[:, :],
                                                     dgbc[:, 64 * j + h:64 * j + h + 1],
                                                     sgb[slot][:, :], ALU.mult, ALU.add)
                    _inc(ve, ins, 1)

        yg_ac = {}
        for c in range(NCH):
            for h in range(H):
                hpar = h % 2
                g2 = h // 2
                sl = h % 2
                ysl = g2 % 2

                @block.sync
                def _(s, c=c, h=h, sl=sl, ysl=ysl, hpar=hpar, g2=g2, ven=ve.n, pen=pe.n, acn=ac.n):
                    s.wait_ge(ve.sem, ven)
                    s.wait_ge(ac.sem, acn)
                    _inc(sls[sl], s.dma_start(out=spb[sl][:, :], in_=sprev_i[c, h]), 16)
                    s.wait_ge(pe.sem, pen)
                    _inc(cxs[sl], s.dma_start(out=cxb[sl][:, :], in_=cexp_i[c, h]), 16)
                    if hpar == 0:
                        _inc(yls[ysl], s.dma_start(out=yst[ysl][:, :],
                                                   in_=y_i[128 * g2:128 * (g2 + 1), 256 * c:256 * (c + 1)]), 16)
                        _inc(zls[ysl], s.dma_start(out=zt[ysl][:, :],
                                                   in_=z_i[128 * g2:128 * (g2 + 1), 256 * c:256 * (c + 1)]), 16)

                nld = c * H + h + 1

                @block.vector
                def _(v, c=c, h=h, sl=sl, need=16 * ((c * H + h) // 2 + 1), pen=pe.n, acn=ac.n):
                    v.wait_ge(sls[sl].sem, need)
                    v.wait_ge(pe.sem, pen)
                    _inc(ve, v.scalar_tensor_tensor(spt[:, :], sinb[:, 64 * h:64 * (h + 1)],
                                                    cdcbc[:, 64 * c + h:64 * c + h + 1],
                                                    spb[sl][:, :], ALU.mult, ALU.add), 1)

                @block.tensor
                def _(tn, c=c, h=h, sl=sl, need=16 * ((c * H + h) // 2 + 1), ven=ve.n):
                    tn.wait_ge(cxs[sl].sem, need)
                    tn.wait_ge(ve.sem, ven)
                    _inc(pe, tn.matmul(ps_y[:, :], spt[:, :], cxb[sl][:, :],
                                       start=True, stop=True), 1)

                @block.vector
                def _(v, c=c, h=h, hpar=hpar, g2=g2, ysl=ysl, base=64 * hpar, pen=pe.n,
                      yneed=16 * (c * 16 + g2 // 2 + 1)):
                    v.wait_ge(pe.sem, pen)
                    if hpar == 0:
                        v.wait_ge(yls[ysl].sem, yneed)
                    ins = v.tensor_tensor(yst[ysl][base:base + 64, :], yst[ysl][base:base + 64, :],
                                          ps_y[:, :], ALU.add)
                    _inc(ve, ins, 1)

                if hpar == 1:
                    @block.scalar
                    def _(sc, c=c, g2=g2, ysl=ysl, ven=ve.n, zneed=16 * (c * 16 + g2 // 2 + 1)):
                        sc.wait_ge(ve.sem, ven)
                        sc.wait_ge(zls[ysl].sem, zneed)
                        _inc(ac, sc.activation(zsil[:, :], zt[ysl][:, :], AF.Silu), 1)

                    @block.vector
                    def _(v, c=c, g2=g2, ysl=ysl, acn=ac.n):
                        v.wait_ge(ac.sem, acn)
                        dst = ygbuf[:, g2 * T + 256 * c:g2 * T + 256 * (c + 1)]
                        _inc(ve, v.tensor_tensor(dst, yst[ysl][:, :], zsil[:, :], ALU.mult), 1)

                    yg_ac[(c, g2)] = ve.n

        for g in range(32):
            slot = g % 2

            @block.scalar
            def _(sc, g=g, slot=slot, ven=ve.n, pen=pe.n):
                sc.wait_ge(ve.sem, ven)
                if g >= 2:
                    sc.wait_ge(pe.sem, pen)
                _inc(ac, sc.activation(sqst[slot][:, :], ygbuf[:, g * T:(g + 1) * T], AF.Square), 1)

            @block.tensor
            def _(tn, g=g, slot=slot, acn=ac.n):
                tn.wait_ge(ac.sem, acn)
                for ci in range(2):
                    _inc(pe, tn.matmul([ps_s0, ps_s1][ci][:, :], ones2b[:, 0:1],
                                       sqst[slot][:, 512 * ci:512 * (ci + 1)],
                                       start=(g == 0), stop=(g == 31)), 1)

        def rowstats(tag, sum_ps, sq_ps, nfeat, have_sum):
            @block.scalar
            def _(sc, pen=pe.n):
                sc.wait_ge(pe.sem, pen)
                if have_sum:
                    _inc(ac, sc.copy(srow[0:1, 0:512], sum_ps[0][:, :]), 1)
                    _inc(ac, sc.copy(srow[0:1, 512:1024], sum_ps[1][:, :]), 1)
                _inc(ac, sc.copy(sqrow[0:1, 0:512], sq_ps[0][:, :]), 1)
                _inc(ac, sc.copy(sqrow[0:1, 512:1024], sq_ps[1][:, :]), 1)

            @block.vector
            def _(v, acn=ac.n, nf=float(nfeat), have_sum=have_sum):
                v.wait_ge(ac.sem, acn)
                if have_sum:
                    v.tensor_scalar(murow[:, :], srow[0:1, :], 1.0 / nf, None, ALU.mult)
                    v.tensor_scalar(scr0[:, :], sqrow[0:1, :], 1.0 / nf, None, ALU.mult)
                    v.tensor_tensor(scr1[:, :], murow[:, :], murow[:, :], ALU.mult)
                    v.tensor_sub(scr0[:, :], scr0[:, :], scr1[:, :])
                else:
                    v.memset(murow[:, :], 0.0)
                    v.tensor_scalar(scr0[:, :], sqrow[0:1, :], 1.0 / nf, None, ALU.mult)
                _inc(ve, v.tensor_scalar(scr0[:, :], scr0[:, :], EPS, None, ALU.add), 1)

            @block.scalar
            def _(sc, ven=ve.n):
                sc.wait_ge(ve.sem, ven)
                _inc(ac, sc.activation(scr1[:, :], scr0[:, :], AF.Sqrt), 1)

            @block.vector
            def _(v, acn=ac.n):
                v.wait_ge(ac.sem, acn)
                v.reciprocal(rsrow[:, :], scr1[:, :])
                v.tensor_copy(murb[:, :], murow[:, :])
                _inc(ve, v.tensor_copy(rsrb[:, :], rsrow[:, :]), 1)

            for rowb, dst in [(murb, mubc), (rsrb, rsbc)]:
                for ci in range(2):
                    @block.tensor
                    def _(tn, rowb=rowb, ci=ci, ven=ve.n, acn=ac.n):
                        tn.wait_ge(ve.sem, ven)
                        tn.wait_ge(ac.sem, acn)
                        _inc(pe, tn.matmul(ps_bc[:, 0:512], ones_c[0:1, :],
                                           rowb[:, 512 * ci:512 * (ci + 1)], start=True, stop=True), 1)

                    @block.scalar
                    def _(sc, dst=dst, ci=ci, pen=pe.n):
                        sc.wait_ge(pe.sem, pen)
                        _inc(ac, sc.copy(dst[:, 512 * ci:512 * (ci + 1)], ps_bc[:, 0:512]), 1)

        rowstats("rms", None, [ps_s0, ps_s1], DI, False)

        for g in range(32):
            @block.vector
            def _(v, g=g, acn=ac.n):
                v.wait_ge(ac.sem, acn)
                ro = rot[g % 2]
                dst = ygbuf[:, g * T:(g + 1) * T]
                v.tensor_tensor(ro[:, :], dst, rsbc[:, :], ALU.mult)
                _inc(ve, v.tensor_scalar(dst, ro[:, :], normw[:, g:g + 1], None, ALU.mult), 1)

        ve_ynorm = ve.n
        pe_mark = {}
        ev_mark = {}
        for ot in range(16):
            slot = ot % 2

            @block.gpsimd
            def _(g, ot=ot, slot=slot):
                if ot >= 2:
                    g.wait_ge(pe.sem, pe_mark[ot - 2])
                _inc(wls[slot], g.dma_start(out=wbuf[slot][:, :], in_=wout_i[ot].transpose([1, 0, 2])), 16)

            @block.tensor
            def _(tn, ot=ot, slot=slot, need=16 * (ot // 2 + 1), ven=ve_ynorm):
                tn.wait_ge(wls[slot].sem, need)
                tn.wait_ge(ve.sem, ven)
                if ot >= 1:
                    tn.wait_ge(ac.sem, ev_mark[ot - 1])
                for ci in range(2):
                    pm = [ps_m0, ps_m1][ci]
                    for k in range(32):
                        ins = tn.matmul(pm[:, 0:512], wbuf[slot][:, k * 128:(k + 1) * 128],
                                        ynbuf[:, k * T + 512 * ci:k * T + 512 * (ci + 1)],
                                        start=(k == 0), stop=(k == 31))
                        if ci == 1 and k == 31:
                            _inc(pe, ins, 1)

            pe_mark[ot] = pe.n

            @block.scalar
            def _(sc, ot=ot, pen=pe.n):
                sc.wait_ge(pe.sem, pen)
                sc.copy(mbuf[:, ot * T:ot * T + 512], ps_m0[:, 0:512])
                _inc(ac, sc.copy(mbuf[:, ot * T + 512:(ot + 1) * T], ps_m1[:, 0:512]), 1)

            ev_mark[ot] = ac.n

        for g in range(16):
            slot = g % 2

            @block.scalar
            def _(sc, g=g, slot=slot, acn=ac.n, pen=pe.n):
                sc.wait_ge(ac.sem, acn)
                if g >= 2:
                    sc.wait_ge(pe.sem, pen)
                _inc(ac, sc.activation(sqst[slot][:, :], mbuf[:, g * T:(g + 1) * T], AF.Square), 1)

            @block.tensor
            def _(tn, g=g, slot=slot, acn=ac.n):
                tn.wait_ge(ac.sem, acn)
                for ci in range(2):
                    _inc(pe, tn.matmul([ps_s0, ps_s1][ci][:, :], ones2b[:, 0:1],
                                       mbuf[:, g * T + 512 * ci:g * T + 512 * (ci + 1)],
                                       start=(g == 0), stop=(g == 15)), 1)
                    _inc(pe, tn.matmul([ps_q0, ps_q1][ci][:, :], ones2b[:, 0:1],
                                       sqst[slot][:, 512 * ci:512 * (ci + 1)],
                                       start=(g == 0), stop=(g == 15)), 1)

        rowstats("ln2", [ps_s0, ps_s1], [ps_q0, ps_q1], DM, True)

        for g in range(16):
            slot = g % 2

            @block.sync
            def _(s, g=g, slot=slot, ven=ve.n):
                s.wait_ge(ve.sem, ven)
                _inc(rls[slot], s.dma_start(out=rst[slot][:, :], in_=resid_i[128 * g:128 * (g + 1), :]), 16)

            @block.vector
            def _(v, g=g, slot=slot, need=16 * (g // 2 + 1), acn=ac.n, ron=ros[slot].n, pen=pe.n):
                v.wait_ge(rls[slot].sem, need)
                v.wait_ge(ac.sem, acn)
                v.wait_ge(ros[slot].sem, ron)
                if g >= 2:
                    v.wait_ge(pe.sem, pen)
                ro = rot[slot]
                v.tensor_sub(ro[:, :], mbuf[:, g * T:(g + 1) * T], mubc[:, :])
                v.tensor_tensor(ro[:, :], ro[:, :], rsbc[:, :], ALU.mult)
                v.tensor_tensor(ro[:, :], ro[:, :], rst[slot][:, :], ALU.add)
                _inc(ve, v.tensor_copy(xlt[slot][:, :], ro[:, :]), 1)

            @block.scalar
            def _(sc, g=g, slot=slot, ven=ve.n):
                sc.wait_ge(ve.sem, ven)
                _inc(ac, sc.activation(sqst[slot][:, :], xlt[slot][:, :], AF.Square), 1)

            @block.tensor
            def _(tn, g=g, slot=slot, acn=ac.n):
                tn.wait_ge(ac.sem, acn)
                for ci in range(2):
                    _inc(pe, tn.matmul([ps_s0, ps_s1][ci][:, :], ones2b[:, 0:1],
                                       xlt[slot][:, 512 * ci:512 * (ci + 1)],
                                       start=(g == 0), stop=(g == 15)), 1)
                    _inc(pe, tn.matmul([ps_q0, ps_q1][ci][:, :], ones2b[:, 0:1],
                                       sqst[slot][:, 512 * ci:512 * (ci + 1)],
                                       start=(g == 0), stop=(g == 15)), 1)

            @block.sync
            def _(s, g=g, slot=slot, ven=ve.n):
                s.wait_ge(ve.sem, ven)
                _inc(ros[slot], s.dma_start(out=resid_o[128 * g:128 * (g + 1), :], in_=rot[slot][:, :]), 16)

        rowstats("ln3", [ps_s0, ps_s1], [ps_q0, ps_q1], DM, True)

        for g in range(16):
            slot = g % 2

            @block.sync
            def _(s, g=g, slot=slot, ven=ve.n, need=16 * (8 + g // 2)):
                s.wait_ge(ve.sem, ven)
                s.wait_ge(ros[slot].sem, ros[slot].n)
                _inc(rls[slot], s.dma_start(out=rst[slot][:, :], in_=resid_o[128 * g:128 * (g + 1), :]), 16)

            @block.vector
            def _(v, g=g, slot=slot, need=16 * (8 + g // 2 + 1), acn=ac.n, xon=xos[slot].n):
                v.wait_ge(rls[slot].sem, need)
                v.wait_ge(ac.sem, acn)
                v.wait_ge(xos[slot].sem, xon)
                ro = rot[slot]
                v.tensor_sub(ro[:, :], rst[slot][:, :], mubc[:, :])
                _inc(ve, v.tensor_tensor(xlt[slot][:, :], ro[:, :], rsbc[:, :], ALU.mult), 1)

            @block.sync
            def _(s, g=g, slot=slot, ven=ve.n):
                s.wait_ge(ve.sem, ven)
                _inc(xos[slot], s.dma_start(out=xln3_o[128 * g:128 * (g + 1), :], in_=xlt[slot][:, :]), 16)

        @block.sync
        def _(s):
            s.wait_ge(mst.sem, mst.n)
            for x in (ros[0], ros[1], xos[0], xos[1]):
                s.wait_ge(x.sem, x.n)
    return nc


def host_p3_inputs(core, res1, res2, all_res1, all_res2, norm_w, wout_tiled, resid_own):
    j = core % 4
    grp = core // 4
    sg = np.zeros((4, H, ND, PD), np.float32)
    dg = np.ones((4, 64), np.float32)
    for jp in range(4):
        peer = 4 * grp + jp
        if jp < j:
            sg[jp] = all_res2[peer]["sloc_o"]
        if 1 <= jp <= j - 1:
            dg[jp] = all_res1[peer]["dec_o"][:, 8]
    sprev = np.zeros((NCH, H, ND, PD), np.float32)
    sprev[1:] = res2["sprev_o"]
    cdc = np.ascontiguousarray(res1["dec_o"][:, 4:4 + NCH].T.reshape(1, NCH * 64))
    return dict(
        y_i=res2["y_o"], sg_i=sg, sprev_i=sprev, cexp_i=res2["cexp_o"],
        z_i=np.ascontiguousarray(res1["z_o"][:, HALO:]),
        dgrow=np.ascontiguousarray(dg.reshape(1, 256)), cdcrow=cdc,
        normw=np.ascontiguousarray(np.asarray(norm_w, np.float32).reshape(32, 128).T),
        wout=wout_tiled, resid=resid_own,
        onesin=np.ones((1, 128), np.float32), ones2=np.ones((128, 1), np.float32),
    )



def build_ffn(kind):
    had = kind == "had"
    nc = bass.Bass("TRN2", target_bir_lowering=False)
    WDT = BF16 if not had else F32
    T2 = 512
    NIT = INTER // 128
    xln_i = nc.dram_tensor("xln_i", [DM, T], BF16, kind="ExternalInput").ap()
    resid_i = nc.dram_tensor("resid", [DM, T], F32, kind="ExternalInput").ap()
    wg_i = nc.dram_tensor("wg_i", [NIT, 16, 128, 128], WDT, kind="ExternalInput").ap()
    wu_i = nc.dram_tensor("wu_i", [NIT, 16, 128, 128], WDT, kind="ExternalInput").ap()
    wd_i = nc.dram_tensor("wd_i", [16, NIT, 128, 128], WDT, kind="ExternalInput").ap()
    ones_i = nc.dram_tensor("onesin", [1, 128], F32, kind="ExternalInput").ap()
    ones2_i = nc.dram_tensor("ones2", [128, 1], F32, kind="ExternalInput").ap()
    if had:
        su_ug_i = nc.dram_tensor("su_ug", [128, 16], F32, kind="ExternalInput").ap()
        su_dp_i = nc.dram_tensor("su_dp", [128, 64], F32, kind="ExternalInput").ap()
        wsup_i = nc.dram_tensor("wsup", [128, 64], F32, kind="ExternalInput").ap()
        wsgt_i = nc.dram_tensor("wsgt", [128, 64], F32, kind="ExternalInput").ap()
        wsdp_i = nc.dram_tensor("wsdp", [128, 16], F32, kind="ExternalInput").ap()
        h128a_i = nc.dram_tensor("h128a", [128, 128], F32, kind="ExternalInput").ap()
        h128b_i = nc.dram_tensor("h128b", [128, 128], F32, kind="ExternalInput").ap()
    resid_o = nc.dram_tensor("resid_o", [DM, T], F32, kind="ExternalOutput").ap()
    m_tmp = nc.dram_tensor("m_tmp", [DM, T], BF16, kind="Internal").ap()

    with contextlib.ExitStack() as stk:
        def sb(name, shape, dt):
            return stk.enter_context(nc.sbuf_tensor(name, shape, dt))

        def sem(name):
            return Ctr(stk.enter_context(nc.semaphore(name)))

        def psm(name, shape):
            return stk.enter_context(nc.psum_tensor(name, shape, F32))

        ones_c = sb("ones_c", [1, 128], BF16)
        ones2b = sb("ones2b", [128, 1], BF16)
        epsc = sb("epsc", [1, 1], F32)
        interbuf = sb("interbuf", [128, NIT * T2], BF16)
        xbuf = sb("xbuf", [128, 16 * T], BF16)
        wbufg = [sb("wbufg0", [128, 2048], BF16), sb("wbufg1", [128, 2048], BF16)]
        wbufu = [sb("wbufu0", [128, 2048], BF16), sb("wbufu1", [128, 2048], BF16)]
        wbufd = [sb("wbufd0", [128, 8192], BF16), sb("wbufd1", [128, 8192], BF16)]
        sgt = sb("sgt", [128, T2], BF16)
        upsc = sb("upsc", [128, T2], F32)
        btmp = sb("btmp", [128, T2], BF16) if had else None
        mstage = [sb("mstage0", [128, T2], BF16), sb("mstage1", [128, T2], BF16)]
        sqst = [sb("sqst0", [128, T2], BF16), sb("sqst1", [128, T2], BF16)]
        rst = [sb("rst0", [128, T], F32), sb("rst1", [128, T], F32)]
        rot = [sb("rot0", [128, T], F32), sb("rot1", [128, T], F32)]
        mld = [sb("mld0", [128, T], BF16), sb("mld1", [128, T], BF16)]
        srow = sb("srow", [1, T], F32)
        sqrow = sb("sqrow", [1, T], F32)
        scr0 = sb("scr0", [1, T], F32)
        scr1 = sb("scr1", [1, T], F32)
        murow = sb("murow", [1, T], F32)
        rsrow = sb("rsrow", [1, T], F32)
        murb = sb("murb", [1, T], BF16)
        rsrb = sb("rsrb", [1, T], BF16)
        mubc = sb("mubc", [128, T], BF16)
        rsbc = sb("rsbc", [128, T], BF16)
        if had:
            su_ug = sb("su_ug_s", [128, 16], F32)
            su_dp = sb("su_dp_s", [128, 64], F32)
            wsup64 = sb("wsup64", [128, 64], F32)
            wsgt64 = sb("wsgt64", [128, 64], F32)
            wsdp64 = sb("wsdp64", [128, 16], F32)

        cst = sem("cst")
        wgs = [sem("wgs0"), sem("wgs1")]
        wus = [sem("wus0"), sem("wus1")]
        wds = [sem("wds0"), sem("wds1")]
        rls = [sem("rls0"), sem("rls1")]
        mls = [sem("mls0"), sem("mls1")]
        mts = [sem("mts0"), sem("mts1")]
        ros = [sem("ros0"), sem("ros1")]
        pe = sem("pe")
        ve = sem("ve")
        ac = sem("ac")
        po = sem("po")

        ps_a = [psm("ps_a0", [128, 512]), psm("ps_a1", [128, 512])]
        ps_s = [psm("ps_s0", [1, 512]), psm("ps_s1", [1, 512])]
        ps_q = [psm("ps_q0", [1, 512]), psm("ps_q1", [1, 512])]
        ps_bc = psm("ps_bc", [128, 512])
        block = stk.enter_context(nc.Block())

        if had:
            h128a = sb("h128a_s", [128, 128], BF16)
            h128b = sb("h128b_s", [128, 128], BF16)

        @block.gpsimd
        def _(g):
            _inc(cst, g.dma_start(out=ones_c[:, :], in_=ones_i[:, :]), 16)
            _inc(cst, g.dma_start(out=ones2b[:, :], in_=ones2_i[:, :]), 16)
            if had:
                _inc(cst, g.dma_start(out=h128a[:, :], in_=h128a_i[:, :]), 16)
                _inc(cst, g.dma_start(out=h128b[:, :], in_=h128b_i[:, :]), 16)

        @block.sync
        def _(s):
            for g in range(16):
                _inc(cst, s.dma_start(out=xbuf[:, g * T:(g + 1) * T],
                                      in_=xln_i[128 * g:128 * (g + 1), :]), 16)
            if had:
                _inc(cst, s.dma_start(out=su_ug[:, :], in_=su_ug_i[:, :]), 16)
                _inc(cst, s.dma_start(out=su_dp[:, :], in_=su_dp_i[:, :]), 16)
                _inc(cst, s.dma_start(out=wsup64[:, :], in_=wsup_i[:, :]), 16)
                _inc(cst, s.dma_start(out=wsgt64[:, :], in_=wsgt_i[:, :]), 16)
                _inc(cst, s.dma_start(out=wsdp64[:, :], in_=wsdp_i[:, :]), 16)

        @block.vector
        def _(v, cstn=cst.n):
            v.wait_ge(cst.sem, cstn)
            v.memset(epsc[:, :], EPS)
            if had:
                v.tensor_scalar(wsup64[:, :], wsup64[:, :], 64.0, None, ALU.mult)
                v.tensor_scalar(wsgt64[:, :], wsgt64[:, :], 64.0, None, ALU.mult)
                v.tensor_scalar(wsdp64[:, :], wsdp64[:, :], 64.0, None, ALU.mult)
                for g in range(16):
                    xt = xbuf[:, g * T:(g + 1) * T]
                    v.tensor_scalar(xt, xt, su_ug[:, g:g + 1], None, ALU.mult)
            _inc(ve, v.memset(epsc[:, :], EPS), 1)

        if had:
            for g in range(16):
                @block.tensor
                def _(tn, g=g, ven=ve.n, acn=ac.n):
                    tn.wait_ge(ve.sem, ven)
                    tn.wait_ge(ac.sem, acn)
                    for ci in range(2):
                        _inc(pe, tn.matmul(ps_a[ci][:, :], h128a[:, :],
                                           xbuf[:, g * T + 512 * ci:g * T + 512 * (ci + 1)],
                                           start=True, stop=True), 1)

                @block.scalar
                def _(sc, g=g, pen=pe.n):
                    sc.wait_ge(pe.sem, pen)
                    sc.copy(xbuf[:, g * T:g * T + 512], ps_a[0][:, :])
                    _inc(ac, sc.copy(xbuf[:, g * T + 512:(g + 1) * T], ps_a[1][:, :]), 1)

            for s_ in range(4):
                step = 1 << s_
                for g0 in range(16):
                    if g0 & step:
                        continue
                    g1 = g0 + step
                    for th in range(2):
                        a = xbuf[:, g0 * T + th * T2:g0 * T + (th + 1) * T2]
                        b = xbuf[:, g1 * T + th * T2:g1 * T + (th + 1) * T2]

                        @block.vector
                        def _(v, a=a, b=b, acn=ac.n, pon=po.n):
                            v.wait_ge(ac.sem, acn)
                            v.wait_ge(po.sem, pon)
                            _inc(ve, v.tensor_add(btmp[:, :], a, b), 1)

                        @block.gpsimd
                        def _(g, a=a, b=b, ven=ve.n):
                            g.wait_ge(ve.sem, ven)
                            _inc(po, g.tensor_tensor(b, a, b, ALU.subtract), 1)

                        @block.scalar
                        def _(sc, a=a, pon=po.n, ven=ve.n):
                            sc.wait_ge(po.sem, pon)
                            sc.wait_ge(ve.sem, ven)
                            _inc(ac, sc.copy(a, btmp[:, :]), 1)

        x_po = po.n

        for th in range(2):
            pe_gu = {}
            sil_ac = {}
            int_ve = {}
            for it in range(NIT):
                slot = it % 2

                @block.gpsimd
                def _(g, th=th, it=it, slot=slot, pen=pe.n):
                    g.wait_ge(pe.sem, pen)
                    _inc(wgs[slot], g.dma_start(out=wbufg[slot][:, :], in_=wg_i[it].transpose([1, 0, 2])), 16)
                    _inc(wus[slot], g.dma_start(out=wbufu[slot][:, :], in_=wu_i[it].transpose([1, 0, 2])), 16)

                @block.tensor
                def _(tn, th=th, it=it, slot=slot, need=16 * (th * NIT // 2 + it // 2 + 1),
                      ven=ve.n, acn=ac.n, pon=x_po, cstn=cst.n):
                    tn.wait_ge(wgs[slot].sem, need)
                    tn.wait_ge(wus[slot].sem, need)
                    tn.wait_ge(cst.sem, cstn)
                    tn.wait_ge(ve.sem, ven)
                    tn.wait_ge(po.sem, pon)
                    tn.wait_ge(ac.sem, acn)
                    for k in range(16):
                        tn.matmul(ps_a[0][:, :], wbufg[slot][:, k * 128:(k + 1) * 128],
                                  xbuf[:, k * T + th * T2:k * T + (th + 1) * T2],
                                  start=(k == 0), stop=(k == 15))
                    for k in range(16):
                        ins = tn.matmul(ps_a[1][:, :], wbufu[slot][:, k * 128:(k + 1) * 128],
                                        xbuf[:, k * T + th * T2:k * T + (th + 1) * T2],
                                        start=(k == 0), stop=(k == 15))
                        if k == 15:
                            _inc(pe, ins, 1)

                pe_gu[(th, it)] = pe.n

                @block.scalar
                def _(sc, th=th, it=it, pen=pe.n, ven=ve.n):
                    sc.wait_ge(pe.sem, pen)
                    sc.wait_ge(ve.sem, ven)
                    if had:
                        ins = sc.activation(sgt[:, :], ps_a[0][:, :], AF.Silu, scale=wsgt64[:, it:it + 1])
                    else:
                        ins = sc.activation(sgt[:, :], ps_a[0][:, :], AF.Silu)
                    _inc(ac, ins, 1)

                @block.vector
                def _(v, th=th, it=it, acn=ac.n):
                    v.wait_ge(ac.sem, acn)
                    dst = interbuf[:, it * T2:(it + 1) * T2]
                    if had:
                        v.tensor_scalar(upsc[:, :], ps_a[1][:, :], wsup64[:, it:it + 1], None, ALU.mult)
                        ins = v.scalar_tensor_tensor(dst, upsc[:, :], su_dp[:, it:it + 1],
                                                     sgt[:, :], ALU.mult, ALU.mult)
                    else:
                        ins = v.tensor_tensor(dst, sgt[:, :], ps_a[1][:, :], ALU.mult)
                    _inc(ve, ins, 1)

            if had:
                for it in range(NIT):
                    @block.tensor
                    def _(tn, it=it, ven=ve.n, acn=ac.n):
                        tn.wait_ge(ve.sem, ven)
                        tn.wait_ge(ac.sem, acn)
                        _inc(pe, tn.matmul(ps_a[0][:, :], h128b[:, :],
                                           interbuf[:, it * T2:(it + 1) * T2],
                                           start=True, stop=True), 1)

                    @block.scalar
                    def _(sc, it=it, pen=pe.n):
                        sc.wait_ge(pe.sem, pen)
                        _inc(ac, sc.copy(interbuf[:, it * T2:(it + 1) * T2], ps_a[0][:, :]), 1)

                for s_ in range(6):
                    step = 1 << s_
                    for g0 in range(NIT):
                        if g0 & step:
                            continue
                        g1 = g0 + step
                        a = interbuf[:, g0 * T2:(g0 + 1) * T2]
                        b = interbuf[:, g1 * T2:(g1 + 1) * T2]

                        @block.vector
                        def _(v, a=a, b=b, acn=ac.n, pon=po.n):
                            v.wait_ge(ac.sem, acn)
                            v.wait_ge(po.sem, pon)
                            _inc(ve, v.tensor_add(btmp[:, :], a, b), 1)

                        @block.gpsimd
                        def _(g, a=a, b=b, ven=ve.n):
                            g.wait_ge(ve.sem, ven)
                            _inc(po, g.tensor_tensor(b, a, b, ALU.subtract), 1)

                        @block.scalar
                        def _(sc, a=a, pon=po.n, ven=ve.n):
                            sc.wait_ge(po.sem, pon)
                            sc.wait_ge(ve.sem, ven)
                            _inc(ac, sc.copy(a, btmp[:, :]), 1)

            pe_d = {}
            for ot in range(16):
                slot = ot % 2

                @block.gpsimd
                def _(g, th=th, ot=ot, slot=slot, pen=pe.n):
                    g.wait_ge(pe.sem, pen)
                    _inc(wds[slot], g.dma_start(out=wbufd[slot][:, :], in_=wd_i[ot].transpose([1, 0, 2])), 16)

                @block.tensor
                def _(tn, th=th, ot=ot, slot=slot, need=16 * (th * 8 + ot // 2 + 1),
                      ven=ve.n, acn=ac.n, pon=po.n):
                    tn.wait_ge(wds[slot].sem, need)
                    tn.wait_ge(ve.sem, ven)
                    tn.wait_ge(po.sem, pon)
                    tn.wait_ge(ac.sem, acn)
                    for k in range(NIT):
                        ins = tn.matmul(ps_a[0][:, :], wbufd[slot][:, k * 128:(k + 1) * 128],
                                        interbuf[:, k * T2:(k + 1) * T2],
                                        start=(k == 0), stop=(k == NIT - 1))
                        if k == NIT - 1:
                            _inc(pe, ins, 1)

                pe_d[ot] = pe.n

                @block.scalar
                def _(sc, th=th, ot=ot, slot=slot, pen=pe.n, mtn=mts[slot].n):
                    sc.wait_ge(pe.sem, pen)
                    sc.wait_ge(mts[slot].sem, mtn)
                    if had:
                        sc.activation(mstage[slot][:, :], ps_a[0][:, :], AF.Copy,
                                      scale=wsdp64[:, ot:ot + 1])
                    else:
                        sc.copy(mstage[slot][:, :], ps_a[0][:, :])
                    _inc(ac, sc.activation(sqst[slot][:, :], mstage[slot][:, :], AF.Square), 1)

                @block.tensor
                def _(tn, th=th, ot=ot, slot=slot, acn=ac.n):
                    tn.wait_ge(ac.sem, acn)
                    _inc(pe, tn.matmul(ps_s[th][:, :], ones2b[:, 0:1], mstage[slot][:, :],
                                       start=(ot == 0), stop=(ot == 15)), 1)
                    _inc(pe, tn.matmul(ps_q[th][:, :], ones2b[:, 0:1], sqst[slot][:, :],
                                       start=(ot == 0), stop=(ot == 15)), 1)

                @block.gpsimd
                def _(g, th=th, ot=ot, slot=slot, acn=ac.n):
                    g.wait_ge(ac.sem, acn)
                    _inc(mts[slot], g.dma_start(out=m_tmp[128 * ot:128 * (ot + 1), th * T2:(th + 1) * T2],
                                                in_=mstage[slot][:, :]), 16)

            prev_ped = pe_d

        @block.scalar
        def _(sc, pen=pe.n):
            sc.wait_ge(pe.sem, pen)
            sc.copy(srow[0:1, 0:512], ps_s[0][:, :])
            sc.copy(srow[0:1, 512:1024], ps_s[1][:, :])
            sc.copy(sqrow[0:1, 0:512], ps_q[0][:, :])
            _inc(ac, sc.copy(sqrow[0:1, 512:1024], ps_q[1][:, :]), 1)

        @block.vector
        def _(v, acn=ac.n):
            v.wait_ge(ac.sem, acn)
            v.tensor_scalar(murow[:, :], srow[0:1, :], 1.0 / DM, None, ALU.mult)
            v.tensor_scalar(scr0[:, :], sqrow[0:1, :], 1.0 / DM, None, ALU.mult)
            v.tensor_tensor(scr1[:, :], murow[:, :], murow[:, :], ALU.mult)
            v.tensor_sub(scr0[:, :], scr0[:, :], scr1[:, :])
            _inc(ve, v.tensor_scalar(scr0[:, :], scr0[:, :], EPS, None, ALU.add), 1)

        @block.scalar
        def _(sc, ven=ve.n):
            sc.wait_ge(ve.sem, ven)
            _inc(ac, sc.activation(scr1[:, :], scr0[:, :], AF.Sqrt), 1)

        @block.vector
        def _(v, acn=ac.n):
            v.wait_ge(ac.sem, acn)
            v.reciprocal(rsrow[:, :], scr1[:, :])
            v.tensor_copy(murb[:, :], murow[:, :])
            _inc(ve, v.tensor_copy(rsrb[:, :], rsrow[:, :]), 1)

        for rowb, dst in [(murb, mubc), (rsrb, rsbc)]:
            for ci in range(2):
                @block.tensor
                def _(tn, rowb=rowb, ci=ci, ven=ve.n, acn=ac.n):
                    tn.wait_ge(ve.sem, ven)
                    tn.wait_ge(ac.sem, acn)
                    _inc(pe, tn.matmul(ps_bc[:, 0:512], ones_c[0:1, :],
                                       rowb[:, 512 * ci:512 * (ci + 1)], start=True, stop=True), 1)

                @block.scalar
                def _(sc, dst=dst, ci=ci, pen=pe.n):
                    sc.wait_ge(pe.sem, pen)
                    _inc(ac, sc.copy(dst[:, 512 * ci:512 * (ci + 1)], ps_bc[:, 0:512]), 1)

        for g in range(16):
            slot = g % 2

            @block.sync
            def _(s, g=g, slot=slot, ven=ve.n):
                s.wait_ge(ve.sem, ven)
                _inc(rls[slot], s.dma_start(out=rst[slot][:, :], in_=resid_i[128 * g:128 * (g + 1), :]), 16)

            @block.gpsimd
            def _(gq, g=g, slot=slot, ven=ve.n, mtn=mts[slot].n):
                gq.wait_ge(ve.sem, ven)
                gq.wait_ge(mts[slot].sem, mtn)
                _inc(mls[slot], gq.dma_start(out=mld[slot][:, :], in_=m_tmp[128 * g:128 * (g + 1), :]), 16)

            @block.vector
            def _(v, g=g, slot=slot, rneed=16 * (g // 2 + 1), mneed=16 * (g // 2 + 1),
                  acn=ac.n, ron=ros[slot].n):
                v.wait_ge(rls[slot].sem, rneed)
                v.wait_ge(mls[slot].sem, mneed)
                v.wait_ge(ac.sem, acn)
                v.wait_ge(ros[slot].sem, ron)
                ro = rot[slot]
                v.tensor_sub(ro[:, :], mld[slot][:, :], mubc[:, :])
                v.tensor_tensor(ro[:, :], ro[:, :], rsbc[:, :], ALU.mult)
                _inc(ve, v.tensor_tensor(ro[:, :], ro[:, :], rst[slot][:, :], ALU.add), 1)

            @block.sync
            def _(s, g=g, slot=slot, ven=ve.n):
                s.wait_ge(ve.sem, ven)
                _inc(ros[slot], s.dma_start(out=resid_o[128 * g:128 * (g + 1), :], in_=rot[slot][:, :]), 16)

        @block.sync
        def _(s):
            for x in (ros[0], ros[1], mts[0], mts[1]):
                s.wait_ge(x.sem, x.n)
    return nc


_P1_CONST = {}


def host_p1_consts(layer, w_inproj, conv_w, conv_b, dt_bias, apar):
    key = ("p1", layer)
    if key not in _P1_CONST:
        cw = np.zeros((34 * 128, 4), np.float32)
        cw[:DXBC] = np.asarray(conv_w, np.float32)
        cb = np.zeros((34 * 128,), np.float32)
        if conv_b is not None:
            cb[:DXBC] = np.asarray(conv_b, np.float32)
        _P1_CONST[key] = dict(
            w_in=_tile_T(np.asarray(w_inproj, np.float32)),
            conv_w=cw.reshape(34, 128, 4),
            conv_b=np.ascontiguousarray(cb.reshape(34, 128).T),
            dtb=np.asarray(dt_bias, np.float32).reshape(64, 1),
            apar=np.asarray(apar, np.float32).reshape(64, 1),
            onesin=np.ones((1, 128), np.float32),
            masks=_tri_masks(),
        )
    return _P1_CONST[key]


def host_p1_inputs(core, resid_tokmajor, consts):
    t0 = core * T
    lo = t0 - HALO
    sl = np.zeros((TE, DM), np.float32)
    src_lo = max(lo, (core // 4) * 4 * T)
    sl[src_lo - lo:, :] = resid_tokmajor[src_lo:t0 + T, :]
    resid = np.ascontiguousarray(sl.T)
    return dict(resid=resid, **consts)


_NC_CACHE = {}


def _nc(key, builder, *a):
    if key not in _NC_CACHE:
        _NC_CACHE[key] = builder(*a)
    return _NC_CACHE[key]


def kernel(**inputs):
    cores = list(range(NCORES))
    u = np.ascontiguousarray(np.asarray(inputs["L0_attn_u"], np.float32).reshape(8192, DM))
    ones1 = np.ones((1, 128), np.float32)
    ones2 = np.ones((128, 1), np.float32)

    ncR = _nc("R", build_recon)
    inR = [host_recon_inputs(c, inputs["L0_ffn_gate_cp"], inputs["L0_ffn_up_cp"],
                             inputs["L0_ffn_down_cp"]) for c in cores]
    resR = run_bass_kernel_spmd(ncR, inR, core_ids=cores).results
    wiv = [np.concatenate([resR[c]["w_iv"][w] for c in cores], axis=0) for w in range(3)]
    wg_t = _tile_T(_dejump(wiv[0]).reshape(INTER, DM))
    wu_t = _tile_T(_dejump(wiv[1]).reshape(INTER, DM))
    wd_t = _tile_T(_dejump(wiv[2]).reshape(DM, INTER))

    resid_tok = u
    for layer in range(2):
        L = f"L{layer}"
        if layer == 0:
            p1c = host_p1_consts(0, inputs["L0_attn_in_proj_w"], inputs["L0_attn_conv_w"],
                                 None, inputs["L0_attn_dt_bias"], inputs["L0_attn_A_log"])
            nc1 = _nc("P1a", build_p1, True)
            dpar = np.asarray(inputs["L0_attn_D_param"], np.float32).reshape(1, 64)
            normw = inputs["L0_attn_norm_w"]
            wout_t = _tile_T(np.asarray(inputs["L0_attn_out_proj_w"], np.float32))
        else:
            p1c = host_p1_consts(1, inputs["L1_attn_in_proj_w"], inputs["L1_attn_conv_w"],
                                 inputs["L1_attn_conv_b"], inputs["L1_attn_dt_bias"],
                                 inputs["L1_attn_A"])
            nc1 = _nc("P1b", build_p1, False)
            dpar = np.asarray(inputs["L1_attn_D"], np.float32).reshape(1, 64)
            normw = inputs["L1_attn_norm_w"]
            wout_t = _tile_T(np.asarray(inputs["L1_attn_out_proj_w"], np.float32))

        in1 = [host_p1_inputs(c, resid_tok, p1c) for c in cores]
        res1 = run_bass_kernel_spmd(nc1, in1, core_ids=cores).results

        nc2 = _nc("P2", build_p2)
        in2 = []
        for c in cores:
            m = host_p2_inputs(res1[c])
            m["dpar"] = dpar
            in2.append(m)
        res2 = run_bass_kernel_spmd(nc2, in2, core_ids=cores).results

        nc3 = _nc("P3", build_p3)
        in3 = []
        for c in cores:
            resid_own = np.ascontiguousarray(resid_tok[c * T:(c + 1) * T].T)
            in3.append(host_p3_inputs(c, res1[c], res2[c], res1, res2, normw, wout_t, resid_own))
        res3 = run_bass_kernel_spmd(nc3, in3, core_ids=cores).results

        if layer == 0:
            ncF = _nc("F0", build_ffn, "spline")
            inF = [dict(xln_i=res3[c]["xln3_o"], resid=res3[c]["resid_o"],
                        wg_i=wg_t, wu_i=wu_t, wd_i=wd_t, onesin=ones1, ones2=ones2)
                   for c in cores]
        else:
            ncF = _nc("F1", build_ffn, "had")
            ws = np.asarray(inputs["L1_ffn_Wscale_ug"], np.float32)
            H128 = _hadamard(128)
            s1 = 1.0 / (np.sqrt(2048.0) * 64.0)
            s2 = 1.0 / (np.sqrt(8192.0) * 64.0)
            f1c = dict(
                wg_i=_tile_T(np.asarray(inputs["L1_ffn_W_gate"], np.float32)),
                wu_i=_tile_T(np.asarray(inputs["L1_ffn_W_up"], np.float32)),
                wd_i=_tile_T(np.asarray(inputs["L1_ffn_W_down"], np.float32)),
                su_ug=np.ascontiguousarray(np.asarray(inputs["L1_ffn_SU_ug"], np.float32).reshape(16, 128).T),
                su_dp=np.ascontiguousarray(np.asarray(inputs["L1_ffn_SU_dp"], np.float32).reshape(64, 128).T),
                wsup=np.ascontiguousarray(ws[:INTER].reshape(64, 128).T),
                wsgt=np.ascontiguousarray(ws[INTER:].reshape(64, 128).T),
                wsdp=np.ascontiguousarray(np.asarray(inputs["L1_ffn_Wscale_dp"], np.float32).reshape(16, 128).T),
                h128a=(H128 * s1).astype(np.float32),
                h128b=(H128 * s2).astype(np.float32),
                onesin=ones1, ones2=ones2,
            )
            inF = [dict(xln_i=res3[c]["xln3_o"], resid=res3[c]["resid_o"], **f1c)
                   for c in cores]
        resF = run_bass_kernel_spmd(ncF, inF, core_ids=cores).results
        resid_tok = np.concatenate([np.asarray(resF[c]["resid_o"]).T for c in cores], axis=0)

    return resid_tok.reshape(2, 4096, DM).astype(np.float32)


# version 04 (final)
# speedup vs baseline: 1.1203x; 1.1203x over previous
import contextlib
import numpy as np
import ml_dtypes
import concourse.bass as bass
import concourse.mybir as mybir
from concourse.bass_utils import run_bass_kernel_spmd

F32 = mybir.dt.float32
BF16 = mybir.dt.bfloat16
AF = mybir.ActivationFunctionType
ALU = mybir.AluOpType
BF = ml_dtypes.bfloat16
ET = mybir.EngineType

NCORES = 8
T = 1024
HALO = 8
TE = T + HALO
CHUNK = 256
NCH = 4
H = 64
PD = 64
ND = 128
DM = 2048
DI = 4096
DXBC = 4352
DPROJ = 8512
INTER = 8192
EPS = 1e-5

N_CTRL = 130156
NSAMP = INTER * DM
NIV = N_CTRL - 1
NIV_PAD = 131072
IVC = NIV_PAD // NCORES

ALL_ENG = [ET.PE, ET.DVE, ET.Activation, ET.SP, ET.Pool]



_SP_CACHE = {}


def _sp():
    if "t" not in _SP_CACHE:
        t = np.linspace(0.0, N_CTRL - 1.0, NSAMP, dtype=np.float64)
        iv = np.clip(np.floor(t).astype(np.int64), 0, N_CTRL - 2)
        start = np.zeros(NIV + 1, dtype=np.int64)
        start[:NIV] = np.searchsorted(iv, np.arange(NIV), side="left")
        start[NIV] = NSAMP
        lens = np.diff(start)
        phi = t[start[:NIV]] - np.arange(NIV)
        c = np.zeros((4, 4))
        c[0] = [0.0, -0.5, 1.0, -0.5]
        c[1] = [1.0, 0.0, -2.5, 1.5]
        c[2] = [0.0, 0.5, 2.0, -1.5]
        c[3] = [0.0, 0.0, -0.5, 0.5]
        binom = np.array([[1, 0, 0, 0],
                          [1, 1, 0, 0],
                          [1, 2, 1, 0],
                          [1, 3, 3, 1]], dtype=np.float64)
        W = np.zeros((4, 4, NIV_PAD), dtype=np.float64)
        for e in range(4):
            for m in range(4):
                acc = np.zeros(NIV)
                for d in range(e, 4):
                    acc += binom[d, e] * (phi ** (d - e)) * c[m, d]
                W[e, m, :NIV] = acc
        r = (N_CTRL - 1.0) / (NSAMP - 1.0)
        s = np.arange(129, dtype=np.float64) * r
        u = np.stack([s ** e for e in range(4)], axis=0)
        ubd = np.zeros((12, 387))
        for tt in range(3):
            for e in range(4):
                ubd[4 * tt + e, 129 * tt:129 * tt + 129] = u[e]
        idx_rows = np.repeat(np.arange(NIV), lens)
        idx_slot = (np.arange(NSAMP) - start[idx_rows]).astype(np.int64)
        _SP_CACHE.update(t=t, start=start, lens=lens, W_em=W.astype(np.float32),
                         u4=u.astype(np.float32), ubd=ubd.astype(np.float32),
                         idx_rows=idx_rows, idx_slot=idx_slot)
    return _SP_CACHE


def _cp_padded(cp):
    tail = NIV_PAD + 8 - 1 - N_CTRL
    return np.concatenate([cp[0:1], cp, np.full(tail, cp[-1], cp.dtype)]).astype(np.float32)


def _dejump(w_iv_full):
    sp = _sp()
    return w_iv_full[sp["idx_rows"], sp["idx_slot"]]


def _hadamard(n):
    h = np.array([[1.0]])
    while h.shape[0] < n:
        h = np.block([[h, h], [h, -h]])
    return h


def _tri_masks():
    m = np.zeros((2, 128, CHUNK), dtype=np.float32)
    for hs in range(2):
        for s in range(128):
            m[hs, s, 128 * hs + s:] = 1.0
    return m


def _tile_T(w):
    O, K = w.shape
    Ot = (O + 127) // 128
    Kt = (K + 127) // 128
    wp = np.zeros((Ot * 128, Kt * 128), dtype=w.dtype)
    wp[:O, :K] = w
    return np.ascontiguousarray(wp.reshape(Ot, 128, Kt, 128).transpose(0, 2, 3, 1))



class Ctr:
    def __init__(self, sem):
        self.sem = sem
        self.n = 0

    def inc(self, instr, k=1):
        self.n += k
        return instr


def _inc(ctr, instr, k):
    ctr.n += k
    instr.then_inc(ctr.sem, k)
    return instr



def build_recon():
    nc = bass.Bass("TRN2", target_bir_lowering=False)
    NW = IVC // 1024
    G = IVC // 384
    REM = (IVC - G * 384) // 128
    cpw = nc.dram_tensor("cpw", [3, IVC + 8], F32, kind="ExternalInput").ap()
    wem = nc.dram_tensor("wem", [16, NW, 1024], F32, kind="ExternalInput").ap()
    ubd_i = nc.dram_tensor("ubd", [12, 387], F32, kind="ExternalInput").ap()
    u4_i = nc.dram_tensor("u4", [4, 129], F32, kind="ExternalInput").ap()
    w_iv = nc.dram_tensor("w_iv", [3, IVC, 129], BF16, kind="ExternalOutput").ap()
    h_dram = nc.dram_tensor("h_dram", [3, 4, IVC], BF16, kind="Internal").ap()

    with (
        nc.sbuf_tensor("cps", [NW, 1024 + 8], F32) as cps,
        nc.sbuf_tensor("wemt", [NW, 16 * 1024], F32) as wemt,
        nc.sbuf_tensor("hacc", [NW, 1024], F32) as hacc,
        nc.sbuf_tensor("htmp", [NW, 1024], F32) as htmp,
        nc.sbuf_tensor("hbf", [NW, 1024], BF16) as hbf,
        nc.sbuf_tensor("hsb", [12, IVC // 128 * 128 // 3 + 256], BF16) as hsb,
        nc.sbuf_tensor("ubd_t", [12, 387], BF16) as ubd_t,
        nc.sbuf_tensor("u4_t", [4, 129], BF16) as u4_t,
        nc.sbuf_tensor("stage", [128, 4 * 387], BF16) as stage,
        nc.psum_tensor("ps0", [128, 387], F32) as ps0,
        nc.psum_tensor("ps1", [128, 387], F32) as ps1,
        nc.semaphore("ld") as ld_s,
        nc.semaphore("dv") as dv_s,
        nc.semaphore("hs") as hs_s,
        nc.semaphore("mm") as mm_s,
        nc.semaphore("ev") as ev_s,
        nc.semaphore("st") as st_s,
        nc.Block() as block,
    ):
        ld = Ctr(ld_s)
        dv = Ctr(dv_s)
        hs = Ctr(hs_s)
        mm = Ctr(mm_s)
        ev = Ctr(ev_s)
        st = Ctr(st_s)
        ps = [ps0, ps1]
        GM = G * 128

        @block.gpsimd
        def _(g):
            _inc(ld, g.dma_start(out=ubd_t[:, :], in_=ubd_i[:, :]), 16)
            _inc(ld, g.dma_start(out=u4_t[:, :], in_=u4_i[:, :]), 16)
            _inc(ld, g.dma_start(out=wemt[:, :], in_=wem.transpose([1, 0, 2])), 16)

        gi = 0
        for w in range(3):
            @block.sync
            def _(s, w=w, dvn=dv.n):
                if w > 0:
                    s.wait_ge(dv.sem, dvn)
                src = cpw[w, :]
                src_ov = bass.AP(src.tensor, src.offset, [[1024, NW], [1, 1032]])
                _inc(ld, s.dma_start(out=cps[:, :], in_=src_ov), 16)

            ld_now = ld.n
            for e in range(4):
                @block.vector
                def _(v, w=w, e=e, ld_now=ld_now, hsn=hs.n):
                    if e == 0:
                        v.wait_ge(ld.sem, ld_now)
                    v.wait_ge(hs.sem, hsn)
                    for m in range(4):
                        wslice = wemt[:, (e * 4 + m) * 1024:(e * 4 + m + 1) * 1024]
                        cslice = cps[:, m:m + 1024]
                        if m == 0:
                            v.tensor_tensor(hacc[:, :], wslice, cslice, ALU.mult)
                        else:
                            v.tensor_tensor(htmp[:, :], wslice, cslice, ALU.mult)
                            if m < 3:
                                v.tensor_add(hacc[:, :], hacc[:, :], htmp[:, :])
                            else:
                                _inc(dv, v.tensor_add(hbf[:, :], hacc[:, :], htmp[:, :]), 1)

                @block.sync
                def _(s, w=w, e=e, dvn=dv.n):
                    s.wait_ge(dv.sem, dvn)
                    _inc(hs, s.dma_start(out=h_dram[w, e, :].rearrange("(p f) -> p f", p=NW),
                                         in_=hbf[:, :]), 16)

            @block.scalar
            def _(s, w=w, hsn=hs.n, mmn=mm.n):
                s.wait_ge(hs.sem, hsn)
                if w > 0:
                    s.wait_ge(mm.sem, mmn)
                hh = h_dram[w]
                for e in range(4):
                    for tt in range(3):
                        src = bass.AP(hh.tensor, hh.offset + e * IVC + 128 * tt, [[384, G], [1, 128]])
                        _inc(hs, s.dma_start(out=hsb[4 * tt + e:4 * tt + e + 1, 0:GM], in_=src), 16)
                _inc(hs, s.dma_start(out=hsb[0:4, GM:GM + 256], in_=h_dram[w][:, G * 384:IVC]), 16)

            hs_ready = hs.n
            for g in range(G + REM):
                psb = ps[gi % 2]

                @block.tensor
                def _(tn, g=g, gi=gi, psb=psb, hs_ready=hs_ready):
                    if g == 0:
                        tn.wait_ge(hs.sem, hs_ready)
                        tn.wait_ge(ld.sem, 32)
                    if gi >= 2:
                        tn.wait_ge(ev.sem, gi - 1)
                    if g < G:
                        _inc(mm, tn.matmul(psb[:, :], hsb[:, g * 128:(g + 1) * 128],
                                           ubd_t[:, :], start=True, stop=True), 1)
                    else:
                        tt = g - G
                        _inc(mm, tn.matmul(psb[:, 0:129], hsb[0:4, GM + tt * 128:GM + (tt + 1) * 128],
                                           u4_t[:, :], start=True, stop=True), 1)

                @block.scalar
                def _(sc, g=g, gi=gi, psb=psb):
                    sc.wait_ge(mm.sem, gi + 1)
                    if gi >= 4:
                        sc.wait_ge(st.sem, 16 * (gi - 3))
                    wdt = 387 if g < G else 129
                    _inc(ev, sc.copy(stage[:, (gi % 4) * 387:(gi % 4) * 387 + wdt], psb[:, 0:wdt]), 1)

                @block.gpsimd
                def _(gp, g=g, gi=gi, w=w):
                    gp.wait_ge(ev.sem, gi + 1)
                    if g < G:
                        base = w_iv[w]
                        dst = bass.AP(base.tensor, base.offset + g * 384 * 129,
                                      [[129, 128], [128 * 129, 3], [1, 129]])
                        wdt = 387
                    else:
                        tt = g - G
                        dst = w_iv[w][G * 384 + tt * 128:G * 384 + (tt + 1) * 128, :]
                        wdt = 129
                    _inc(st, gp.dma_start(out=dst, in_=stage[:, (gi % 4) * 387:(gi % 4) * 387 + wdt]), 16)

                gi += 1

        @block.sync
        def _(s):
            s.wait_ge(st.sem, st.n)
    return nc


def host_recon_inputs(core, gate_cp, up_cp, down_cp):
    sp = _sp()
    cps = []
    for cp in (gate_cp, up_cp, down_cp):
        pad = _cp_padded(np.asarray(cp))
        cps.append(pad[IVC * core: IVC * core + IVC + 8])
    wem = sp["W_em"][:, :, IVC * core: IVC * (core + 1)].reshape(16, IVC // 1024, 1024)
    return {
        "cpw": np.stack(cps).astype(np.float32),
        "wem": np.ascontiguousarray(wem),
        "ubd": sp["ubd"],
        "u4": sp["u4"],
    }



def build_p1(a_from_log):
    nc = bass.Bass("TRN2", target_bir_lowering=False)
    NOT = 67
    resid_i = nc.dram_tensor("resid", [DM, TE], F32, kind="ExternalInput").ap()
    w_in = nc.dram_tensor("w_in", [NOT, 16, 128, 128], F32, kind="ExternalInput").ap()
    conv_w = nc.dram_tensor("conv_w", [34, 128, 4], F32, kind="ExternalInput").ap()
    conv_b = nc.dram_tensor("conv_b", [128, 34], F32, kind="ExternalInput").ap()
    dtb_i = nc.dram_tensor("dtb", [64, 1], F32, kind="ExternalInput").ap()
    apar_i = nc.dram_tensor("apar", [64, 1], F32, kind="ExternalInput").ap()
    ones_i = nc.dram_tensor("onesin", [1, 128], F32, kind="ExternalInput").ap()
    masks_i = nc.dram_tensor("masks", [2, 128, 256], F32, kind="ExternalInput").ap()

    z_o = nc.dram_tensor("z_o", [DI, TE], BF16, kind="ExternalOutput").ap()
    x_o = nc.dram_tensor("x_o", [DI, T], BF16, kind="ExternalOutput").ap()
    b_o = nc.dram_tensor("b_o", [ND, T], BF16, kind="ExternalOutput").ap()
    c_o = nc.dram_tensor("c_o", [ND, T], BF16, kind="ExternalOutput").ap()
    dt_o = nc.dram_tensor("dt_o", [64, T], F32, kind="ExternalOutput").ap()
    cs_o = nc.dram_tensor("cs_o", [64, T], F32, kind="ExternalOutput").ap()
    dte_o = nc.dram_tensor("dte_o", [64, T], F32, kind="ExternalOutput").ap()
    dec_o = nc.dram_tensor("dec_o", [64, 16], F32, kind="ExternalOutput").ap()
    cbm_o = nc.dram_tensor("cbm_o", [NCH, 2, 128, 256], BF16, kind="ExternalOutput").ap()

    CH = [(0, 512), (512, 512), (1024, 8)]

    with contextlib.ExitStack() as stk:
        def sb(name, shape, dt):
            return stk.enter_context(nc.sbuf_tensor(name, shape, dt))

        def sem(name):
            return Ctr(stk.enter_context(nc.semaphore(name)))

        ones_c = sb("ones_c", [1, 128], BF16)
        sumsel = sb("sumsel", [128, 2], BF16)
        sqsel = sb("sqsel", [128, 2], BF16)
        xln = sb("xln", [128, 16 * TE], BF16)
        xbc = sb("xbc", [128, 34 * TE], BF16)
        bsb = sb("bsb", [128, T], BF16)
        csb = sb("csb", [128, T], BF16)
        masksb = sb("masksb", [128, 512], BF16)
        wbuf = [sb("wbuf0", [128, 2048], BF16), sb("wbuf1", [128, 2048], BF16)]
        zst = [sb("zst0", [128, TE], BF16), sb("zst1", [128, TE], BF16)]
        cvst = sb("cvst", [128, T], BF16)
        cacc = sb("cacc", [128, T], F32)
        convw = sb("convw", [128, 34 * 4], F32)
        convb = sb("convb", [128, 34], F32)
        srow = sb("srow", [2, TE], F32)
        sqrow = sb("sqrow", [1, TE], F32)
        scr0 = sb("scr0", [1, TE], F32)
        scr1 = sb("scr1", [1, TE], F32)
        murow = sb("murow", [1, TE], F32)
        rsrow = sb("rsrow", [1, TE], F32)
        murb = sb("murb", [1, TE], BF16)
        rsrb = sb("rsrb", [1, TE], BF16)
        mubc = sb("mubc", [128, TE], F32)
        rsbc = sb("rsbc", [128, TE], F32)
        rt = [sb("rt0", [128, TE], F32), sb("rt1", [128, TE], F32)]
        castb = [sb("cast0", [128, TE], BF16), sb("cast1", [128, TE], BF16)]
        sqb = [sb("sq0", [128, TE], BF16), sb("sq1", [128, TE], BF16)]
        dts = sb("dts", [64, 2 * TE + 8], F32)
        dtv = sb("dtv", [64, T], F32)
        csv = sb("csv", [64, T], F32)
        dtev = sb("dtev", [64, T], F32)
        decv = sb("decv", [64, 16], F32)
        zeros64 = sb("zeros64", [64, 256], F32)
        acol = sb("acol", [64, 1], F32)
        onecol = sb("onecol", [64, 1], F32)
        epsc = sb("epsc", [1, 1], F32)
        cbst = sb("cbst", [128, 256], BF16)

        cst = sem("cst")
        sqs = sem("sqs")
        rts = [sem("rts0"), sem("rts1")]
        wbs = [sem("wbs0"), sem("wbs1")]
        zss = [sem("zss0"), sem("zss1")]
        cvs = sem("cvs")
        cbs = sem("cbs")
        mst = sem("mst")
        pe = sem("pe")
        ve = sem("ve")
        ac = sem("ac")
        block = stk.enter_context(nc.Block())

        @block.gpsimd
        def _(g):
            _inc(cst, g.dma_start(out=ones_c[:, :], in_=ones_i[:, :]), 16)
            _inc(cst, g.dma_start(out=masksb[:, 0:256], in_=masks_i[0]), 16)
            _inc(cst, g.dma_start(out=masksb[:, 256:512], in_=masks_i[1]), 16)

        @block.sync
        def _(s):
            _inc(cst, s.dma_start(out=convw[:, :], in_=conv_w.transpose([1, 0, 2])), 16)
            _inc(cst, s.dma_start(out=convb[:, :], in_=conv_b[:, :]), 16)
            _inc(cst, s.dma_start(out=dts[:, 2 * TE:2 * TE + 1], in_=dtb_i[:, :]), 16)
            _inc(cst, s.dma_start(out=dts[:, 2 * TE + 1:2 * TE + 2], in_=apar_i[:, :]), 16)

        @block.vector
        def _(v):
            v.wait_ge(cst.sem, cst.n)
            v.memset(sumsel[:, 0:1], 1.0)
            v.memset(sumsel[:, 1:2], 0.0)
            v.memset(sqsel[:, 0:1], 0.0)
            v.memset(sqsel[:, 1:2], 1.0)
            v.memset(zeros64[:, :], 0.0)
            v.memset(onecol[:, :], 1.0)
            _inc(ve, v.memset(epsc[:, :], EPS), 1)

        with (
            nc.psum_tensor("st0", [2, 512], F32) as st0,
            nc.psum_tensor("st1", [2, 512], F32) as st1,
            nc.psum_tensor("st2", [2, 8], F32) as st2,
            nc.psum_tensor("bc", [128, 512], F32) as bc_ps,
        ):
            stp = [st0, st1, st2]
            for g in range(16):
                slot = g % 2

                @block.sync
                def _(s, g=g, slot=slot, ven=ve.n):
                    s.wait_ge(ve.sem, ven)
                    _inc(rts[slot], s.dma_start(out=rt[slot][:, :],
                                                in_=resid_i[128 * g:128 * (g + 1), :]), 16)

                @block.vector
                def _(v, g=g, slot=slot, need=16 * (g // 2 + 1), pen=pe.n):
                    v.wait_ge(rts[slot].sem, need)
                    if g >= 2:
                        v.wait_ge(pe.sem, pen - 6)
                    _inc(ve, v.tensor_copy(castb[slot][:, :], rt[slot][:, :]), 1)

                @block.scalar
                def _(sc, g=g, slot=slot, ven=ve.n, pen=pe.n):
                    sc.wait_ge(ve.sem, ven)
                    if g >= 2:
                        sc.wait_ge(pe.sem, pen - 6)
                    _inc(ac, sc.activation(sqb[slot][:, :], castb[slot][:, :], AF.Square), 1)

                @block.tensor
                def _(tn, g=g, slot=slot, ven=ve.n, acn=ac.n):
                    tn.wait_ge(ve.sem, ven)
                    tn.wait_ge(ac.sem, acn)
                    for ci, (c0, cw) in enumerate(CH):
                        _inc(pe, tn.matmul(stp[ci][:, 0:cw], sumsel[:, :], castb[slot][:, c0:c0 + cw],
                                           start=(g == 0), stop=(g == 15)), 1)
                        _inc(pe, tn.matmul(stp[ci][:, 0:cw], sqsel[:, :], sqb[slot][:, c0:c0 + cw],
                                           start=False, stop=(g == 15)), 1)

            @block.scalar
            def _(sc, pen=pe.n):
                sc.wait_ge(pe.sem, pen)
                for ci, (c0, cw) in enumerate(CH):
                    _inc(ac, sc.copy(srow[:, c0:c0 + cw], stp[ci][:, 0:cw]), 1)

            @block.gpsimd
            def _(g, acn=ac.n):
                g.wait_ge(ac.sem, acn)
                _inc(sqs, g.dma_start(out=sqrow[:, :], in_=srow[1:2, :]), 16)

            @block.vector
            def _(v, acn=ac.n):
                v.wait_ge(ac.sem, acn)
                v.wait_ge(sqs.sem, 16)
                v.tensor_scalar(murow[:, :], srow[0:1, :], 1.0 / DM, None, ALU.mult)
                v.tensor_scalar(scr0[:, :], sqrow[0:1, :], 1.0 / DM, None, ALU.mult)
                v.tensor_tensor(scr1[:, :], murow[:, :], murow[:, :], ALU.mult)
                v.tensor_sub(scr0[:, :], scr0[:, :], scr1[:, :])
                _inc(ve, v.tensor_scalar(scr0[:, :], scr0[:, :], EPS, None, ALU.add), 1)

            @block.scalar
            def _(sc, ven=ve.n):
                sc.wait_ge(ve.sem, ven)
                _inc(ac, sc.activation(scr1[:, :], scr0[:, :], AF.Sqrt), 1)

            @block.vector
            def _(v, acn=ac.n):
                v.wait_ge(ac.sem, acn)
                v.reciprocal(rsrow[:, :], scr1[:, :])
                v.tensor_copy(murb[:, :], murow[:, :])
                _inc(ve, v.tensor_copy(rsrb[:, :], rsrow[:, :]), 1)

            for rowb, dst in [(murb, mubc), (rsrb, rsbc)]:
                for c0, cw in CH:
                    @block.tensor
                    def _(tn, rowb=rowb, c0=c0, cw=cw, ven=ve.n, acn=ac.n):
                        tn.wait_ge(ve.sem, ven)
                        tn.wait_ge(ac.sem, acn)
                        _inc(pe, tn.matmul(bc_ps[:, 0:cw], ones_c[0:1, :], rowb[:, c0:c0 + cw],
                                           start=True, stop=True), 1)

                    @block.scalar
                    def _(sc, dst=dst, c0=c0, cw=cw, pen=pe.n):
                        sc.wait_ge(pe.sem, pen)
                        _inc(ac, sc.copy(dst[:, c0:c0 + cw], bc_ps[:, 0:cw]), 1)

            for g in range(16):
                slot = g % 2

                @block.sync
                def _(s, g=g, slot=slot, ven=ve.n):
                    s.wait_ge(ve.sem, ven)
                    _inc(rts[slot], s.dma_start(out=rt[slot][:, :],
                                                in_=resid_i[128 * g:128 * (g + 1), :]), 16)

                @block.vector
                def _(v, g=g, slot=slot, need=16 * (8 + g // 2 + 1), acn=ac.n):
                    v.wait_ge(rts[slot].sem, need)
                    v.wait_ge(ac.sem, acn)
                    xt = xln[:, g * TE:(g + 1) * TE]
                    v.tensor_sub(rt[slot][:, :], rt[slot][:, :], mubc[:, :])
                    _inc(ve, v.tensor_tensor(xt, rt[slot][:, :], rsbc[:, :], ALU.mult), 1)

        with (
            nc.psum_tensor("pp0", [128, 512], F32) as pp0,
            nc.psum_tensor("pp1", [128, 512], F32) as pp1,
            nc.psum_tensor("pp2", [128, 8], F32) as pp2,
            nc.psum_tensor("pp3", [128, 512], F32) as pp3,
            nc.psum_tensor("pp4", [128, 512], F32) as pp4,
            nc.psum_tensor("pp5", [128, 8], F32) as pp5,
        ):
            pps = [[pp0, pp1, pp2], [pp3, pp4, pp5]]
            ve_apply = ve.n
            pe_mark = {}
            ev_done = {}

            for ot in range(NOT):
                slot = ot % 2

                @block.gpsimd
                def _(g, ot=ot, slot=slot):
                    if ot >= 2:
                        g.wait_ge(pe.sem, pe_mark[ot - 2])
                    _inc(wbs[slot], g.dma_start(out=wbuf[slot][:, :],
                                                in_=w_in[ot].transpose([1, 0, 2])), 16)

                @block.tensor
                def _(tn, ot=ot, slot=slot, need=16 * (ot // 2 + 1), ven=ve_apply):
                    tn.wait_ge(wbs[slot].sem, need)
                    tn.wait_ge(ve.sem, ven)
                    if ot >= 2:
                        tn.wait_ge(ac.sem, ev_done[ot - 2])
                    for ci, (c0, cw) in enumerate(CH):
                        for g in range(16):
                            ins = tn.matmul(pps[slot][ci][:, 0:cw], wbuf[slot][:, g * 128:(g + 1) * 128],
                                            xln[:, g * TE + c0:g * TE + c0 + cw],
                                            start=(g == 0), stop=(g == 15))
                            if ci == 2 and g == 15:
                                _inc(pe, ins, 1)

                pe_mark[ot] = pe.n

                @block.scalar
                def _(sc, ot=ot, slot=slot, pen=pe.n):
                    sc.wait_ge(pe.sem, pen)
                    if ot < 32:
                        if ot >= 2:
                            sc.wait_ge(zss[ot % 2].sem, 16 * (ot // 2))
                        for ci, (c0, cw) in enumerate(CH):
                            ins = sc.copy(zst[ot % 2][:, c0:c0 + cw], pps[slot][ci][:, 0:cw])
                            if ci == 2:
                                _inc(ac, ins, 1)
                    elif ot < 66:
                        xt = xbc[:, (ot - 32) * TE:(ot - 31) * TE]
                        for ci, (c0, cw) in enumerate(CH):
                            ins = sc.copy(xt[:, c0:c0 + cw], pps[slot][ci][:, 0:cw])
                            if ci == 2:
                                _inc(ac, ins, 1)
                    else:
                        for ci, (c0, cw) in enumerate(CH):
                            ins = sc.copy(dts[0:64, c0:c0 + cw], pps[slot][ci][0:64, 0:cw])
                            if ci == 2:
                                _inc(ac, ins, 1)

                ev_done[ot] = ac.n

                if ot < 32:
                    @block.sync
                    def _(s, ot=ot, acn=ac.n):
                        s.wait_ge(ac.sem, acn)
                        _inc(zss[ot % 2], s.dma_start(out=z_o[128 * ot:128 * (ot + 1), :],
                                                      in_=zst[ot % 2][:, :]), 16)

        in_proj_ac = ac.n
        for ft in range(34):
            @block.vector
            def _(v, ft=ft, acn=ac.n):
                v.wait_ge(ac.sem, acn)
                xt = xbc[:, ft * TE:(ft + 1) * TE]
                w0 = convw[:, 4 * ft + 0:4 * ft + 1]
                v.tensor_scalar(cacc[:, :], xt[:, 5:5 + T], w0, None, ALU.mult)
                for k in (1, 2, 3):
                    wk = convw[:, 4 * ft + k:4 * ft + k + 1]
                    ins = v.scalar_tensor_tensor(cacc[:, :], xt[:, 5 + k:5 + k + T], wk, cacc[:, :],
                                                 ALU.mult, ALU.add)
                    if k == 3:
                        _inc(ve, ins, 1)

            @block.scalar
            def _(sc, ft=ft, ven=ve.n):
                sc.wait_ge(ve.sem, ven)
                if 1 <= ft <= 32:
                    sc.wait_ge(cvs.sem, 16 * ft)
                dd = cvst[:, :] if ft < 32 else (bsb[:, :] if ft == 32 else csb[:, :])
                _inc(ac, sc.activation(dd, cacc[:, :], AF.Silu, bias=convb[:, ft:ft + 1]), 1)

            @block.sync
            def _(s, ft=ft, acn=ac.n):
                s.wait_ge(ac.sem, acn)
                if ft < 32:
                    _inc(cvs, s.dma_start(out=x_o[128 * ft:128 * (ft + 1), :], in_=cvst[:, :]), 16)
                elif ft == 32:
                    _inc(mst, s.dma_start(out=b_o[:, :], in_=bsb[:, :]), 16)
                else:
                    _inc(mst, s.dma_start(out=c_o[:, :], in_=csb[:, :]), 16)

        @block.scalar
        def _(sc, acn=ac.n, cstn=cst.n):
            sc.wait_ge(ac.sem, acn)
            sc.wait_ge(cst.sem, cstn)
            dtb = dts[:, 2 * TE:2 * TE + 1]
            raw = dts[0:64, HALO:HALO + T]
            B = dts[0:64, TE:TE + T]
            _inc(ac, sc.activation(B, raw, AF.Abs, bias=dtb), 1)
            _inc(ac, sc.activation(dtev[:, :], B, AF.Exp, scale=-1.0), 1)
            _inc(ac, sc.activation(B, dtev[:, :], AF.Ln, bias=onecol[:, 0:1]), 1)
            _inc(ac, sc.activation(dtev[:, :], raw, AF.Relu, bias=dtb), 1)
            if a_from_log:
                _inc(ac, sc.activation(acol[:, :], dts[0:64, 2 * TE + 1:2 * TE + 2], AF.Exp), 1)

        @block.vector
        def _(v, acn=ac.n):
            v.wait_ge(ac.sem, acn)
            v.tensor_add(dtv[:, :], dts[0:64, TE:TE + T], dtev[:, :])
            sgn = -1.0 if a_from_log else 1.0
            a_src = acol[:, 0:1] if a_from_log else dts[0:64, 2 * TE + 1:2 * TE + 2]
            v.tensor_scalar(dts[0:64, 0:T], dtv[:, :], a_src, sgn, ALU.mult, ALU.mult)
            for c in range(NCH):
                last = (c == NCH - 1)
                ins = v.tensor_tensor_scan(csv[:, c * 256:(c + 1) * 256], dts[0:64, c * 256:(c + 1) * 256],
                                           zeros64[:, :], 0.0, ALU.add, ALU.add)
                if last:
                    _inc(ve, ins, 1)

        @block.scalar
        def _(sc, ven=ve.n):
            sc.wait_ge(ve.sem, ven)
            for c in range(NCH):
                last = csv[:, c * 256 + 255:c * 256 + 256]
                _inc(ac, sc.activation(decv[:, c:c + 1], last, AF.Exp), 1)
                _inc(ac, sc.activation(dtev[:, c * 256:(c + 1) * 256], csv[:, c * 256:(c + 1) * 256],
                                       AF.Exp, scale=-1.0, bias=last), 1)

        @block.vector
        def _(v, acn=ac.n):
            v.wait_ge(ac.sem, acn)
            v.memset(decv[:, 4:5], 1.0)
            for c in range(1, NCH):
                v.tensor_tensor(decv[:, 4 + c:5 + c], decv[:, 3 + c:4 + c], decv[:, c - 1:c], ALU.mult)
            v.tensor_tensor(decv[:, 8:9], decv[:, 7:8], decv[:, 3:4], ALU.mult)
            sgn = -1.0 if a_from_log else 1.0
            a_src = acol[:, :] if a_from_log else dts[0:64, 2 * TE + 1:2 * TE + 2]
            _inc(ve, v.tensor_scalar(decv[:, 9:10], a_src, sgn, None, ALU.mult), 1)

        @block.sync
        def _(s, ven=ve.n):
            s.wait_ge(ve.sem, ven)
            _inc(mst, s.dma_start(out=dt_o[:, :], in_=dtv[:, :]), 16)
            _inc(mst, s.dma_start(out=cs_o[:, :], in_=csv[:, :]), 16)
            _inc(mst, s.dma_start(out=dte_o[:, :], in_=dtev[:, :]), 16)
            _inc(mst, s.dma_start(out=dec_o[:, :], in_=decv[:, :]), 16)

        with nc.psum_tensor("cbp", [128, 256], F32) as cbp:
            first = True
            for c in range(NCH):
                for hs in range(2):
                    @block.tensor
                    def _(tn, c=c, hs=hs, first=first, acn=in_proj_ac, ven=ve.n):
                        tn.wait_ge(ac.sem, ac.n)
                        if not first:
                            tn.wait_ge(ve.sem, ven)
                        _inc(pe, tn.matmul(cbp[:, :], bsb[:, c * 256 + hs * 128:c * 256 + (hs + 1) * 128],
                                           csb[:, c * 256:(c + 1) * 256], start=True, stop=True), 1)

                    @block.vector
                    def _(v, c=c, hs=hs, first=first, pen=pe.n, cbn=cbs.n):
                        v.wait_ge(pe.sem, pen)
                        if not first:
                            v.wait_ge(cbs.sem, cbn)
                        _inc(ve, v.tensor_tensor(cbst[:, :], cbp[:, :],
                                                 masksb[:, hs * 256:(hs + 1) * 256], ALU.mult), 1)

                    @block.sync
                    def _(s, c=c, hs=hs, ven=ve.n):
                        s.wait_ge(ve.sem, ven)
                        _inc(cbs, s.dma_start(out=cbm_o[c, hs], in_=cbst[:, :]), 16)

                    first = False

        @block.sync
        def _(s):
            s.wait_ge(mst.sem, mst.n)
            s.wait_ge(cvs.sem, cvs.n)
            s.wait_ge(cbs.sem, cbs.n)
            s.wait_ge(zss[0].sem, zss[0].n)
            s.wait_ge(zss[1].sem, zss[1].n)
    return nc



def build_p2():
    nc = bass.Bass("TRN2", target_bir_lowering=False)
    xT_i = nc.dram_tensor("xT", [T, DI], BF16, kind="ExternalInput").ap()
    bT_i = nc.dram_tensor("bT", [T, ND], BF16, kind="ExternalInput").ap()
    cfm_i = nc.dram_tensor("cfm", [ND, T], BF16, kind="ExternalInput").ap()
    xfm_i = nc.dram_tensor("xfm", [DI, T], BF16, kind="ExternalInput").ap()
    dtT_i = nc.dram_tensor("dtT", [T, 64], F32, kind="ExternalInput").ap()
    dteT_i = nc.dram_tensor("dteT", [T, 64], F32, kind="ExternalInput").ap()
    csT_i = nc.dram_tensor("csT", [T, 64], F32, kind="ExternalInput").ap()
    csf_i = nc.dram_tensor("csf", [NCH, 16384], F32, kind="ExternalInput").ap()
    cbm_i = nc.dram_tensor("cbm", [NCH, 2, 128, 256], BF16, kind="ExternalInput").ap()
    dpar_i = nc.dram_tensor("dpar", [1, 64], F32, kind="ExternalInput").ap()
    cdrow_i = nc.dram_tensor("cdrow", [1, NCH * 64], F32, kind="ExternalInput").ap()
    ones_i = nc.dram_tensor("onesin", [1, 128], F32, kind="ExternalInput").ap()

    y_o = nc.dram_tensor("y_o", [DI, T], F32, kind="ExternalOutput").ap()
    sloc_o = nc.dram_tensor("sloc_o", [H, ND, PD], F32, kind="ExternalOutput").ap()
    sprev_o = nc.dram_tensor("sprev_o", [NCH - 1, H, ND, PD], F32, kind="ExternalOutput").ap()
    cexp_o = nc.dram_tensor("cexp_o", [NCH, H, ND, CHUNK], BF16, kind="ExternalOutput").ap()

    with contextlib.ExitStack() as stk:
        def sb(name, shape, dt):
            return stk.enter_context(nc.sbuf_tensor(name, shape, dt))

        def sem(name):
            return Ctr(stk.enter_context(nc.semaphore(name)))

        def psm(name, shape):
            return stk.enter_context(nc.psum_tensor(name, shape, F32))

        ones_c = sb("ones_c", [1, 128], BF16)
        ones_f = sb("ones_f", [1, 128], F32)
        cbm_s = sb("cbm_s", [128, 8 * 256], BF16)
        dbc = sb("dbc", [128, 64], F32)
        cdbc = sb("cdbc", [128, NCH * 64], F32)
        xts = [sb("xts0", [128, 2 * DI], BF16), sb("xts1", [128, 2 * DI], BF16)]
        bts = [sb("bts0", [128, 2 * ND], BF16), sb("bts1", [128, 2 * ND], BF16)]
        sml = [sb("sml0", [128, 2 * 192], F32), sb("sml1", [128, 2 * 192], F32)]
        csf = sb("csf_s", [1, 16384], F32)
        cfm = [sb("cfm0", [128, 256], BF16), sb("cfm1", [128, 256], BF16)]
        xfm = [sb("xfm0", [128, 32 * 256], BF16), sb("xfm1", [128, 32 * 256], BF16)]
        csn = [sb("csn0", [128, 2 * 64], F32), sb("csn1", [128, 2 * 64], F32)]
        xdt = [sb("xdtA", [128, 2 * 64], BF16), sb("xdtB", [128, 2 * 64], BF16)]
        bdec = [sb("bdecA", [128, 2 * ND], BF16), sb("bdecB", [128, 2 * ND], BF16)]
        eb = [[sb("eb00", [128, 256], BF16), sb("eb01", [128, 256], BF16)],
              [sb("eb10", [128, 256], BF16), sb("eb11", [128, 256], BF16)]]
        ebx = [[sb("ebx00", [128, 256], BF16), sb("ebx01", [128, 256], BF16)],
               [sb("ebx10", [128, 256], BF16), sb("ebx11", [128, 256], BF16)]]
        pb = [[sb("pb00", [128, 256], BF16), sb("pb01", [128, 256], BF16)],
              [sb("pb10", [128, 256], BF16), sb("pb11", [128, 256], BF16)]]
        expb = [sb("expbA", [128, 256], F32), sb("expbB", [128, 256], F32)]
        cexb = [sb("cexb0", [128, 256], BF16), sb("cexb1", [128, 256], BF16)]
        yst = [sb("yst0", [128, 256], F32), sb("yst1", [128, 256], F32)]
        saft = [sb("saft0", [128, H * PD], F32), sb("saft1", [128, H * PD], F32)]

        cst = sem("cst")
        ldc = [sem("ldc0"), sem("ldc1")]
        ys = [sem("ys0"), sem("ys1")]
        cxs = [sem("cxs0"), sem("cxs1")]
        sps = [sem("sps0"), sem("sps1")]
        mst = sem("mst")
        pe = sem("pe")
        ve = sem("ve")
        ac = sem("ac")

        ps_cs = [psm("ps_cs0", [128, 256]), psm("ps_cs1", [128, 256])]
        ps_y = [psm("ps_y0", [64, 256]), psm("ps_y1", [64, 256])]
        ps_s = [psm("ps_s0", [128, 64]), psm("ps_s1", [128, 64])]
        ps_bc = psm("ps_bc", [128, 256])
        block = stk.enter_context(nc.Block())

        @block.gpsimd
        def _(g):
            _inc(cst, g.dma_start(out=ones_c[:, :], in_=ones_i[:, :]), 16)
            _inc(cst, g.dma_start(out=ones_f[:, :], in_=ones_i[:, :]), 16)
            for c in range(NCH):
                for hs in range(2):
                    _inc(cst, g.dma_start(out=cbm_s[:, (c * 2 + hs) * 256:(c * 2 + hs + 1) * 256],
                                          in_=cbm_i[c, hs]), 16)

        @block.sync
        def _(s):
            _inc(cst, s.dma_start(out=dbc[0:1, 0:64], in_=dpar_i[:, :]), 16)
            _inc(cst, s.dma_start(out=cdbc[0:1, :], in_=cdrow_i[:, :]), 16)

        @block.tensor
        def _(tn, cstn=cst.n):
            tn.wait_ge(cst.sem, cstn)
            pass

        rowbf = sb("rowbf", [1, NCH * 64], BF16)

        @block.vector
        def _(v, cstn=cst.n):
            v.wait_ge(cst.sem, cstn)
            _inc(ve, v.tensor_copy(rowbf[0:1, 0:64], dbc[0:1, 0:64]), 1)

        @block.tensor
        def _(tn, ven=ve.n):
            tn.wait_ge(ve.sem, ven)
            _inc(pe, tn.matmul(ps_bc[:, 0:64], ones_c[0:1, :], rowbf[0:1, 0:64],
                               start=True, stop=True), 1)

        @block.scalar
        def _(sc, pen=pe.n):
            sc.wait_ge(pe.sem, pen)
            _inc(ac, sc.copy(dbc[:, :], ps_bc[:, 0:64]), 1)

        @block.vector
        def _(v, acn=ac.n):
            v.wait_ge(ac.sem, acn)
            _inc(ve, v.tensor_copy(rowbf[0:1, :], cdbc[0:1, :]), 1)

        @block.tensor
        def _(tn, ven=ve.n):
            tn.wait_ge(ve.sem, ven)
            _inc(pe, tn.matmul(ps_bc[:, 0:256], ones_c[0:1, :], rowbf[0:1, :],
                               start=True, stop=True), 1)

        @block.scalar
        def _(sc, pen=pe.n):
            sc.wait_ge(pe.sem, pen)
            _inc(ac, sc.copy(cdbc[:, :], ps_bc[:, 0:256]), 1)

        @block.vector
        def _(v, acn=ac.n):
            v.wait_ge(ac.sem, acn)
            v.memset(saft[1][:, :], 0.0)
            _inc(ve, v.nop(hint="init_done"), 1)
        ve_init = ve.n
        ac_init = ac.n

        ys_cnt = {0: 0, 1: 0}
        M_xdt, M_csbc, M_expb, M_seg, M_cex, M_px, M_pb, M_mm, M_scan = ({} for _ in range(9))

        def sp_need_fn(c, h):
            if c - 1 >= 1:
                prior_chunks = len([cc for cc in range(1, c - 1) if (cc + 1) % 2 == c % 2])
                return 16 * (prior_chunks * 64 + h + 1)
            return None

        ld_need = {}
        for c in range(NCH):
            slot = c % 2

            @block.sync
            def _(s, c=c, slot=slot, ven=ve.n, acn=ac.n, pen=pe.n):
                s.wait_ge(ve.sem, ven)
                s.wait_ge(ac.sem, acn)
                s.wait_ge(pe.sem, pen)
                n0 = 256 * c
                _inc(ldc[slot], s.dma_start(out=xts[slot][:, 0:DI], in_=xT_i[n0:n0 + 128, :]), 16)
                _inc(ldc[slot], s.dma_start(out=xts[slot][:, DI:2 * DI], in_=xT_i[n0 + 128:n0 + 256, :]), 16)
                _inc(ldc[slot], s.dma_start(out=bts[slot][:, 0:ND], in_=bT_i[n0:n0 + 128, :]), 16)
                _inc(ldc[slot], s.dma_start(out=bts[slot][:, ND:2 * ND], in_=bT_i[n0 + 128:n0 + 256, :]), 16)
                for half in range(2):
                    hb = n0 + 128 * half
                    _inc(ldc[slot], s.dma_start(out=sml[slot][:, half * 192:half * 192 + 64],
                                                in_=dtT_i[hb:hb + 128, :]), 16)
                    _inc(ldc[slot], s.dma_start(out=sml[slot][:, half * 192 + 64:half * 192 + 128],
                                                in_=dteT_i[hb:hb + 128, :]), 16)
                    _inc(ldc[slot], s.dma_start(out=sml[slot][:, half * 192 + 128:half * 192 + 192],
                                                in_=csT_i[hb:hb + 128, :]), 16)
                _inc(ldc[slot], s.dma_start(out=csf[:, :], in_=csf_i[c:c + 1, :]), 16)
                _inc(ldc[slot], s.dma_start(out=cfm[slot][:, :], in_=cfm_i[:, n0:n0 + 256]), 16)
                src = xfm_i[:, n0:n0 + 256]
                src3 = bass.AP(src.tensor, src.offset, [[T, 128], [128 * T, 32], [1, 256]])
                _inc(ldc[slot], s.dma_start(out=xfm[slot][:, :], in_=src3), 16)

            ld_need[c] = 16 * 13 * (c // 2 + 1)

            @block.vector
            def _(v, c=c, slot=slot, need=ld_need[c]):
                v.wait_ge(ldc[slot].sem, need)
                v.tensor_scalar(csn[slot][:, 0:64], sml[slot][:, 128:192], -1.0, None, ALU.mult)
                _inc(ve, v.tensor_scalar(csn[slot][:, 64:128], sml[slot][:, 192 + 128:192 + 192],
                                         -1.0, None, ALU.mult), 1)

            for h in range(H):
                hpar = h % 2
                g2 = h // 2
                base = 64 * hpar
                slot = c % 2
                hs2 = h % 2
                gh = c * H + h

                @block.vector
                def _(v, c=c, slot=slot, h=h, hs2=hs2,
                      w_pe=M_mm.get(gh - 2), ldn=ld_need[c]):
                    v.wait_ge(ldc[c % 2].sem, ldn)
                    if w_pe is not None:
                        v.wait_ge(pe.sem, w_pe)
                    for half in range(2):
                        v.tensor_scalar(xdt[hs2][:, half * 64:(half + 1) * 64],
                                        xts[slot][:, half * DI + 64 * h:half * DI + 64 * (h + 1)],
                                        sml[slot][:, half * 192 + h:half * 192 + h + 1],
                                        None, ALU.mult)
                        ins = v.tensor_scalar(bdec[hs2][:, half * ND:(half + 1) * ND],
                                              bts[slot][:, half * ND:(half + 1) * ND],
                                              sml[slot][:, half * 192 + 64 + h:half * 192 + 64 + h + 1],
                                              None, ALU.mult)
                        if half == 1:
                            _inc(ve, ins, 1)

                M_xdt[gh] = ve.n

                @block.tensor
                def _(tn, c=c, slot=slot, h=h, hs2=hs2, ldn=ld_need[c],
                      w_ve=M_seg.get(gh - 1), w_ac=M_expb.get(gh - 1)):
                    tn.wait_ge(ldc[c % 2].sem, ldn)
                    if w_ve is not None:
                        tn.wait_ge(ve.sem, w_ve)
                    if w_ac is not None:
                        tn.wait_ge(ac.sem, w_ac)
                    _inc(pe, tn.matmul(ps_cs[hs2][:, :], ones_f[0:1, :],
                                       csf[0:1, 256 * h:256 * (h + 1)],
                                       start=True, stop=True), 1)

                M_csbc[gh] = pe.n

                @block.scalar
                def _(sc, hs2=hs2, w_pe=M_csbc[gh], w_ve=M_cex.get(gh - 1)):
                    sc.wait_ge(pe.sem, w_pe)
                    if w_ve is not None:
                        sc.wait_ge(ve.sem, w_ve)
                    _inc(ac, sc.activation(expb[hs2][:, :], ps_cs[hs2][:, :], AF.Exp), 1)

                M_expb[gh] = ac.n

                @block.vector
                def _(v, c=c, slot=slot, h=h, hs2=hs2, w_pe=M_csbc[gh], w_ac=M_expb[gh],
                      w_acp=M_px.get(gh - 1), cxn=cxs[hs2].n):
                    v.wait_ge(pe.sem, w_pe)
                    v.wait_ge(ac.sem, w_ac)
                    if w_acp is not None:
                        v.wait_ge(ac.sem, w_acp)
                    v.wait_ge(cxs[hs2].sem, cxn)
                    for half in range(2):
                        v.tensor_scalar(eb[hs2][half][:, :], ps_cs[hs2][:, :],
                                        csn[slot][:, half * 64 + h:half * 64 + h + 1],
                                        0.0, ALU.add, ALU.min)
                    ins = v.tensor_tensor(cexb[hs2][:, :], cfm[slot][:, :], expb[hs2][:, :], ALU.mult)
                    _inc(ve, ins, 1)

                M_seg[gh] = ve.n
                M_cex[gh] = ve.n

                @block.scalar
                def _(sc, hs2=hs2, w_ve=M_seg[gh], w_vp=M_pb.get(gh - 1)):
                    sc.wait_ge(ve.sem, w_ve)
                    if w_vp is not None:
                        sc.wait_ge(ve.sem, w_vp)
                    _inc(ac, sc.activation(ebx[hs2][0][:, :], eb[hs2][0][:, :], AF.Exp), 1)
                    _inc(ac, sc.activation(ebx[hs2][1][:, :], eb[hs2][1][:, :], AF.Exp), 1)

                M_px[gh] = ac.n

                @block.vector
                def _(v, c=c, hs2=hs2, w_ac=M_px[gh], w_pe=M_mm.get(gh - 1)):
                    v.wait_ge(ac.sem, w_ac)
                    if w_pe is not None:
                        v.wait_ge(pe.sem, w_pe)
                    v.tensor_tensor(pb[hs2][0][:, :], ebx[hs2][0][:, :],
                                    cbm_s[:, (c * 2 + 0) * 256:(c * 2 + 1) * 256], ALU.mult)
                    _inc(ve, v.tensor_tensor(pb[hs2][1][:, :], ebx[hs2][1][:, :],
                                             cbm_s[:, (c * 2 + 1) * 256:(c * 2 + 2) * 256], ALU.mult), 1)

                M_pb[gh] = ve.n

                @block.tensor
                def _(tn, hs2=hs2, w_ve=M_pb[gh], w_vx=M_xdt[gh], w_vs=M_scan.get(gh - 2)):
                    tn.wait_ge(ve.sem, max(w_ve, w_vx))
                    if w_vs is not None:
                        tn.wait_ge(ve.sem, w_vs)
                    for half in range(2):
                        tn.matmul(ps_y[hs2][:, :], xdt[hs2][:, half * 64:(half + 1) * 64],
                                  pb[hs2][half][:, :], start=(half == 0), stop=(half == 1))
                    for half in range(2):
                        ins = tn.matmul(ps_s[hs2][:, :], bdec[hs2][:, half * ND:(half + 1) * ND],
                                        xdt[hs2][:, half * 64:(half + 1) * 64],
                                        start=(half == 0), stop=(half == 1))
                        if half == 1:
                            _inc(pe, ins, 1)

                M_mm[gh] = pe.n

                @block.vector
                def _(v, c=c, slot=slot, h=h, hpar=hpar, g2=g2, base=base, hs2=hs2,
                      w_pe=M_mm[gh], ys_need=16 * ys_cnt[g2 % 2], sp_need=sp_need_fn(c, h)):
                    v.wait_ge(pe.sem, w_pe)
                    if hpar == 0:
                        v.wait_ge(ys[g2 % 2].sem, ys_need)
                    dst = yst[g2 % 2][base:base + 64, :]
                    v.tensor_scalar(dst, xfm[slot][base:base + 64, g2 * 256:(g2 + 1) * 256],
                                    dbc[base:base + 64, h:h + 1], None, ALU.mult)
                    _inc(ve, v.tensor_tensor(dst, dst, ps_y[hs2][:, :], ALU.add), 1)
                    if sp_need is not None:
                        v.wait_ge(sps[c % 2].sem, sp_need)
                    cdcol = cdbc[:, 64 * c + h:64 * c + h + 1]
                    _inc(ve, v.scalar_tensor_tensor(saft[c % 2][:, 64 * h:64 * (h + 1)],
                                                    saft[(c + 1) % 2][:, 64 * h:64 * (h + 1)],
                                                    cdcol, ps_s[hs2][:, :], ALU.mult, ALU.add), 1)

                M_scan[gh] = ve.n

                @block.gpsimd
                def _(g, c=c, h=h, hpar=hpar, g2=g2, hs2=hs2, w_ve=M_scan[gh]):
                    g.wait_ge(ve.sem, w_ve)
                    _inc(cxs[hs2], g.dma_start(out=cexp_o[c, h], in_=cexb[hs2][:, :]), 16)
                    if hpar == 1:
                        _inc(ys[g2 % 2], g.dma_start(out=y_o[128 * g2:128 * (g2 + 1), 256 * c:256 * (c + 1)],
                                                     in_=yst[g2 % 2][:, :]), 16)
                    if c >= 1:
                        _inc(sps[(c + 1) % 2], g.dma_start(out=sprev_o[c - 1, h],
                                                           in_=saft[(c + 1) % 2][:, 64 * h:64 * (h + 1)]), 16)
                    if c == NCH - 1:
                        _inc(mst, g.dma_start(out=sloc_o[h], in_=saft[c % 2][:, 64 * h:64 * (h + 1)]), 16)

                if hpar == 1:
                    ys_cnt[g2 % 2] += 1

        @block.sync
        def _(s):
            s.wait_ge(mst.sem, mst.n)
            s.wait_ge(cxs[0].sem, cxs[0].n)
            s.wait_ge(cxs[1].sem, cxs[1].n)
            s.wait_ge(ys[0].sem, ys[0].n)
            s.wait_ge(ys[1].sem, ys[1].n)
            s.wait_ge(sps[0].sem, sps[0].n)
            s.wait_ge(sps[1].sem, sps[1].n)
    return nc


def host_p2_inputs(r1, masks_dummy=None):
    xT = np.ascontiguousarray(r1["x_o"].T)
    bT = np.ascontiguousarray(r1["b_o"].T)
    dtT = np.ascontiguousarray(r1["dt_o"].T)
    dteT = np.ascontiguousarray(r1["dte_o"].T)
    csT = np.ascontiguousarray(r1["cs_o"].T)
    csf = np.ascontiguousarray(r1["cs_o"].reshape(64, NCH, 256).transpose(1, 0, 2).reshape(NCH, 16384))
    cdrow = np.ascontiguousarray(r1["dec_o"][:, 0:NCH].T.reshape(1, NCH * 64))
    return dict(
        xT=xT, bT=bT, cfm=r1["c_o"], xfm=r1["x_o"], dtT=dtT, dteT=dteT, csT=csT,
        csf=csf, cbm=r1["cbm_o"], dpar=None, cdrow=cdrow,
        onesin=np.ones((1, 128), np.float32),
    )



def build_p3():
    nc = bass.Bass("TRN2", target_bir_lowering=False)
    y_i = nc.dram_tensor("y_i", [DI, T], F32, kind="ExternalInput").ap()
    sg_i = nc.dram_tensor("sg_i", [4, H, ND, PD], F32, kind="ExternalInput").ap()
    sprev_i = nc.dram_tensor("sprev_i", [NCH, H, ND, PD], F32, kind="ExternalInput").ap()
    cexp_i = nc.dram_tensor("cexp_i", [NCH, H, ND, CHUNK], BF16, kind="ExternalInput").ap()
    z_i = nc.dram_tensor("z_i", [DI, T], BF16, kind="ExternalInput").ap()
    dgrow_i = nc.dram_tensor("dgrow", [1, 4 * 64], F32, kind="ExternalInput").ap()
    cdcrow_i = nc.dram_tensor("cdcrow", [1, NCH * 64], F32, kind="ExternalInput").ap()
    normw_i = nc.dram_tensor("normw", [128, 32], F32, kind="ExternalInput").ap()
    wout_i = nc.dram_tensor("wout", [16, 32, 128, 128], F32, kind="ExternalInput").ap()
    resid_i = nc.dram_tensor("resid", [DM, T], F32, kind="ExternalInput").ap()
    ones_i = nc.dram_tensor("onesin", [1, 128], F32, kind="ExternalInput").ap()
    ones2_i = nc.dram_tensor("ones2", [128, 1], F32, kind="ExternalInput").ap()

    resid_o = nc.dram_tensor("resid_o", [DM, T], F32, kind="ExternalOutput").ap()
    xln3_o = nc.dram_tensor("xln3_o", [DM, T], BF16, kind="ExternalOutput").ap()

    with contextlib.ExitStack() as stk:
        def sb(name, shape, dt):
            return stk.enter_context(nc.sbuf_tensor(name, shape, dt))

        def sem(name):
            return Ctr(stk.enter_context(nc.semaphore(name)))

        def psm(name, shape):
            return stk.enter_context(nc.psum_tensor(name, shape, F32))

        ones_c = sb("ones_c", [1, 128], BF16)
        ones2b = sb("ones2b", [128, 1], BF16)
        rowbf = sb("rowbf", [1, 256], BF16)
        dgbc = sb("dgbc", [128, 256], F32)
        cdcbc = sb("cdcbc", [128, 256], F32)
        normw = sb("normw_s", [128, 32], F32)
        epsc = sb("epsc", [1, 1], F32)
        sgb = [sb("sgb0", [128, 64], F32), sb("sgb1", [128, 64], F32)]
        spb = [sb("spb0", [128, 64], F32), sb("spb1", [128, 64], F32)]
        cxb = [sb("cxb0", [128, 256], BF16), sb("cxb1", [128, 256], BF16)]
        spt = sb("spt", [128, 64], BF16)
        sinb = sb("sinb", [128, H * 64], F32)
        acc = [sb("acc0", [128, 64], F32), sb("acc1", [128, 64], F32)]
        yst = [sb("yst0", [128, 256], F32), sb("yst1", [128, 256], F32)]
        zt = [sb("zt0", [128, 256], BF16), sb("zt1", [128, 256], BF16)]
        zsil = sb("zsil", [128, 256], BF16)
        ygbuf = sb("ygbuf", [128, 32 * T], BF16)
        ynbuf = ygbuf
        mbuf = sb("mbuf", [128, 16 * T], BF16)
        sqst = [sb("sqst0", [128, T], BF16), sb("sqst1", [128, T], BF16)]
        wbuf = [sb("wbuf0", [128, 4096], BF16), sb("wbuf1", [128, 4096], BF16)]
        srow = sb("srow", [2, T], F32)
        sqrow = sb("sqrow", [1, T], F32)
        scr0 = sb("scr0", [1, T], F32)
        scr1 = sb("scr1", [1, T], F32)
        murow = sb("murow", [1, T], F32)
        rsrow = sb("rsrow", [1, T], F32)
        murb = sb("murb", [1, T], BF16)
        rsrb = sb("rsrb", [1, T], BF16)
        mubc = sb("mubc", [128, T], F32)
        rsbc = sb("rsbc", [128, T], F32)
        rst = [sb("rst0", [128, T], F32), sb("rst1", [128, T], F32)]
        rot0 = sb("rot0", [128, T], F32)
        rot = [rot0, rot0]
        xlt = [sb("xlt0", [128, T], BF16), sb("xlt1", [128, T], BF16)]

        cst = sem("cst")
        sgs = [sem("sgs0"), sem("sgs1")]
        sls = [sem("sls0"), sem("sls1")]
        cxs = [sem("cxs0"), sem("cxs1")]
        yls = [sem("yls0"), sem("yls1")]
        zls = [sem("zls0"), sem("zls1")]
        wls = [sem("wls0"), sem("wls1")]
        rls = [sem("rls0"), sem("rls1")]
        ros = [sem("ros0"), sem("ros1")]
        xos = [sem("xos0"), sem("xos1")]
        mst = sem("mst")
        pe = sem("pe")
        ve = sem("ve")
        ac = sem("ac")

        ps_bc = psm("ps_bc", [128, 512])
        ps_y = psm("ps_y", [64, 256])
        ps_s0 = psm("ps_s0", [1, 512])
        ps_s1 = psm("ps_s1", [1, 512])
        ps_q0 = psm("ps_q0", [1, 512])
        ps_q1 = psm("ps_q1", [1, 512])
        ps_m0 = psm("ps_m0", [128, 512])
        ps_m1 = psm("ps_m1", [128, 512])
        block = stk.enter_context(nc.Block())

        @block.gpsimd
        def _(g):
            _inc(cst, g.dma_start(out=ones_c[:, :], in_=ones_i[:, :]), 16)
            _inc(cst, g.dma_start(out=ones2b[:, :], in_=ones2_i[:, :]), 16)

        @block.sync
        def _(s):
            _inc(cst, s.dma_start(out=normw[:, :], in_=normw_i[:, :]), 16)
            _inc(cst, s.dma_start(out=dgbc[0:1, :], in_=dgrow_i[:, :]), 16)
            _inc(cst, s.dma_start(out=cdcbc[0:1, :], in_=cdcrow_i[:, :]), 16)

        for rowsrc, dst in [(dgbc, dgbc), (cdcbc, cdcbc)]:
            @block.vector
            def _(v, rowsrc=rowsrc, cstn=cst.n, acn=ac.n):
                v.wait_ge(cst.sem, cstn)
                v.wait_ge(ac.sem, acn)
                _inc(ve, v.tensor_copy(rowbf[0:1, :], rowsrc[0:1, :]), 1)

            @block.tensor
            def _(tn, ven=ve.n):
                tn.wait_ge(ve.sem, ven)
                _inc(pe, tn.matmul(ps_bc[:, 0:256], ones_c[0:1, :], rowbf[0:1, :],
                                   start=True, stop=True), 1)

            @block.scalar
            def _(sc, dst=dst, pen=pe.n):
                sc.wait_ge(pe.sem, pen)
                _inc(ac, sc.copy(dst[:, :], ps_bc[:, 0:256]), 1)

        @block.vector
        def _(v, cstn=cst.n):
            v.wait_ge(cst.sem, cstn)
            _inc(ve, v.memset(epsc[:, :], EPS), 1)

        for h in range(H):
            for j in range(4):
                slot = j % 2

                @block.sync
                def _(s, h=h, j=j, slot=slot, ven=ve.n):
                    s.wait_ge(ve.sem, ven)
                    _inc(sgs[slot], s.dma_start(out=sgb[slot][:, :], in_=sg_i[j, h]), 16)

                @block.vector
                def _(v, h=h, j=j, slot=slot, need=16 * (h * 2 + j // 2 + 1), acn=ac.n):
                    v.wait_ge(sgs[slot].sem, need)
                    v.wait_ge(ac.sem, acn)
                    a_in, a_out = acc[(j + 1) % 2], acc[j % 2]
                    if j == 0:
                        ins = v.tensor_copy(a_out[:, :], sgb[slot][:, :])
                    elif j < 3:
                        ins = v.scalar_tensor_tensor(a_out[:, :], a_in[:, :],
                                                     dgbc[:, 64 * j + h:64 * j + h + 1],
                                                     sgb[slot][:, :], ALU.mult, ALU.add)
                    else:
                        ins = v.scalar_tensor_tensor(sinb[:, 64 * h:64 * (h + 1)], a_in[:, :],
                                                     dgbc[:, 64 * j + h:64 * j + h + 1],
                                                     sgb[slot][:, :], ALU.mult, ALU.add)
                    _inc(ve, ins, 1)

        yg_ac = {}
        for c in range(NCH):
            for h in range(H):
                hpar = h % 2
                g2 = h // 2
                sl = h % 2
                ysl = g2 % 2

                @block.sync
                def _(s, c=c, h=h, sl=sl, ysl=ysl, hpar=hpar, g2=g2, ven=ve.n, pen=pe.n, acn=ac.n):
                    s.wait_ge(ve.sem, ven)
                    s.wait_ge(ac.sem, acn)
                    _inc(sls[sl], s.dma_start(out=spb[sl][:, :], in_=sprev_i[c, h]), 16)
                    s.wait_ge(pe.sem, pen)
                    _inc(cxs[sl], s.dma_start(out=cxb[sl][:, :], in_=cexp_i[c, h]), 16)
                    if hpar == 0:
                        _inc(yls[ysl], s.dma_start(out=yst[ysl][:, :],
                                                   in_=y_i[128 * g2:128 * (g2 + 1), 256 * c:256 * (c + 1)]), 16)
                        _inc(zls[ysl], s.dma_start(out=zt[ysl][:, :],
                                                   in_=z_i[128 * g2:128 * (g2 + 1), 256 * c:256 * (c + 1)]), 16)

                nld = c * H + h + 1

                @block.vector
                def _(v, c=c, h=h, sl=sl, need=16 * ((c * H + h) // 2 + 1), pen=pe.n, acn=ac.n):
                    v.wait_ge(sls[sl].sem, need)
                    v.wait_ge(pe.sem, pen)
                    _inc(ve, v.scalar_tensor_tensor(spt[:, :], sinb[:, 64 * h:64 * (h + 1)],
                                                    cdcbc[:, 64 * c + h:64 * c + h + 1],
                                                    spb[sl][:, :], ALU.mult, ALU.add), 1)

                @block.tensor
                def _(tn, c=c, h=h, sl=sl, need=16 * ((c * H + h) // 2 + 1), ven=ve.n):
                    tn.wait_ge(cxs[sl].sem, need)
                    tn.wait_ge(ve.sem, ven)
                    _inc(pe, tn.matmul(ps_y[:, :], spt[:, :], cxb[sl][:, :],
                                       start=True, stop=True), 1)

                @block.vector
                def _(v, c=c, h=h, hpar=hpar, g2=g2, ysl=ysl, base=64 * hpar, pen=pe.n,
                      yneed=16 * (c * 16 + g2 // 2 + 1)):
                    v.wait_ge(pe.sem, pen)
                    if hpar == 0:
                        v.wait_ge(yls[ysl].sem, yneed)
                    ins = v.tensor_tensor(yst[ysl][base:base + 64, :], yst[ysl][base:base + 64, :],
                                          ps_y[:, :], ALU.add)
                    _inc(ve, ins, 1)

                if hpar == 1:
                    @block.scalar
                    def _(sc, c=c, g2=g2, ysl=ysl, ven=ve.n, zneed=16 * (c * 16 + g2 // 2 + 1)):
                        sc.wait_ge(ve.sem, ven)
                        sc.wait_ge(zls[ysl].sem, zneed)
                        _inc(ac, sc.activation(zsil[:, :], zt[ysl][:, :], AF.Silu), 1)

                    @block.vector
                    def _(v, c=c, g2=g2, ysl=ysl, acn=ac.n):
                        v.wait_ge(ac.sem, acn)
                        dst = ygbuf[:, g2 * T + 256 * c:g2 * T + 256 * (c + 1)]
                        _inc(ve, v.tensor_tensor(dst, yst[ysl][:, :], zsil[:, :], ALU.mult), 1)

                    yg_ac[(c, g2)] = ve.n

        for g in range(32):
            slot = g % 2

            @block.scalar
            def _(sc, g=g, slot=slot, ven=ve.n, pen=pe.n):
                sc.wait_ge(ve.sem, ven)
                if g >= 2:
                    sc.wait_ge(pe.sem, pen)
                _inc(ac, sc.activation(sqst[slot][:, :], ygbuf[:, g * T:(g + 1) * T], AF.Square), 1)

            @block.tensor
            def _(tn, g=g, slot=slot, acn=ac.n):
                tn.wait_ge(ac.sem, acn)
                for ci in range(2):
                    _inc(pe, tn.matmul([ps_s0, ps_s1][ci][:, :], ones2b[:, 0:1],
                                       sqst[slot][:, 512 * ci:512 * (ci + 1)],
                                       start=(g == 0), stop=(g == 31)), 1)

        def rowstats(tag, sum_ps, sq_ps, nfeat, have_sum):
            @block.scalar
            def _(sc, pen=pe.n):
                sc.wait_ge(pe.sem, pen)
                if have_sum:
                    _inc(ac, sc.copy(srow[0:1, 0:512], sum_ps[0][:, :]), 1)
                    _inc(ac, sc.copy(srow[0:1, 512:1024], sum_ps[1][:, :]), 1)
                _inc(ac, sc.copy(sqrow[0:1, 0:512], sq_ps[0][:, :]), 1)
                _inc(ac, sc.copy(sqrow[0:1, 512:1024], sq_ps[1][:, :]), 1)

            @block.vector
            def _(v, acn=ac.n, nf=float(nfeat), have_sum=have_sum):
                v.wait_ge(ac.sem, acn)
                if have_sum:
                    v.tensor_scalar(murow[:, :], srow[0:1, :], 1.0 / nf, None, ALU.mult)
                    v.tensor_scalar(scr0[:, :], sqrow[0:1, :], 1.0 / nf, None, ALU.mult)
                    v.tensor_tensor(scr1[:, :], murow[:, :], murow[:, :], ALU.mult)
                    v.tensor_sub(scr0[:, :], scr0[:, :], scr1[:, :])
                else:
                    v.memset(murow[:, :], 0.0)
                    v.tensor_scalar(scr0[:, :], sqrow[0:1, :], 1.0 / nf, None, ALU.mult)
                _inc(ve, v.tensor_scalar(scr0[:, :], scr0[:, :], EPS, None, ALU.add), 1)

            @block.scalar
            def _(sc, ven=ve.n):
                sc.wait_ge(ve.sem, ven)
                _inc(ac, sc.activation(scr1[:, :], scr0[:, :], AF.Sqrt), 1)

            @block.vector
            def _(v, acn=ac.n):
                v.wait_ge(ac.sem, acn)
                v.reciprocal(rsrow[:, :], scr1[:, :])
                v.tensor_copy(murb[:, :], murow[:, :])
                _inc(ve, v.tensor_copy(rsrb[:, :], rsrow[:, :]), 1)

            for rowb, dst in [(murb, mubc), (rsrb, rsbc)]:
                for ci in range(2):
                    @block.tensor
                    def _(tn, rowb=rowb, ci=ci, ven=ve.n, acn=ac.n):
                        tn.wait_ge(ve.sem, ven)
                        tn.wait_ge(ac.sem, acn)
                        _inc(pe, tn.matmul(ps_bc[:, 0:512], ones_c[0:1, :],
                                           rowb[:, 512 * ci:512 * (ci + 1)], start=True, stop=True), 1)

                    @block.scalar
                    def _(sc, dst=dst, ci=ci, pen=pe.n):
                        sc.wait_ge(pe.sem, pen)
                        _inc(ac, sc.copy(dst[:, 512 * ci:512 * (ci + 1)], ps_bc[:, 0:512]), 1)

        rowstats("rms", None, [ps_s0, ps_s1], DI, False)

        for g in range(32):
            @block.vector
            def _(v, g=g, acn=ac.n):
                v.wait_ge(ac.sem, acn)
                ro = rot[g % 2]
                dst = ygbuf[:, g * T:(g + 1) * T]
                v.tensor_tensor(ro[:, :], dst, rsbc[:, :], ALU.mult)
                _inc(ve, v.tensor_scalar(dst, ro[:, :], normw[:, g:g + 1], None, ALU.mult), 1)

        ve_ynorm = ve.n
        pe_mark = {}
        ev_mark = {}
        for ot in range(16):
            slot = ot % 2

            @block.gpsimd
            def _(g, ot=ot, slot=slot):
                if ot >= 2:
                    g.wait_ge(pe.sem, pe_mark[ot - 2])
                _inc(wls[slot], g.dma_start(out=wbuf[slot][:, :], in_=wout_i[ot].transpose([1, 0, 2])), 16)

            @block.tensor
            def _(tn, ot=ot, slot=slot, need=16 * (ot // 2 + 1), ven=ve_ynorm):
                tn.wait_ge(wls[slot].sem, need)
                tn.wait_ge(ve.sem, ven)
                if ot >= 1:
                    tn.wait_ge(ac.sem, ev_mark[ot - 1])
                for ci in range(2):
                    pm = [ps_m0, ps_m1][ci]
                    for k in range(32):
                        ins = tn.matmul(pm[:, 0:512], wbuf[slot][:, k * 128:(k + 1) * 128],
                                        ynbuf[:, k * T + 512 * ci:k * T + 512 * (ci + 1)],
                                        start=(k == 0), stop=(k == 31))
                        if ci == 1 and k == 31:
                            _inc(pe, ins, 1)

            pe_mark[ot] = pe.n

            @block.scalar
            def _(sc, ot=ot, pen=pe.n):
                sc.wait_ge(pe.sem, pen)
                sc.copy(mbuf[:, ot * T:ot * T + 512], ps_m0[:, 0:512])
                _inc(ac, sc.copy(mbuf[:, ot * T + 512:(ot + 1) * T], ps_m1[:, 0:512]), 1)

            ev_mark[ot] = ac.n

        for g in range(16):
            slot = g % 2

            @block.scalar
            def _(sc, g=g, slot=slot, acn=ac.n, pen=pe.n):
                sc.wait_ge(ac.sem, acn)
                if g >= 2:
                    sc.wait_ge(pe.sem, pen)
                _inc(ac, sc.activation(sqst[slot][:, :], mbuf[:, g * T:(g + 1) * T], AF.Square), 1)

            @block.tensor
            def _(tn, g=g, slot=slot, acn=ac.n):
                tn.wait_ge(ac.sem, acn)
                for ci in range(2):
                    _inc(pe, tn.matmul([ps_s0, ps_s1][ci][:, :], ones2b[:, 0:1],
                                       mbuf[:, g * T + 512 * ci:g * T + 512 * (ci + 1)],
                                       start=(g == 0), stop=(g == 15)), 1)
                    _inc(pe, tn.matmul([ps_q0, ps_q1][ci][:, :], ones2b[:, 0:1],
                                       sqst[slot][:, 512 * ci:512 * (ci + 1)],
                                       start=(g == 0), stop=(g == 15)), 1)

        rowstats("ln2", [ps_s0, ps_s1], [ps_q0, ps_q1], DM, True)

        for g in range(16):
            slot = g % 2

            @block.sync
            def _(s, g=g, slot=slot, ven=ve.n):
                s.wait_ge(ve.sem, ven)
                _inc(rls[slot], s.dma_start(out=rst[slot][:, :], in_=resid_i[128 * g:128 * (g + 1), :]), 16)

            @block.vector
            def _(v, g=g, slot=slot, need=16 * (g // 2 + 1), acn=ac.n, ron=ros[slot].n, pen=pe.n):
                v.wait_ge(rls[slot].sem, need)
                v.wait_ge(ac.sem, acn)
                v.wait_ge(ros[slot].sem, ron)
                if g >= 2:
                    v.wait_ge(pe.sem, pen)
                ro = rot[slot]
                v.tensor_sub(ro[:, :], mbuf[:, g * T:(g + 1) * T], mubc[:, :])
                v.tensor_tensor(ro[:, :], ro[:, :], rsbc[:, :], ALU.mult)
                v.tensor_tensor(ro[:, :], ro[:, :], rst[slot][:, :], ALU.add)
                _inc(ve, v.tensor_copy(xlt[slot][:, :], ro[:, :]), 1)

            @block.scalar
            def _(sc, g=g, slot=slot, ven=ve.n):
                sc.wait_ge(ve.sem, ven)
                _inc(ac, sc.activation(sqst[slot][:, :], xlt[slot][:, :], AF.Square), 1)

            @block.tensor
            def _(tn, g=g, slot=slot, acn=ac.n):
                tn.wait_ge(ac.sem, acn)
                for ci in range(2):
                    _inc(pe, tn.matmul([ps_s0, ps_s1][ci][:, :], ones2b[:, 0:1],
                                       xlt[slot][:, 512 * ci:512 * (ci + 1)],
                                       start=(g == 0), stop=(g == 15)), 1)
                    _inc(pe, tn.matmul([ps_q0, ps_q1][ci][:, :], ones2b[:, 0:1],
                                       sqst[slot][:, 512 * ci:512 * (ci + 1)],
                                       start=(g == 0), stop=(g == 15)), 1)

            @block.sync
            def _(s, g=g, slot=slot, ven=ve.n):
                s.wait_ge(ve.sem, ven)
                _inc(ros[slot], s.dma_start(out=resid_o[128 * g:128 * (g + 1), :], in_=rot[slot][:, :]), 16)

        rowstats("ln3", [ps_s0, ps_s1], [ps_q0, ps_q1], DM, True)

        for g in range(16):
            slot = g % 2

            @block.sync
            def _(s, g=g, slot=slot, ven=ve.n, need=16 * (8 + g // 2)):
                s.wait_ge(ve.sem, ven)
                s.wait_ge(ros[slot].sem, ros[slot].n)
                _inc(rls[slot], s.dma_start(out=rst[slot][:, :], in_=resid_o[128 * g:128 * (g + 1), :]), 16)

            @block.vector
            def _(v, g=g, slot=slot, need=16 * (8 + g // 2 + 1), acn=ac.n, xon=xos[slot].n):
                v.wait_ge(rls[slot].sem, need)
                v.wait_ge(ac.sem, acn)
                v.wait_ge(xos[slot].sem, xon)
                ro = rot[slot]
                v.tensor_sub(ro[:, :], rst[slot][:, :], mubc[:, :])
                _inc(ve, v.tensor_tensor(xlt[slot][:, :], ro[:, :], rsbc[:, :], ALU.mult), 1)

            @block.sync
            def _(s, g=g, slot=slot, ven=ve.n):
                s.wait_ge(ve.sem, ven)
                _inc(xos[slot], s.dma_start(out=xln3_o[128 * g:128 * (g + 1), :], in_=xlt[slot][:, :]), 16)

        @block.sync
        def _(s):
            s.wait_ge(mst.sem, mst.n)
            for x in (ros[0], ros[1], xos[0], xos[1]):
                s.wait_ge(x.sem, x.n)
    return nc


def host_p3_inputs(core, res1, res2, all_res1, all_res2, norm_w, wout_tiled, resid_own):
    j = core % 4
    grp = core // 4
    sg = np.zeros((4, H, ND, PD), np.float32)
    dg = np.ones((4, 64), np.float32)
    for jp in range(4):
        peer = 4 * grp + jp
        if jp < j:
            sg[jp] = all_res2[peer]["sloc_o"]
        if 1 <= jp <= j - 1:
            dg[jp] = all_res1[peer]["dec_o"][:, 8]
    sprev = np.zeros((NCH, H, ND, PD), np.float32)
    sprev[1:] = res2["sprev_o"]
    cdc = np.ascontiguousarray(res1["dec_o"][:, 4:4 + NCH].T.reshape(1, NCH * 64))
    return dict(
        y_i=res2["y_o"], sg_i=sg, sprev_i=sprev, cexp_i=res2["cexp_o"],
        z_i=np.ascontiguousarray(res1["z_o"][:, HALO:]),
        dgrow=np.ascontiguousarray(dg.reshape(1, 256)), cdcrow=cdc,
        normw=np.ascontiguousarray(np.asarray(norm_w, np.float32).reshape(32, 128).T),
        wout=wout_tiled, resid=resid_own,
        onesin=np.ones((1, 128), np.float32), ones2=np.ones((128, 1), np.float32),
    )



def build_ffn(kind):
    had = kind == "had"
    nc = bass.Bass("TRN2", target_bir_lowering=False)
    WDT = BF16 if not had else F32
    T2 = 512
    NIT = INTER // 128
    xln_i = nc.dram_tensor("xln_i", [DM, T], BF16, kind="ExternalInput").ap()
    resid_i = nc.dram_tensor("resid", [DM, T], F32, kind="ExternalInput").ap()
    wg_i = nc.dram_tensor("wg_i", [NIT, 16, 128, 128], WDT, kind="ExternalInput").ap()
    wu_i = nc.dram_tensor("wu_i", [NIT, 16, 128, 128], WDT, kind="ExternalInput").ap()
    wd_i = nc.dram_tensor("wd_i", [16, NIT, 128, 128], WDT, kind="ExternalInput").ap()
    ones_i = nc.dram_tensor("onesin", [1, 128], F32, kind="ExternalInput").ap()
    ones2_i = nc.dram_tensor("ones2", [128, 1], F32, kind="ExternalInput").ap()
    if had:
        su_ug_i = nc.dram_tensor("su_ug", [128, 16], F32, kind="ExternalInput").ap()
        su_dp_i = nc.dram_tensor("su_dp", [128, 64], F32, kind="ExternalInput").ap()
        wsup_i = nc.dram_tensor("wsup", [128, 64], F32, kind="ExternalInput").ap()
        wsgt_i = nc.dram_tensor("wsgt", [128, 64], F32, kind="ExternalInput").ap()
        wsdp_i = nc.dram_tensor("wsdp", [128, 16], F32, kind="ExternalInput").ap()
        h128a_i = nc.dram_tensor("h128a", [128, 128], F32, kind="ExternalInput").ap()
        h128b_i = nc.dram_tensor("h128b", [128, 128], F32, kind="ExternalInput").ap()
    resid_o = nc.dram_tensor("resid_o", [DM, T], F32, kind="ExternalOutput").ap()
    m_tmp = nc.dram_tensor("m_tmp", [DM, T], BF16, kind="Internal").ap()

    with contextlib.ExitStack() as stk:
        def sb(name, shape, dt):
            return stk.enter_context(nc.sbuf_tensor(name, shape, dt))

        def sem(name):
            return Ctr(stk.enter_context(nc.semaphore(name)))

        def psm(name, shape):
            return stk.enter_context(nc.psum_tensor(name, shape, F32))

        ones_c = sb("ones_c", [1, 128], BF16)
        ones2b = sb("ones2b", [128, 1], BF16)
        epsc = sb("epsc", [1, 1], F32)
        interbuf = sb("interbuf", [128, NIT * T2], BF16)
        xbuf = sb("xbuf", [128, 16 * T], BF16)
        wbufg = [sb("wbufg0", [128, 2048], BF16), sb("wbufg1", [128, 2048], BF16)]
        wbufu = [sb("wbufu0", [128, 2048], BF16), sb("wbufu1", [128, 2048], BF16)]
        wbufd = [sb("wbufd0", [128, 8192], BF16), sb("wbufd1", [128, 8192], BF16)]
        sgt = sb("sgt", [128, T2], BF16)
        upsc = sb("upsc", [128, T2], F32)
        btmp = sb("btmp", [128, T2], BF16) if had else None
        mstage = [sb("mstage0", [128, T2], BF16), sb("mstage1", [128, T2], BF16)]
        sqst = [sb("sqst0", [128, T2], BF16), sb("sqst1", [128, T2], BF16)]
        rst = [sb("rst0", [128, T], F32), sb("rst1", [128, T], F32)]
        rot = [sb("rot0", [128, T], F32), sb("rot1", [128, T], F32)]
        mld = [sb("mld0", [128, T], BF16), sb("mld1", [128, T], BF16)]
        srow = sb("srow", [1, T], F32)
        sqrow = sb("sqrow", [1, T], F32)
        scr0 = sb("scr0", [1, T], F32)
        scr1 = sb("scr1", [1, T], F32)
        murow = sb("murow", [1, T], F32)
        rsrow = sb("rsrow", [1, T], F32)
        murb = sb("murb", [1, T], BF16)
        rsrb = sb("rsrb", [1, T], BF16)
        mubc = sb("mubc", [128, T], BF16)
        rsbc = sb("rsbc", [128, T], BF16)
        if had:
            su_ug = sb("su_ug_s", [128, 16], F32)
            su_dp = sb("su_dp_s", [128, 64], F32)
            wsup64 = sb("wsup64", [128, 64], F32)
            wsgt64 = sb("wsgt64", [128, 64], F32)
            wsdp64 = sb("wsdp64", [128, 16], F32)

        cst = sem("cst")
        wgs = [sem("wgs0"), sem("wgs1")]
        wus = [sem("wus0"), sem("wus1")]
        wds = [sem("wds0"), sem("wds1")]
        rls = [sem("rls0"), sem("rls1")]
        mls = [sem("mls0"), sem("mls1")]
        mts = [sem("mts0"), sem("mts1")]
        ros = [sem("ros0"), sem("ros1")]
        pe = sem("pe")
        ve = sem("ve")
        ac = sem("ac")
        po = sem("po")

        ps_a = [psm("ps_a0", [128, 512]), psm("ps_a1", [128, 512])]
        ps_s = [psm("ps_s0", [1, 512]), psm("ps_s1", [1, 512])]
        ps_q = [psm("ps_q0", [1, 512]), psm("ps_q1", [1, 512])]
        ps_bc = psm("ps_bc", [128, 512])
        block = stk.enter_context(nc.Block())

        if had:
            h128a = sb("h128a_s", [128, 128], BF16)
            h128b = sb("h128b_s", [128, 128], BF16)

        @block.gpsimd
        def _(g):
            _inc(cst, g.dma_start(out=ones_c[:, :], in_=ones_i[:, :]), 16)
            _inc(cst, g.dma_start(out=ones2b[:, :], in_=ones2_i[:, :]), 16)
            if had:
                _inc(cst, g.dma_start(out=h128a[:, :], in_=h128a_i[:, :]), 16)
                _inc(cst, g.dma_start(out=h128b[:, :], in_=h128b_i[:, :]), 16)

        @block.sync
        def _(s):
            for g in range(16):
                _inc(cst, s.dma_start(out=xbuf[:, g * T:(g + 1) * T],
                                      in_=xln_i[128 * g:128 * (g + 1), :]), 16)
            if had:
                _inc(cst, s.dma_start(out=su_ug[:, :], in_=su_ug_i[:, :]), 16)
                _inc(cst, s.dma_start(out=su_dp[:, :], in_=su_dp_i[:, :]), 16)
                _inc(cst, s.dma_start(out=wsup64[:, :], in_=wsup_i[:, :]), 16)
                _inc(cst, s.dma_start(out=wsgt64[:, :], in_=wsgt_i[:, :]), 16)
                _inc(cst, s.dma_start(out=wsdp64[:, :], in_=wsdp_i[:, :]), 16)

        @block.vector
        def _(v, cstn=cst.n):
            v.wait_ge(cst.sem, cstn)
            v.memset(epsc[:, :], EPS)
            if had:
                v.tensor_scalar(wsup64[:, :], wsup64[:, :], 64.0, None, ALU.mult)
                v.tensor_scalar(wsgt64[:, :], wsgt64[:, :], 64.0, None, ALU.mult)
                v.tensor_scalar(wsdp64[:, :], wsdp64[:, :], 64.0, None, ALU.mult)
                for g in range(16):
                    xt = xbuf[:, g * T:(g + 1) * T]
                    v.tensor_scalar(xt, xt, su_ug[:, g:g + 1], None, ALU.mult)
            _inc(ve, v.memset(epsc[:, :], EPS), 1)

        if had:
            for g in range(16):
                @block.tensor
                def _(tn, g=g, ven=ve.n, acn=ac.n):
                    tn.wait_ge(ve.sem, ven)
                    tn.wait_ge(ac.sem, acn)
                    for ci in range(2):
                        _inc(pe, tn.matmul(ps_a[ci][:, :], h128a[:, :],
                                           xbuf[:, g * T + 512 * ci:g * T + 512 * (ci + 1)],
                                           start=True, stop=True), 1)

                @block.scalar
                def _(sc, g=g, pen=pe.n):
                    sc.wait_ge(pe.sem, pen)
                    sc.copy(xbuf[:, g * T:g * T + 512], ps_a[0][:, :])
                    _inc(ac, sc.copy(xbuf[:, g * T + 512:(g + 1) * T], ps_a[1][:, :]), 1)

            for s_ in range(4):
                step = 1 << s_
                for g0 in range(16):
                    if g0 & step:
                        continue
                    g1 = g0 + step
                    for th in range(2):
                        a = xbuf[:, g0 * T + th * T2:g0 * T + (th + 1) * T2]
                        b = xbuf[:, g1 * T + th * T2:g1 * T + (th + 1) * T2]

                        @block.vector
                        def _(v, a=a, b=b, acn=ac.n, pon=po.n):
                            v.wait_ge(ac.sem, acn)
                            v.wait_ge(po.sem, pon)
                            _inc(ve, v.tensor_add(btmp[:, :], a, b), 1)

                        @block.gpsimd
                        def _(g, a=a, b=b, ven=ve.n):
                            g.wait_ge(ve.sem, ven)
                            _inc(po, g.tensor_tensor(b, a, b, ALU.subtract), 1)

                        @block.scalar
                        def _(sc, a=a, pon=po.n, ven=ve.n):
                            sc.wait_ge(po.sem, pon)
                            sc.wait_ge(ve.sem, ven)
                            _inc(ac, sc.copy(a, btmp[:, :]), 1)

        x_po = po.n

        for th in range(2):
            if th == 0:
                pe_gu = {}
            gu_prev = {it: pe_gu.get((0, NIT - 2 + it)) for it in (0, 1)} if th == 1 else {}
            sil_ac = {}
            int_ve = {}
            for it in range(NIT):
                slot = it % 2

                @block.gpsimd
                def _(g, th=th, it=it, slot=slot, pen=pe_gu.get((th, it - 2), gu_prev.get(it))):
                    if pen is not None:
                        g.wait_ge(pe.sem, pen)
                    _inc(wgs[slot], g.dma_start(out=wbufg[slot][:, :], in_=wg_i[it].transpose([1, 0, 2])), 16)
                    _inc(wus[slot], g.dma_start(out=wbufu[slot][:, :], in_=wu_i[it].transpose([1, 0, 2])), 16)

                @block.tensor
                def _(tn, th=th, it=it, slot=slot, need=16 * (th * NIT // 2 + it // 2 + 1),
                      ven=ve.n, acn=ac.n, pon=x_po, cstn=cst.n):
                    tn.wait_ge(wgs[slot].sem, need)
                    tn.wait_ge(wus[slot].sem, need)
                    tn.wait_ge(cst.sem, cstn)
                    tn.wait_ge(ve.sem, ven)
                    tn.wait_ge(po.sem, pon)
                    tn.wait_ge(ac.sem, acn)
                    for k in range(16):
                        tn.matmul(ps_a[0][:, :], wbufg[slot][:, k * 128:(k + 1) * 128],
                                  xbuf[:, k * T + th * T2:k * T + (th + 1) * T2],
                                  start=(k == 0), stop=(k == 15))
                    for k in range(16):
                        ins = tn.matmul(ps_a[1][:, :], wbufu[slot][:, k * 128:(k + 1) * 128],
                                        xbuf[:, k * T + th * T2:k * T + (th + 1) * T2],
                                        start=(k == 0), stop=(k == 15))
                        if k == 15:
                            _inc(pe, ins, 1)

                pe_gu[(th, it)] = pe.n

                @block.scalar
                def _(sc, th=th, it=it, pen=pe.n, ven=ve.n):
                    sc.wait_ge(pe.sem, pen)
                    sc.wait_ge(ve.sem, ven)
                    if had:
                        ins = sc.activation(sgt[:, :], ps_a[0][:, :], AF.Silu, scale=wsgt64[:, it:it + 1])
                    else:
                        ins = sc.activation(sgt[:, :], ps_a[0][:, :], AF.Silu)
                    _inc(ac, ins, 1)

                @block.vector
                def _(v, th=th, it=it, acn=ac.n):
                    v.wait_ge(ac.sem, acn)
                    dst = interbuf[:, it * T2:(it + 1) * T2]
                    if had:
                        v.tensor_scalar(upsc[:, :], ps_a[1][:, :], wsup64[:, it:it + 1], None, ALU.mult)
                        ins = v.scalar_tensor_tensor(dst, upsc[:, :], su_dp[:, it:it + 1],
                                                     sgt[:, :], ALU.mult, ALU.mult)
                    else:
                        ins = v.tensor_tensor(dst, sgt[:, :], ps_a[1][:, :], ALU.mult)
                    _inc(ve, ins, 1)

            if had:
                for it in range(NIT):
                    @block.tensor
                    def _(tn, it=it, ven=ve.n, acn=ac.n):
                        tn.wait_ge(ve.sem, ven)
                        tn.wait_ge(ac.sem, acn)
                        _inc(pe, tn.matmul(ps_a[0][:, :], h128b[:, :],
                                           interbuf[:, it * T2:(it + 1) * T2],
                                           start=True, stop=True), 1)

                    @block.scalar
                    def _(sc, it=it, pen=pe.n):
                        sc.wait_ge(pe.sem, pen)
                        _inc(ac, sc.copy(interbuf[:, it * T2:(it + 1) * T2], ps_a[0][:, :]), 1)

                for s_ in range(6):
                    step = 1 << s_
                    for g0 in range(NIT):
                        if g0 & step:
                            continue
                        g1 = g0 + step
                        a = interbuf[:, g0 * T2:(g0 + 1) * T2]
                        b = interbuf[:, g1 * T2:(g1 + 1) * T2]

                        @block.vector
                        def _(v, a=a, b=b, acn=ac.n, pon=po.n):
                            v.wait_ge(ac.sem, acn)
                            v.wait_ge(po.sem, pon)
                            _inc(ve, v.tensor_add(btmp[:, :], a, b), 1)

                        @block.gpsimd
                        def _(g, a=a, b=b, ven=ve.n):
                            g.wait_ge(ve.sem, ven)
                            _inc(po, g.tensor_tensor(b, a, b, ALU.subtract), 1)

                        @block.scalar
                        def _(sc, a=a, pon=po.n, ven=ve.n):
                            sc.wait_ge(po.sem, pon)
                            sc.wait_ge(ve.sem, ven)
                            _inc(ac, sc.copy(a, btmp[:, :]), 1)

            d_prev = {ot: pe_d.get(14 + ot) for ot in (0, 1)} if th == 1 else {}
            pe_d = {}
            for ot in range(16):
                slot = ot % 2

                @block.gpsimd
                def _(g, th=th, ot=ot, slot=slot,
                      pen=None):
                    pv = pe_d.get(ot - 2)
                    if pv is None and th == 1:
                        pv = d_prev.get(ot)
                    if pv is not None:
                        g.wait_ge(pe.sem, pv)
                    _inc(wds[slot], g.dma_start(out=wbufd[slot][:, :], in_=wd_i[ot].transpose([1, 0, 2])), 16)

                @block.tensor
                def _(tn, th=th, ot=ot, slot=slot, need=16 * (th * 8 + ot // 2 + 1),
                      ven=ve.n, acn=ac.n, pon=po.n):
                    tn.wait_ge(wds[slot].sem, need)
                    tn.wait_ge(ve.sem, ven)
                    tn.wait_ge(po.sem, pon)
                    tn.wait_ge(ac.sem, acn)
                    for k in range(NIT):
                        ins = tn.matmul(ps_a[0][:, :], wbufd[slot][:, k * 128:(k + 1) * 128],
                                        interbuf[:, k * T2:(k + 1) * T2],
                                        start=(k == 0), stop=(k == NIT - 1))
                        if k == NIT - 1:
                            _inc(pe, ins, 1)

                pe_d[ot] = pe.n

                @block.scalar
                def _(sc, th=th, ot=ot, slot=slot, pen=pe.n, mtn=mts[slot].n):
                    sc.wait_ge(pe.sem, pen)
                    sc.wait_ge(mts[slot].sem, mtn)
                    if had:
                        sc.activation(mstage[slot][:, :], ps_a[0][:, :], AF.Copy,
                                      scale=wsdp64[:, ot:ot + 1])
                    else:
                        sc.copy(mstage[slot][:, :], ps_a[0][:, :])
                    _inc(ac, sc.activation(sqst[slot][:, :], mstage[slot][:, :], AF.Square), 1)

                @block.tensor
                def _(tn, th=th, ot=ot, slot=slot, acn=ac.n):
                    tn.wait_ge(ac.sem, acn)
                    _inc(pe, tn.matmul(ps_s[th][:, :], ones2b[:, 0:1], mstage[slot][:, :],
                                       start=(ot == 0), stop=(ot == 15)), 1)
                    _inc(pe, tn.matmul(ps_q[th][:, :], ones2b[:, 0:1], sqst[slot][:, :],
                                       start=(ot == 0), stop=(ot == 15)), 1)

                @block.gpsimd
                def _(g, th=th, ot=ot, slot=slot, acn=ac.n):
                    g.wait_ge(ac.sem, acn)
                    _inc(mts[slot], g.dma_start(out=m_tmp[128 * ot:128 * (ot + 1), th * T2:(th + 1) * T2],
                                                in_=mstage[slot][:, :]), 16)

            prev_ped = pe_d

        @block.scalar
        def _(sc, pen=pe.n):
            sc.wait_ge(pe.sem, pen)
            sc.copy(srow[0:1, 0:512], ps_s[0][:, :])
            sc.copy(srow[0:1, 512:1024], ps_s[1][:, :])
            sc.copy(sqrow[0:1, 0:512], ps_q[0][:, :])
            _inc(ac, sc.copy(sqrow[0:1, 512:1024], ps_q[1][:, :]), 1)

        @block.vector
        def _(v, acn=ac.n):
            v.wait_ge(ac.sem, acn)
            v.tensor_scalar(murow[:, :], srow[0:1, :], 1.0 / DM, None, ALU.mult)
            v.tensor_scalar(scr0[:, :], sqrow[0:1, :], 1.0 / DM, None, ALU.mult)
            v.tensor_tensor(scr1[:, :], murow[:, :], murow[:, :], ALU.mult)
            v.tensor_sub(scr0[:, :], scr0[:, :], scr1[:, :])
            _inc(ve, v.tensor_scalar(scr0[:, :], scr0[:, :], EPS, None, ALU.add), 1)

        @block.scalar
        def _(sc, ven=ve.n):
            sc.wait_ge(ve.sem, ven)
            _inc(ac, sc.activation(scr1[:, :], scr0[:, :], AF.Sqrt), 1)

        @block.vector
        def _(v, acn=ac.n):
            v.wait_ge(ac.sem, acn)
            v.reciprocal(rsrow[:, :], scr1[:, :])
            v.tensor_copy(murb[:, :], murow[:, :])
            _inc(ve, v.tensor_copy(rsrb[:, :], rsrow[:, :]), 1)

        for rowb, dst in [(murb, mubc), (rsrb, rsbc)]:
            for ci in range(2):
                @block.tensor
                def _(tn, rowb=rowb, ci=ci, ven=ve.n, acn=ac.n):
                    tn.wait_ge(ve.sem, ven)
                    tn.wait_ge(ac.sem, acn)
                    _inc(pe, tn.matmul(ps_bc[:, 0:512], ones_c[0:1, :],
                                       rowb[:, 512 * ci:512 * (ci + 1)], start=True, stop=True), 1)

                @block.scalar
                def _(sc, dst=dst, ci=ci, pen=pe.n):
                    sc.wait_ge(pe.sem, pen)
                    _inc(ac, sc.copy(dst[:, 512 * ci:512 * (ci + 1)], ps_bc[:, 0:512]), 1)

        for g in range(16):
            slot = g % 2

            @block.sync
            def _(s, g=g, slot=slot, ven=ve.n):
                s.wait_ge(ve.sem, ven)
                _inc(rls[slot], s.dma_start(out=rst[slot][:, :], in_=resid_i[128 * g:128 * (g + 1), :]), 16)

            @block.gpsimd
            def _(gq, g=g, slot=slot, ven=ve.n, mtn=mts[slot].n):
                gq.wait_ge(ve.sem, ven)
                gq.wait_ge(mts[slot].sem, mtn)
                _inc(mls[slot], gq.dma_start(out=mld[slot][:, :], in_=m_tmp[128 * g:128 * (g + 1), :]), 16)

            @block.vector
            def _(v, g=g, slot=slot, rneed=16 * (g // 2 + 1), mneed=16 * (g // 2 + 1),
                  acn=ac.n, ron=ros[slot].n):
                v.wait_ge(rls[slot].sem, rneed)
                v.wait_ge(mls[slot].sem, mneed)
                v.wait_ge(ac.sem, acn)
                v.wait_ge(ros[slot].sem, ron)
                ro = rot[slot]
                v.tensor_sub(ro[:, :], mld[slot][:, :], mubc[:, :])
                v.tensor_tensor(ro[:, :], ro[:, :], rsbc[:, :], ALU.mult)
                _inc(ve, v.tensor_tensor(ro[:, :], ro[:, :], rst[slot][:, :], ALU.add), 1)

            @block.sync
            def _(s, g=g, slot=slot, ven=ve.n):
                s.wait_ge(ve.sem, ven)
                _inc(ros[slot], s.dma_start(out=resid_o[128 * g:128 * (g + 1), :], in_=rot[slot][:, :]), 16)

        @block.sync
        def _(s):
            for x in (ros[0], ros[1], mts[0], mts[1]):
                s.wait_ge(x.sem, x.n)
    return nc


_P1_CONST = {}


def host_p1_consts(layer, w_inproj, conv_w, conv_b, dt_bias, apar):
    key = ("p1", layer)
    if key not in _P1_CONST:
        cw = np.zeros((34 * 128, 4), np.float32)
        cw[:DXBC] = np.asarray(conv_w, np.float32)
        cb = np.zeros((34 * 128,), np.float32)
        if conv_b is not None:
            cb[:DXBC] = np.asarray(conv_b, np.float32)
        _P1_CONST[key] = dict(
            w_in=_tile_T(np.asarray(w_inproj, np.float32)),
            conv_w=cw.reshape(34, 128, 4),
            conv_b=np.ascontiguousarray(cb.reshape(34, 128).T),
            dtb=np.asarray(dt_bias, np.float32).reshape(64, 1),
            apar=np.asarray(apar, np.float32).reshape(64, 1),
            onesin=np.ones((1, 128), np.float32),
            masks=_tri_masks(),
        )
    return _P1_CONST[key]


def host_p1_inputs(core, resid_tokmajor, consts):
    t0 = core * T
    lo = t0 - HALO
    sl = np.zeros((TE, DM), np.float32)
    src_lo = max(lo, (core // 4) * 4 * T)
    sl[src_lo - lo:, :] = resid_tokmajor[src_lo:t0 + T, :]
    resid = np.ascontiguousarray(sl.T)
    return dict(resid=resid, **consts)


_NC_CACHE = {}


def _nc(key, builder, *a):
    if key not in _NC_CACHE:
        _NC_CACHE[key] = builder(*a)
    return _NC_CACHE[key]


def kernel(**inputs):
    cores = list(range(NCORES))
    u = np.ascontiguousarray(np.asarray(inputs["L0_attn_u"], np.float32).reshape(8192, DM))
    ones1 = np.ones((1, 128), np.float32)
    ones2 = np.ones((128, 1), np.float32)

    ncR = _nc("R", build_recon)
    inR = [host_recon_inputs(c, inputs["L0_ffn_gate_cp"], inputs["L0_ffn_up_cp"],
                             inputs["L0_ffn_down_cp"]) for c in cores]
    resR = run_bass_kernel_spmd(ncR, inR, core_ids=cores).results
    wiv = [np.concatenate([resR[c]["w_iv"][w] for c in cores], axis=0) for w in range(3)]
    wg_t = _tile_T(_dejump(wiv[0]).reshape(INTER, DM))
    wu_t = _tile_T(_dejump(wiv[1]).reshape(INTER, DM))
    wd_t = _tile_T(_dejump(wiv[2]).reshape(DM, INTER))

    resid_tok = u
    for layer in range(2):
        L = f"L{layer}"
        if layer == 0:
            p1c = host_p1_consts(0, inputs["L0_attn_in_proj_w"], inputs["L0_attn_conv_w"],
                                 None, inputs["L0_attn_dt_bias"], inputs["L0_attn_A_log"])
            nc1 = _nc("P1a", build_p1, True)
            dpar = np.asarray(inputs["L0_attn_D_param"], np.float32).reshape(1, 64)
            normw = inputs["L0_attn_norm_w"]
            wout_t = _tile_T(np.asarray(inputs["L0_attn_out_proj_w"], np.float32))
        else:
            p1c = host_p1_consts(1, inputs["L1_attn_in_proj_w"], inputs["L1_attn_conv_w"],
                                 inputs["L1_attn_conv_b"], inputs["L1_attn_dt_bias"],
                                 inputs["L1_attn_A"])
            nc1 = _nc("P1b", build_p1, False)
            dpar = np.asarray(inputs["L1_attn_D"], np.float32).reshape(1, 64)
            normw = inputs["L1_attn_norm_w"]
            wout_t = _tile_T(np.asarray(inputs["L1_attn_out_proj_w"], np.float32))

        in1 = [host_p1_inputs(c, resid_tok, p1c) for c in cores]
        res1 = run_bass_kernel_spmd(nc1, in1, core_ids=cores).results

        nc2 = _nc("P2", build_p2)
        in2 = []
        for c in cores:
            m = host_p2_inputs(res1[c])
            m["dpar"] = dpar
            in2.append(m)
        res2 = run_bass_kernel_spmd(nc2, in2, core_ids=cores).results

        nc3 = _nc("P3", build_p3)
        in3 = []
        for c in cores:
            resid_own = np.ascontiguousarray(resid_tok[c * T:(c + 1) * T].T)
            in3.append(host_p3_inputs(c, res1[c], res2[c], res1, res2, normw, wout_t, resid_own))
        res3 = run_bass_kernel_spmd(nc3, in3, core_ids=cores).results

        if layer == 0:
            ncF = _nc("F0", build_ffn, "spline")
            inF = [dict(xln_i=res3[c]["xln3_o"], resid=res3[c]["resid_o"],
                        wg_i=wg_t, wu_i=wu_t, wd_i=wd_t, onesin=ones1, ones2=ones2)
                   for c in cores]
        else:
            ncF = _nc("F1", build_ffn, "had")
            ws = np.asarray(inputs["L1_ffn_Wscale_ug"], np.float32)
            H128 = _hadamard(128)
            s1 = 1.0 / (np.sqrt(2048.0) * 64.0)
            s2 = 1.0 / (np.sqrt(8192.0) * 64.0)
            f1c = dict(
                wg_i=_tile_T(np.asarray(inputs["L1_ffn_W_gate"], np.float32)),
                wu_i=_tile_T(np.asarray(inputs["L1_ffn_W_up"], np.float32)),
                wd_i=_tile_T(np.asarray(inputs["L1_ffn_W_down"], np.float32)),
                su_ug=np.ascontiguousarray(np.asarray(inputs["L1_ffn_SU_ug"], np.float32).reshape(16, 128).T),
                su_dp=np.ascontiguousarray(np.asarray(inputs["L1_ffn_SU_dp"], np.float32).reshape(64, 128).T),
                wsup=np.ascontiguousarray(ws[:INTER].reshape(64, 128).T),
                wsgt=np.ascontiguousarray(ws[INTER:].reshape(64, 128).T),
                wsdp=np.ascontiguousarray(np.asarray(inputs["L1_ffn_Wscale_dp"], np.float32).reshape(16, 128).T),
                h128a=(H128 * s1).astype(np.float32),
                h128b=(H128 * s2).astype(np.float32),
                onesin=ones1, ones2=ones2,
            )
            inF = [dict(xln_i=res3[c]["xln3_o"], resid=res3[c]["resid_o"], **f1c)
                   for c in cores]
        resF = run_bass_kernel_spmd(ncF, inF, core_ids=cores).results
        resid_tok = np.concatenate([np.asarray(resF[c]["resid_o"]).T for c in cores], axis=0)

    return resid_tok.reshape(2, 4096, DM).astype(np.float32)
